# version 02
# baseline (speedup 1.0000x reference)
import contextlib
import numpy as np
import ml_dtypes
import concourse.bass as bass
import concourse.mybir as mybir
from concourse.bass_utils import run_bass_kernel_spmd

F32 = mybir.dt.float32
BF16 = mybir.dt.bfloat16
U8 = mybir.dt.uint8
AF = mybir.ActivationFunctionType
ALU = mybir.AluOpType
AX = mybir.AxisListType

N_CORES = 8
B_PER_CORE = 2
SEQ = 2048
TOK = B_PER_CORE * SEQ
D = 1024
INTER = 2048
NTILE = TOK // 128
EPS = 1e-5
NEG_BIG = -1e30


class Ctx:

    def __init__(self, nc, es):
        self.nc = nc
        self.es = es
        self.sems = {}

    def sem(self, name):
        if name not in self.sems:
            h = self.es.enter_context(self.nc.semaphore(name))
            self.sems[name] = [h, 0]
        return self.sems[name][0]

    def bump(self, name, n=1):
        self.sems[name][1] += n
        return self.sems[name][1]

    def val(self, name):
        if name not in self.sems:
            return 0
        return self.sems[name][1]

    def inc(self, instr, name, n=1):
        instr.then_inc(self.sem(name), n)
        return self.bump(name, n)

    def chain(self, eng, name):
        return _Chained(eng, self, name)

    def barrier(self, blk, engines=("tensor", "vector", "scalar", "gpsimd",
                                    "sync")):
        snap = [(name, rec[1]) for name, rec in self.sems.items() if rec[1] > 0]

        def mk(engname):
            def f(e):
                for name, val in snap:
                    e.wait_ge(self.sem(name), val)
            return f
        for engname in engines:
            getattr(blk, engname)(mk(engname))


class _Chained:

    _PASSTHROUGH = {"wait_ge", "wait_op", "sem_inc", "nop", "drain"}

    def __init__(self, eng, ctx, name):
        self._eng = eng
        self._ctx = ctx
        self._name = name

    def __getattr__(self, attr):
        fn = getattr(self._eng, attr)
        if attr in self._PASSTHROUGH or not callable(fn):
            return fn
        ctx, name, eng = self._ctx, self._name, self._eng

        def wrapped(*a, **k):
            eng.wait_ge(ctx.sem(name), ctx.val(name))
            i = fn(*a, **k)
            ctx.inc(i, name)
            return i

        return wrapped


def _host_bf16(x):
    return np.ascontiguousarray(x).astype(ml_dtypes.bfloat16)




_DT_BYTES = {F32: 4, BF16: 2, U8: 1}


def _cursor_alloc(nc, pfx, cur):
    def sb(n, s, d=F32):
        nb = _DT_BYTES[d]
        for x in s[1:]:
            nb *= x
        t = nc.alloc_sbuf_tensor_at(pfx + n, s, d, offset=cur[0])
        cur[0] += (nb + 31) // 32 * 32
        return t
    return sb


def _declare_shared_consts(nc):
    return dict(
        id_bf=nc.dram_tensor("id_bf", [128, 128], BF16, kind="ExternalInput").ap(),
        id_f=nc.dram_tensor("id_f", [128, 128], F32, kind="ExternalInput").ap(),
        ones_bf=nc.dram_tensor("ones_bf", [1, 128], BF16,
                               kind="ExternalInput").ap(),
        ones_f=nc.dram_tensor("ones_f", [1, 128], F32, kind="ExternalInput").ap(),
        neg1_bf=nc.dram_tensor("neg1_bf", [1, 128], BF16,
                               kind="ExternalInput").ap(),
        tril=nc.dram_tensor("tril", [128, 128], F32, kind="ExternalInput").ap(),
    )


_SHARED_CONST_VALUES = None


def _shared_const_values():
    global _SHARED_CONST_VALUES
    if _SHARED_CONST_VALUES is None:
        _SHARED_CONST_VALUES = dict(
            id_bf=np.eye(128, dtype=ml_dtypes.bfloat16),
            id_f=np.eye(128, dtype=np.float32),
            ones_bf=np.ones((1, 128), ml_dtypes.bfloat16),
            ones_f=np.ones((1, 128), np.float32),
            neg1_bf=np.full((1, 128), -1.0, ml_dtypes.bfloat16),
            tril=np.ascontiguousarray(np.tril(np.ones((128, 128), np.float32))),
        )
    return _SHARED_CONST_VALUES


def emit_l0ffn(nc, c, es, blk, aps, pfx, sbase, PB, PM, PA):
    x_in = aps["x_in"]
    wA = aps["wA"]
    w2 = aps["w2"]
    gw = aps["gw"]
    gbias = aps["gbias"]
    id_bf = aps["id_bf"]
    id_f = aps["id_f"]
    ones_bf = aps["ones_bf"]
    ones_f = aps["ones_f"]
    x_out = aps["x_out"]
    combT_dram = aps["combT_dram"]

    SBTOK = 1024
    NSB = TOK // SBTOK
    NMT = SBTOK // 128
    NTC = SBTOK // 512

    if True:
        cur = [sbase]
        sb = _cursor_alloc(nc, pfx, cur)

        x_sb = sb("x_sb", [128, 2, D])
        xo_sb = sb("xo_sb", [128, 2, D])
        sq_sc = sb("sq_sc", [128, D])
        st = sb("st", [128, 16])
        ln_xT = sb("ln_xT", [128, 8, SBTOK], BF16)
        ln_bf = sb("ln_bf", [128, D], BF16)
        gw_sb = sb("gw_sb", [128, 8, 8], BF16)
        gb_bc = sb("gb_bc", [128, 8])
        neg_t = sb("neg_t", [128, 8])
        gwork = sb("gwork", [128, 40])
        gmask = sb("gmask", [128, 8], U8)
        comb_f = sb("comb_f", [128, 8])
        combT8 = sb("combT8", [8, 128])
        comb_row = sb("comb_row", [1, 512])
        comb_sb = sb("comb_sb", [128, 512])
        wA_sb = sb("wA_sb", [128, 8, 2 * INTER], BF16)
        w2_sb = sb("w2_sb", [128, 16, D], BF16)
        hT_sb = sb("hT_sb", [128, 16, 512], BF16)
        hs_sc = sb("hs_sc", [128, 2, 512], BF16)
        yev = sb("yev", [128, 512])
        yacc = sb("yacc", [128, 8, SBTOK])
        id_bf_sb = sb("id_bf_sb", [128, 128], BF16)
        id_f_sb = sb("id_f_sb", [128, 128], F32)
        ones_bf_sb = sb("ones_bf_sb", [1, 128], BF16)
        ones_f_sb = sb("ones_f_sb", [1, 128], F32)
        gb_row = sb("gb_row", [1, 8])

        ps_a = [PB[:, i * 512:(i + 1) * 512] for i in range(4)]
        ps_y = [PM[:, i, :] for i in range(2)]
        ps_mix = PA[:, 0, :]
        ps_tr = PA[:, 1, :].bitcast(BF16).rearrange("p (g j) -> p g j", g=8)

        def pre_sync(sy):
            for dst, srcap in ((gb_row, gbias), (id_bf_sb, id_bf), (id_f_sb, id_f),
                               (ones_bf_sb, ones_bf), (ones_f_sb, ones_f)):
                i = sy.dma_start(out=dst[:], in_=srcap[:])
                c.inc(i, "syc", 16)
        blk.sync(pre_sync)

        def pre_gp(g):
            i1 = g.dma_start(out=gw_sb[:], in_=gw.rearrange("(c p) j -> p c j", p=128))
            c.inc(i1, "sgw", 16)
        blk.gpsimd(pre_gp)

        pre_sy = c.val("syc")
        sgw_v = c.val("sgw")

        def pre_t(t):
            t.wait_ge(c.sem("syc"), pre_sy)
            i = t.matmul(ps_mix[:, 0:8], ones_f_sb[:], gb_row[:], start=True, stop=True)
            c.inc(i, "pe", 1)
        blk.tensor(pre_t)
        pe_gb = c.val("pe")

        def pre_v(v):
            v = c.chain(v, "ve")
            v.wait_ge(c.sem("pe"), pe_gb)
            i = v.tensor_copy(out=gb_bc[:], in_=ps_mix[:, 0:8])
            i = v.memset(neg_t[:], NEG_BIG)
        blk.vector(pre_v)
        ve_pre = c.val("ve")

        last_lnx_reads_pe = [0]
        last_wA_reads_pe = [0]
        last_w2_reads_pe = [0]
        last_h_reads_pe = [0]
        x_tile_ve = {}
        xo_tile_sy = {}
        ln_tr_pe = []
        ln_st1_ve = [0]
        comb_tr_pe = []
        comb_st_sy = []
        comb_row_pe = [0]

        for s in range(NSB):
            tok0 = s * SBTOK
            ln_copy_ve = []
            for m in range(NMT):
                par = m % 2
                row0 = tok0 + m * 128
                def ld(sy, par=par, row0=row0, m=m):
                    if x_tile_ve.get(par) is not None:
                        sy.wait_ge(c.sem("ve"), x_tile_ve[par])
                    i = sy.dma_start(out=x_sb[:, par, :], in_=x_in[row0:row0 + 128, :])
                    c.inc(i, f"sx{par}", 16)
                blk.sync(ld)
                sy_here = c.val(f"sx{par}")

                def vstats(v, par=par, sy_here=sy_here):
                    v = c.chain(v, "ve")
                    v.wait_ge(c.sem(f"sx{par}"), sy_here)
                    i = v.reduce_sum(st[:, 0:1], x_sb[:, par, :], axis=AX.X)
                blk.vector(vstats)
                ve_sum = c.val("ve")

                def sstats(sc, par=par, sy_here=sy_here,
                           prev_math=ln_st1_ve[0]):
                    sc = c.chain(sc, "se")
                    sc.wait_ge(c.sem(f"sx{par}"), sy_here)
                    if prev_math:
                        sc.wait_ge(c.sem("ve"), prev_math)
                    i = sc.activation(out=sq_sc[:], in_=x_sb[:, par, :], func=AF.Square,
                                      accum_out=st[:, 1:2])
                blk.scalar(sstats)
                se_sq = c.val("se")

                def vmath(v, par=par, se_sq=se_sq):
                    v = c.chain(v, "ve")
                    v.wait_ge(c.sem("se"), se_sq)
                    i = v.tensor_scalar(out=st[:, 2:3], in0=st[:, 0:1], scalar1=1.0 / D,
                                        scalar2=None, op0=ALU.mult)
                    i = v.tensor_scalar(out=st[:, 3:4], in0=st[:, 1:2], scalar1=1.0 / D,
                                        scalar2=None, op0=ALU.mult)
                    i = v.tensor_tensor(out=st[:, 4:5], in0=st[:, 2:3], in1=st[:, 2:3],
                                        op=ALU.mult)
                    i = v.tensor_tensor(out=st[:, 5:6], in0=st[:, 3:4], in1=st[:, 4:5],
                                        op=ALU.subtract)
                    i = v.tensor_scalar(out=st[:, 5:6], in0=st[:, 5:6], scalar1=EPS,
                                        scalar2=None, op0=ALU.add)
                blk.vector(vmath)
                ve_var = c.val("ve")
                ln_st1_ve[0] = ve_var

                def srstd(sc, ve_var=ve_var):
                    sc = c.chain(sc, "se")
                    sc.wait_ge(c.sem("ve"), ve_var)
                    i = sc.activation(out=st[:, 6:7], in_=st[:, 5:6], func=AF.Ln)
                    i = sc.activation(out=st[:, 7:8], in_=st[:, 6:7], func=AF.Exp,
                                      scale=-0.5)
                blk.scalar(srstd)
                se_rstd = c.val("se")

                def vln(v, par=par, se_rstd=se_rstd):
                    v = c.chain(v, "ve")
                    v.wait_ge(c.sem("se"), se_rstd)
                    if ln_tr_pe:
                        v.wait_ge(c.sem("pe"), ln_tr_pe[-1])
                    i = v.tensor_scalar(out=ln_bf[:], in0=x_sb[:, par, :],
                                        scalar1=st[:, 2:3], scalar2=st[:, 7:8],
                                        op0=ALU.subtract, op1=ALU.mult)
                blk.vector(vln)
                ve_ln = c.val("ve")
                x_tile_ve[par] = ve_ln

                def ttr(t, ve_ln=ve_ln, m=m):
                    t.wait_ge(c.sem("ve"), ve_ln)
                    if ln_copy_ve:
                        t.wait_ge(c.sem("ve"), ln_copy_ve[-1])
                    for g in range(8):
                        i = t.transpose(ps_tr[:, g, :], ln_bf[:, g * 128:(g + 1) * 128],
                                        id_bf_sb[:])
                        c.inc(i, "pe", 1)
                blk.tensor(ttr)
                pe_tr = c.val("pe")
                ln_tr_pe.append(pe_tr)

                def vcp(v, pe_tr=pe_tr, m=m):
                    v = c.chain(v, "ve")
                    v.wait_ge(c.sem("pe"), pe_tr)
                    i = v.tensor_copy(out=ln_xT[:, :, m * 128:(m + 1) * 128],
                                      in_=ps_tr[:, :, :])
                blk.vector(vcp)
                ln_copy_ve.append(c.val("ve"))

            ve_ln_done = c.val("ve")

            comb_tr_ve = []
            for m in range(NMT):
                def tlog(t, m=m, ve_ln_done=ve_ln_done):
                    t.wait_ge(c.sem("sgw"), sgw_v)
                    t.wait_ge(c.sem("ve"), ve_ln_done)
                    if comb_tr_ve:
                        t.wait_ge(c.sem("ve"), comb_tr_ve[-1])
                    for g in range(8):
                        i = t.matmul(ps_mix[:, 0:8],
                                     ln_xT[:, g, m * 128:(m + 1) * 128],
                                     gw_sb[:, g, :], start=(g == 0), stop=(g == 7))
                        c.inc(i, "pe", 1)
                blk.tensor(tlog)
                pe_log = c.val("pe")

                def vsm1(v, pe_log=pe_log):
                    v = c.chain(v, "ve")
                    v.wait_ge(c.sem("pe"), pe_log)
                    i = v.reduce_max(st[:, 8:9], ps_mix[:, 0:8], axis=AX.X, negate=True)
                blk.vector(vsm1)
                ve_nm = c.val("ve")

                def sexp(sc, ve_nm=ve_nm):
                    sc = c.chain(sc, "se")
                    sc.wait_ge(c.sem("ve"), ve_nm)
                    i = sc.activation(out=gwork[:, 0:8], in_=ps_mix[:, 0:8], func=AF.Exp,
                                      bias=st[:, 8:9], accum_out=st[:, 9:10])
                blk.scalar(sexp)
                se_exp = c.val("se")

                def vsm2(v, se_exp=se_exp, m=m):
                    v = c.chain(v, "ve")
                    v.wait_ge(c.sem("se"), se_exp)
                    if comb_tr_pe:
                        v.wait_ge(c.sem("pe"), comb_tr_pe[-1])
                    i = v.reciprocal(st[:, 10:11], st[:, 9:10])
                    i = v.tensor_scalar(out=gwork[:, 8:16], in0=gwork[:, 0:8],
                                        scalar1=st[:, 10:11], scalar2=None, op0=ALU.mult)
                    i = v.tensor_tensor(out=gwork[:, 16:24], in0=gwork[:, 8:16],
                                        in1=gb_bc[:], op=ALU.add)
                    i = v.reduce_max(st[:, 11:12], gwork[:, 16:24], axis=AX.X)
                    i = v.tensor_scalar(out=gwork[:, 24:32], in0=gwork[:, 16:24],
                                        scalar1=st[:, 11:12], scalar2=None,
                                        op0=ALU.is_equal)
                    i = v.scalar_tensor_tensor(out=gwork[:, 24:32],
                                               in0=gwork[:, 24:32], scalar=NEG_BIG,
                                               in1=gwork[:, 16:24],
                                               op0=ALU.mult, op1=ALU.add)
                    i = v.reduce_max(st[:, 12:13], gwork[:, 24:32], axis=AX.X)
                    i = v.tensor_scalar(out=gwork[:, 32:40], in0=gwork[:, 16:24],
                                        scalar1=st[:, 12:13], scalar2=None, op0=ALU.is_ge)
                    i = v.tensor_tensor(out=comb_f[:], in0=gwork[:, 8:16],
                                        in1=gwork[:, 32:40], op=ALU.mult)
                blk.vector(vsm2)
                ve_cb = c.val("ve")

                def tcb(t, ve_cb=ve_cb):
                    t.wait_ge(c.sem("ve"), ve_cb)
                    i = t.transpose(ps_mix[0:8, 0:128], comb_f[:], id_f_sb[:])
                    c.inc(i, "pe", 1)
                blk.tensor(tcb)
                pe_cb = c.val("pe")
                comb_tr_pe.append(pe_cb)

                def vcb(v, pe_cb=pe_cb, m=m):
                    v = c.chain(v, "ve")
                    v.wait_ge(c.sem("pe"), pe_cb)
                    if comb_st_sy:
                        v.wait_ge(c.sem("scb"), comb_st_sy[-1])
                    i = v.tensor_copy(out=combT8[:], in_=ps_mix[0:8, 0:128])
                blk.vector(vcb)
                comb_tr_ve.append(c.val("ve"))
                ve_cbc = c.val("ve")

                def stcb(sy, m=m, tok0=tok0, ve_cbc=ve_cbc):
                    sy.wait_ge(c.sem("ve"), ve_cbc)
                    col0 = tok0 + m * 128
                    i = sy.dma_start(out=combT_dram[:, col0:col0 + 128],
                                     in_=combT8[:])
                    c.inc(i, "scb", 16)
                blk.sync(stcb)
                comb_st_sy.append(c.val("scb"))

            ve_gate_done = c.val("ve")

            def vz(v, ve_gate_done=ve_gate_done):
                v = c.chain(v, "ve")
                v.wait_ge(c.sem("pe"), last_lnx_reads_pe[0])
                i = v.memset(yacc[:], 0.0)
            blk.vector(vz)
            ve_yz = c.val("ve")

            h_write_ve = [0]
            evac_ve = {0: 0, 1: 0}
            silu_se = {}
            hmul_ve = {}
            for e in range(9):
                def lw(g, e=e):
                    g.wait_ge(c.sem("pe"), last_wA_reads_pe[0])
                    for h2 in range(2):
                        i = g.dma_start(
                            out=wA_sb[:, 4 * h2:4 * (h2 + 1), :],
                            in_=wA[e].rearrange("(c p) i -> p c i", p=128)[:, 4 * h2:4 * (h2 + 1), :])
                        c.inc(i, "swa", 16)
                blk.gpsimd(lw)
                gp_wA = c.val("swa")

                def lw2(g, e=e):
                    g.wait_ge(c.sem("pe"), last_w2_reads_pe[0])
                    i = g.dma_start(out=w2_sb[:],
                                    in_=w2[e].rearrange("(c p) j -> p c j", p=128))
                    c.inc(i, "sw2", 16)
                blk.gpsimd(lw2)
                gp_w2 = c.val("sw2")

                for tc in range(NTC):
                    if e < 9 - 1:
                        def ldcr(sy, e=e, tc=tc, tok0=tok0):
                            sy.wait_ge(c.sem("scb"), comb_st_sy[-1])
                            sy.wait_ge(c.sem("pe"), comb_row_pe[0])
                            col0 = tok0 + tc * 512
                            i = sy.dma_start(out=comb_row[:],
                                             in_=combT_dram[e:e + 1, col0:col0 + 512])
                            c.inc(i, "scr", 16)
                        blk.sync(ldcr)
                        sy_cr = c.val("scr")

                        def tbc(t, sy_cr=sy_cr, ve_gate_done=ve_gate_done):
                            t.wait_ge(c.sem("scr"), sy_cr)
                            t.wait_ge(c.sem("ve"), ve_gate_done)
                            i = t.matmul(ps_mix[:, 0:512], ones_f_sb[:],
                                         comb_row[:], start=True, stop=True)
                            c.inc(i, "pe", 1)
                        blk.tensor(tbc)
                        pe_bc = c.val("pe")
                        comb_row_pe[0] = pe_bc

                        def vbc(v, pe_bc=pe_bc):
                            v = c.chain(v, "ve")
                            v.wait_ge(c.sem("pe"), pe_bc)
                            i = v.tensor_copy(out=comb_sb[:], in_=ps_mix[:, 0:512])
                        blk.vector(vbc)
                    ve_comb = c.val("ve")

                    for ib in range(16):
                        pp = ib % 2
                        bank_g1, bank_g3 = ps_a[2 * pp], ps_a[2 * pp + 1]

                        def tA(t, e=e, tc=tc, ib=ib, pp=pp, gp_wA=gp_wA,
                               bank_g1=bank_g1, bank_g3=bank_g3):
                            t.wait_ge(c.sem("swa"), gp_wA)
                            if silu_se.get(pp):
                                t.wait_ge(c.sem("se"), silu_se[pp])
                            if hmul_ve.get(pp):
                                t.wait_ge(c.sem("ve"), hmul_ve[pp])
                            for g in range(8):
                                i = t.matmul(bank_g1[:],
                                             wA_sb[:, g, ib * 128:(ib + 1) * 128],
                                             ln_xT[:, g, tc * 512:(tc + 1) * 512],
                                             start=(g == 0), stop=(g == 7))
                                c.inc(i, "pe", 1)
                            for g in range(8):
                                i = t.matmul(bank_g3[:],
                                             wA_sb[:, g, INTER + ib * 128:INTER + (ib + 1) * 128],
                                             ln_xT[:, g, tc * 512:(tc + 1) * 512],
                                             start=(g == 0), stop=(g == 7))
                                c.inc(i, "pe", 1)
                        blk.tensor(tA)
                        pe_A = c.val("pe")
                        last_lnx_reads_pe[0] = pe_A
                        last_wA_reads_pe[0] = pe_A

                        def sA(sc, pe_A=pe_A, pp=pp, bank_g1=bank_g1,
                               hmv=hmul_ve.get(pp)):
                            sc = c.chain(sc, "se")
                            sc.wait_ge(c.sem("pe"), pe_A)
                            if hmv:
                                sc.wait_ge(c.sem("ve"), hmv)
                            i = sc.activation(out=hs_sc[:, pp, :], in_=bank_g1[:],
                                              func=AF.Silu)
                        blk.scalar(sA)
                        silu_se[pp] = c.val("se")

                        def vA(v, pe_A=pe_A, ib=ib, pp=pp, bank_g3=bank_g3,
                               se_h=silu_se[pp]):
                            v = c.chain(v, "ve")
                            v.wait_ge(c.sem("se"), se_h)
                            v.wait_ge(c.sem("pe"), pe_A)
                            v.wait_ge(c.sem("pe"), last_h_reads_pe[0])
                            i = v.tensor_tensor(out=hT_sb[:, ib, :],
                                                in0=hs_sc[:, pp, :],
                                                in1=bank_g3[:], op=ALU.mult)
                        blk.vector(vA)
                        hmul_ve[pp] = c.val("ve")
                    h_write_ve[0] = c.val("ve")

                    for g in range(8):
                        pp = g % 2
                        bank_y = ps_y[pp]

                        def tB(t, e=e, tc=tc, g=g, pp=pp, gp_w2=gp_w2,
                               hv=h_write_ve[0], bank_y=bank_y):
                            t.wait_ge(c.sem("sw2"), gp_w2)
                            t.wait_ge(c.sem("ve"), hv)
                            if evac_ve[pp]:
                                t.wait_ge(c.sem("ve"), evac_ve[pp])
                            for ic in range(16):
                                i = t.matmul(bank_y[:],
                                             w2_sb[:, ic, g * 128:(g + 1) * 128],
                                             hT_sb[:, ic, :],
                                             start=(ic == 0), stop=(ic == 15))
                                c.inc(i, "pe", 1)
                        blk.tensor(tB)
                        pe_B = c.val("pe")
                        last_h_reads_pe[0] = pe_B
                        last_w2_reads_pe[0] = pe_B

                        def vB(v, e=e, tc=tc, g=g, pe_B=pe_B, ve_comb=ve_comb,
                               bank_y=bank_y):
                            v = c.chain(v, "ve")
                            v.wait_ge(c.sem("pe"), pe_B)
                            if e < 8:
                                i = v.tensor_tensor(out=yev[:], in0=bank_y[:],
                                                    in1=comb_sb[:], op=ALU.mult)
                                i = v.tensor_tensor(
                                    out=yacc[:, g, tc * 512:(tc + 1) * 512],
                                    in0=yacc[:, g, tc * 512:(tc + 1) * 512],
                                    in1=yev[:], op=ALU.add)
                            else:
                                i = v.tensor_tensor(
                                    out=yacc[:, g, tc * 512:(tc + 1) * 512],
                                    in0=yacc[:, g, tc * 512:(tc + 1) * 512],
                                    in1=bank_y[:], op=ALU.add)
                        blk.vector(vB)
                        evac_ve[pp] = c.val("ve")

            ve_experts_done = c.val("ve")

            wb_ve_list = []
            for m in range(NMT):
                par = m % 2
                row0 = tok0 + m * 128

                def ldx(sy, par=par, row0=row0, m=m):
                    if x_tile_ve.get(par) is not None:
                        sy.wait_ge(c.sem("ve"), x_tile_ve[par])
                    i = sy.dma_start(out=x_sb[:, par, :], in_=x_in[row0:row0 + 128, :])
                    c.inc(i, f"sx{par}", 16)
                blk.sync(ldx)
                sy_x = c.val(f"sx{par}")

                def twb(t, m=m, ve_experts_done=ve_experts_done):
                    t.wait_ge(c.sem("ve"), ve_experts_done)
                    if wb_ve_list:
                        t.wait_ge(c.sem("ve"), wb_ve_list[-1])
                    for g in range(8):
                        bank = ps_a[0] if g < 4 else ps_a[1]
                        i = t.transpose(bank[:, (g % 4) * 128:(g % 4 + 1) * 128],
                                        yacc[:, g, m * 128:(m + 1) * 128], id_f_sb[:])
                        c.inc(i, "pe", 1)
                blk.tensor(twb)
                pe_wb = c.val("pe")

                def vwb(v, par=par, m=m, pe_wb=pe_wb, sy_x=sy_x):
                    v = c.chain(v, "ve")
                    v.wait_ge(c.sem("pe"), pe_wb)
                    v.wait_ge(c.sem(f"sx{par}"), sy_x)
                    if xo_tile_sy.get(par) is not None:
                        v.wait_ge(c.sem(f"so{par}"), xo_tile_sy[par])
                    i = v.scalar_tensor_tensor(out=xo_sb[:, par, 0:512],
                                               in0=ps_a[0][:], scalar=0.5,
                                               in1=x_sb[:, par, 0:512],
                                               op0=ALU.mult, op1=ALU.add)
                    i = v.scalar_tensor_tensor(out=xo_sb[:, par, 512:1024],
                                               in0=ps_a[1][:], scalar=0.5,
                                               in1=x_sb[:, par, 512:1024],
                                               op0=ALU.mult, op1=ALU.add)
                blk.vector(vwb)
                ve_wb = c.val("ve")
                wb_ve_list.append(ve_wb)
                x_tile_ve[par] = ve_wb

                def stx(sy, par=par, row0=row0, ve_wb=ve_wb):
                    sy.wait_ge(c.sem("ve"), ve_wb)
                    i = sy.dma_start(out=x_out[row0:row0 + 128, :], in_=xo_sb[:, par, :])
                    c.inc(i, f"so{par}", 16)
                blk.sync(stx)
                xo_tile_sy[par] = c.val(f"so{par}")

            wb_pe = c.val("pe")
            last_lnx_reads_pe[0] = max(last_lnx_reads_pe[0], wb_pe)
            hmul_ve[0] = c.val("ve")
            hmul_ve[1] = c.val("ve")
            silu_se[0] = c.val("se")
            silu_se[1] = c.val("se")

        so0_t, so1_t = c.val("so0"), c.val("so1")

        def fin(sy, so0_t=so0_t, so1_t=so1_t):
            sy.wait_ge(c.sem("so0"), so0_t)
            sy.wait_ge(c.sem("so1"), so1_t)
        blk.sync(fin)


def _declare_l0ffn_drams(nc, pfx=""):
    aps = dict(
        wA=nc.dram_tensor(pfx + "wA", [9, D, 2 * INTER], F32,
                          kind="ExternalInput").ap(),
        w2=nc.dram_tensor(pfx + "w2", [9, INTER, D], F32,
                          kind="ExternalInput").ap(),
        gw=nc.dram_tensor(pfx + "gw", [D, 8], F32, kind="ExternalInput").ap(),
        gbias=nc.dram_tensor(pfx + "gbias", [1, 8], F32,
                             kind="ExternalInput").ap(),
    )
    return aps


def build_l0ffn():
    nc = bass.Bass("TRN2", target_bir_lowering=False, debug=False)
    aps = _declare_l0ffn_drams(nc)
    aps.update(_declare_shared_consts(nc))
    aps["x_in"] = nc.dram_tensor("x_in", [TOK, D], F32, kind="ExternalInput").ap()
    aps["x_out"] = nc.dram_tensor("x_out", [TOK, D], F32,
                                  kind="ExternalOutput").ap()
    aps["combT_dram"] = nc.dram_tensor("combT_dram", [8, TOK], F32,
                                       kind="Internal").ap()
    with contextlib.ExitStack() as es:
        c = Ctx(nc, es)
        PB = es.enter_context(nc.psum_tensor("PB", [128, 2048], F32))
        PM = es.enter_context(nc.psum_tensor("PM", [128, 2, 512], F32))
        PA = es.enter_context(nc.psum_tensor("PA", [128, 2, 512], F32))
        sbase, _ = nc.bump_sbuf(nc.sbuf_bytes_remaining - 64)
        blk = es.enter_context(nc.Block())
        emit_l0ffn(nc, c, es, blk, aps, "f_", sbase, PB, PM, PA)
    return nc



def emit_l0attn(nc, c, es, blk, aps, pfx, sbase, PB, PM, PA):
    x_in = aps["x_in"]
    wqkv = aps["wqkv"]
    bqk = aps["bqk"]
    bv = aps["bv"]
    id_bf = aps["id_bf"]
    tril = aps["tril"]
    ones_f = aps["ones_f"]
    x_out = aps["x_out"]

    NQT = SEQ // 128
    SCALE = 1.0 / 32.0

    if True:
        cur = [sbase]
        sb = _cursor_alloc(nc, pfx, cur)

        x_sb = sb("x_sb", [128, 2, D])
        xo_sb = sb("xo_sb", [128, 2, D])
        sq_sc = sb("sq_sc", [128, D])
        st = sb("st", [128, 16])
        ln_bf = sb("ln_bf", [128, D], BF16)
        lnxT = sb("lnxT", [128, 8, SEQ], BF16)
        qT = sb("qT", [128, 8, SEQ], BF16)
        kT = sb("kT", [128, 8, SEQ], BF16)
        s_row = sb("s_row", [128, SEQ])
        p_buf = sb("p_buf", [128, SEQ], BF16)
        w_sb = sb("w_sb", [128, 8, 3 * D], BF16)
        bqk_sb = sb("bqk_sb", [128, 16])
        bv_row = sb("bv_row", [1, D])
        bv_bc = sb("bv_bc", [128, D])
        v_sc = sb("v_sc", [128, D])
        id_bf_sb = sb("id_bf_sb", [128, 128], BF16)
        tril_sb = sb("tril_sb", [128, 128], F32)
        ones_f_sb = sb("ones_f_sb", [1, 128], F32)

        ps_s = PB[:, :]
        ps_q = [PM[:, i, :] for i in range(2)]
        ps_tr = PA[:, 1, :].bitcast(BF16).rearrange("p (g j) -> p g j", g=8)

        def pre_sync(sy):
            for dst, srcap in ((bqk_sb, bqk), (bv_row, bv), (id_bf_sb, id_bf),
                               (tril_sb, tril), (ones_f_sb, ones_f)):
                i = sy.dma_start(out=dst[:], in_=srcap[:])
                c.inc(i, "syc", 16)
        blk.sync(pre_sync)
        syc_v = c.val("syc")

        def pre_gp(g):
            i = g.dma_start(out=w_sb[:],
                            in_=wqkv.rearrange("(c p) j -> p c j", p=128))
            c.inc(i, "swq", 16)
        blk.gpsimd(pre_gp)
        swq_v = c.val("swq")

        def pre_t(t):
            t.wait_ge(c.sem("syc"), syc_v)
            for h in range(2):
                i = t.matmul(ps_q[h][:], ones_f_sb[:], bv_row[0:1, h * 512:(h + 1) * 512],
                             start=True, stop=True)
                c.inc(i, "pe", 1)
        blk.tensor(pre_t)
        pe_bv = c.val("pe")

        def pre_v(v):
            v = c.chain(v, "ve")
            v.wait_ge(c.sem("pe"), pe_bv)
            for h in range(2):
                v.tensor_copy(out=bv_bc[:, h * 512:(h + 1) * 512], in_=ps_q[h][:])
        blk.vector(pre_v)
        ve_bvbc = c.val("ve")

        x_tile_ve = {}
        xo_tile_sy = {}
        ln_tr_pe = []
        ln_st1_ve = [0]
        ln_copy_pe_guard = [0]
        qk_evac_se = {0: 0, 1: 0}
        v_evac_ve = {0: 0, 1: 0}
        sbank_evac_ve = {j: 0 for j in range(4)}
        exp_se = [0]
        pmul_ve = [0]

        for b in range(B_PER_CORE):
            base_row = b * SEQ
            ln_copy_ve = []
            for m in range(NQT):
                par = m % 2
                row0 = base_row + m * 128

                def ld(sy, par=par, row0=row0):
                    if x_tile_ve.get(par) is not None:
                        sy.wait_ge(c.sem("ve"), x_tile_ve[par])
                    i = sy.dma_start(out=x_sb[:, par, :], in_=x_in[row0:row0 + 128, :])
                    c.inc(i, f"sx{par}", 16)
                blk.sync(ld)
                sy_here = c.val(f"sx{par}")

                def vstats(v, par=par, sy_here=sy_here):
                    v = c.chain(v, "ve")
                    v.wait_ge(c.sem(f"sx{par}"), sy_here)
                    v.reduce_sum(st[:, 0:1], x_sb[:, par, :], axis=AX.X)
                blk.vector(vstats)

                def sstats(sc, par=par, sy_here=sy_here, prev_math=ln_st1_ve[0]):
                    sc = c.chain(sc, "se")
                    sc.wait_ge(c.sem(f"sx{par}"), sy_here)
                    if prev_math:
                        sc.wait_ge(c.sem("ve"), prev_math)
                    sc.activation(out=sq_sc[:], in_=x_sb[:, par, :], func=AF.Square,
                                  accum_out=st[:, 1:2])
                blk.scalar(sstats)
                se_sq = c.val("se")

                def vmath(v, se_sq=se_sq):
                    v = c.chain(v, "ve")
                    v.wait_ge(c.sem("se"), se_sq)
                    v.tensor_scalar(out=st[:, 2:3], in0=st[:, 0:1], scalar1=1.0 / D,
                                    scalar2=None, op0=ALU.mult)
                    v.tensor_scalar(out=st[:, 3:4], in0=st[:, 1:2], scalar1=1.0 / D,
                                    scalar2=None, op0=ALU.mult)
                    v.tensor_tensor(out=st[:, 4:5], in0=st[:, 2:3], in1=st[:, 2:3],
                                    op=ALU.mult)
                    v.tensor_tensor(out=st[:, 5:6], in0=st[:, 3:4], in1=st[:, 4:5],
                                    op=ALU.subtract)
                    v.tensor_scalar(out=st[:, 5:6], in0=st[:, 5:6], scalar1=EPS,
                                    scalar2=None, op0=ALU.add)
                blk.vector(vmath)
                ve_var = c.val("ve")
                ln_st1_ve[0] = ve_var

                def srstd(sc, ve_var=ve_var):
                    sc = c.chain(sc, "se")
                    sc.wait_ge(c.sem("ve"), ve_var)
                    sc.activation(out=st[:, 6:7], in_=st[:, 5:6], func=AF.Ln)
                    sc.activation(out=st[:, 7:8], in_=st[:, 6:7], func=AF.Exp,
                                  scale=-0.5)
                blk.scalar(srstd)
                se_rstd = c.val("se")

                def vln(v, par=par, se_rstd=se_rstd):
                    v = c.chain(v, "ve")
                    v.wait_ge(c.sem("se"), se_rstd)
                    if ln_tr_pe:
                        v.wait_ge(c.sem("pe"), ln_tr_pe[-1])
                    v.tensor_scalar(out=ln_bf[:], in0=x_sb[:, par, :],
                                    scalar1=st[:, 2:3], scalar2=st[:, 7:8],
                                    op0=ALU.subtract, op1=ALU.mult)
                blk.vector(vln)
                ve_ln = c.val("ve")
                x_tile_ve[par] = ve_ln

                def ttr(t, ve_ln=ve_ln):
                    t.wait_ge(c.sem("ve"), ve_ln)
                    if ln_copy_ve:
                        t.wait_ge(c.sem("ve"), ln_copy_ve[-1])
                    for g in range(8):
                        i = t.transpose(ps_tr[:, g, :], ln_bf[:, g * 128:(g + 1) * 128],
                                        id_bf_sb[:])
                        c.inc(i, "pe", 1)
                blk.tensor(ttr)
                pe_tr = c.val("pe")
                ln_tr_pe.append(pe_tr)

                def vcp(v, pe_tr=pe_tr, m=m):
                    v = c.chain(v, "ve")
                    v.wait_ge(c.sem("pe"), pe_tr)
                    v.tensor_copy(out=lnxT[:, :, m * 128:(m + 1) * 128],
                                  in_=ps_tr[:, :, :])
                blk.vector(vcp)
                ln_copy_ve.append(c.val("ve"))
            ve_ln_done = c.val("ve")

            for kind in range(2):
                dst = qT if kind == 0 else kT
                for jb in range(8):
                    for tc in range(4):
                        pp = (jb * 4 + tc) % 2
                        bank = ps_q[pp]

                        def tqk(t, kind=kind, jb=jb, tc=tc, pp=pp, bank=bank,
                                ve_ln_done=ve_ln_done):
                            t.wait_ge(c.sem("swq"), swq_v)
                            t.wait_ge(c.sem("ve"), ve_ln_done)
                            if qk_evac_se[pp]:
                                t.wait_ge(c.sem("se"), qk_evac_se[pp])
                            if v_evac_ve[pp]:
                                t.wait_ge(c.sem("ve"), v_evac_ve[pp])
                            for g in range(8):
                                i = t.matmul(bank[:],
                                             w_sb[:, g, kind * D + jb * 128:
                                                  kind * D + (jb + 1) * 128],
                                             lnxT[:, g, tc * 512:(tc + 1) * 512],
                                             start=(g == 0), stop=(g == 7))
                                c.inc(i, "pe", 1)
                        blk.tensor(tqk)
                        pe_qk = c.val("pe")
                        ln_copy_pe_guard[0] = pe_qk

                        def sqk(sc, kind=kind, jb=jb, tc=tc, bank=bank, pe_qk=pe_qk,
                                dst=dst):
                            sc = c.chain(sc, "se")
                            sc.wait_ge(c.sem("pe"), pe_qk)
                            sc.activation(out=dst[:, jb, tc * 512:(tc + 1) * 512],
                                          in_=bank[:], func=AF.Identity,
                                          bias=bqk_sb[:, kind * 8 + jb:
                                                      kind * 8 + jb + 1])
                        blk.scalar(sqk)
                        qk_evac_se[pp] = c.val("se")
            qkprod_se = c.val("se")

            for qi in range(NQT):
                par = qi % 2
                row0 = base_row + qi * 128
                kend = (qi + 1) * 128
                nchunk = (kend + 511) // 512

                def vmemset(v, qi=qi, se_prev=exp_se[0]):
                    v = c.chain(v, "ve")
                    if se_prev:
                        v.wait_ge(c.sem("se"), se_prev)
                    v.memset(s_row[:], 0.0)
                blk.vector(vmemset)
                ve_ms = c.val("ve")

                def tsc(t, qi=qi, kend=kend, nchunk=nchunk, ve_ms=ve_ms,
                        qkprod_se=qkprod_se):
                    t.wait_ge(c.sem("se"), qkprod_se)
                    for j in range(nchunk):
                        cols = min(512, kend - j * 512)
                        if sbank_evac_ve[j]:
                            t.wait_ge(c.sem("ve"), sbank_evac_ve[j])
                        for g in range(8):
                            i = t.matmul(ps_s[:, j * 512:j * 512 + cols],
                                         qT[:, g, qi * 128:(qi + 1) * 128],
                                         kT[:, g, j * 512:j * 512 + cols],
                                         start=(g == 0), stop=(g == 7))
                            c.inc(i, "pe", 1)
                blk.tensor(tsc)
                pe_s = c.val("pe")

                def vevac(v, qi=qi, kend=kend, nchunk=nchunk, pe_s=pe_s,
                          ve_ms=ve_ms):
                    v = c.chain(v, "ve")
                    v.wait_ge(c.sem("pe"), pe_s)
                    for j in range(nchunk):
                        cols = min(512, kend - j * 512)
                        if cols > 128:
                            v.tensor_scalar(out=s_row[:, j * 512:j * 512 + cols - 128],
                                            in0=ps_s[:, j * 512:j * 512 + cols - 128],
                                            scalar1=SCALE, scalar2=None, op0=ALU.mult)
                        d0 = j * 512 + cols - 128
                        v.scalar_tensor_tensor(out=s_row[:, d0:d0 + 128],
                                               in0=ps_s[:, d0:d0 + 128],
                                               scalar=SCALE, in1=tril_sb[:],
                                               op0=ALU.mult, op1=ALU.mult)
                    v.reduce_max(st[:, 8:9], s_row[:], axis=AX.X, negate=True)
                blk.vector(vevac)
                ve_sev = c.val("ve")
                for j in range(nchunk):
                    sbank_evac_ve[j] = ve_sev

                def sexp(sc, ve_sev=ve_sev, pm=pmul_ve[0]):
                    sc = c.chain(sc, "se")
                    sc.wait_ge(c.sem("ve"), ve_sev)
                    if pm:
                        sc.wait_ge(c.sem("ve"), pm)
                    sc.activation(out=p_buf[:], in_=s_row[:], func=AF.Exp,
                                  bias=st[:, 8:9], accum_out=st[:, 9:10])
                blk.scalar(sexp)
                exp_se[0] = c.val("se")
                se_e = c.val("se")

                def ldx(sy, par=par, row0=row0):
                    if x_tile_ve.get(par) is not None:
                        sy.wait_ge(c.sem("ve"), x_tile_ve[par])
                    i = sy.dma_start(out=x_sb[:, par, :], in_=x_in[row0:row0 + 128, :])
                    c.inc(i, f"sx{par}", 16)
                blk.sync(ldx)
                sy_x = c.val(f"sx{par}")

                def vsum(v, se_e=se_e):
                    v = c.chain(v, "ve")
                    v.wait_ge(c.sem("se"), se_e)
                    v.reciprocal(st[:, 10:11], st[:, 9:10])
                    v.tensor_scalar(out=p_buf[:], in0=p_buf[:], scalar1=st[:, 10:11],
                                    scalar2=None, op0=ALU.mult, op1=ALU.add,
                                    accum_out=st[:, 11:12])
                blk.vector(vsum)
                pmul_ve[0] = c.val("ve")
                ve_r = c.val("ve")

                def tv(t, qi=qi, ve_r=ve_r):
                    for h in range(2):
                        bank = ps_q[h]
                        if qk_evac_se[h]:
                            t.wait_ge(c.sem("se"), qk_evac_se[h])
                        if v_evac_ve[h]:
                            t.wait_ge(c.sem("ve"), v_evac_ve[h])
                        for g in range(8):
                            i = t.matmul(bank[:],
                                         lnxT[:, g, qi * 128:(qi + 1) * 128],
                                         w_sb[:, g, 2 * D + h * 512:
                                              2 * D + (h + 1) * 512],
                                         start=(g == 0), stop=(g == 7))
                            c.inc(i, "pe", 1)
                blk.tensor(tv)
                pe_v = c.val("pe")
                qk_evac_se[0] = 0
                qk_evac_se[1] = 0

                def vwb(v, par=par, pe_v=pe_v, sy_x=sy_x, ve_r=ve_r):
                    v = c.chain(v, "ve")
                    v.wait_ge(c.sem("pe"), pe_v)
                    v.wait_ge(c.sem(f"sx{par}"), sy_x)
                    if xo_tile_sy.get(par) is not None:
                        v.wait_ge(c.sem(f"so{par}"), xo_tile_sy[par])
                    for h in range(2):
                        v.tensor_tensor(out=v_sc[:, h * 512:(h + 1) * 512],
                                        in0=ps_q[h][:],
                                        in1=bv_bc[:, h * 512:(h + 1) * 512],
                                        op=ALU.add)
                    v.tensor_scalar(out=v_sc[:], in0=v_sc[:], scalar1=st[:, 11:12],
                                    scalar2=None, op0=ALU.mult)
                    v.tensor_tensor(out=xo_sb[:, par, :], in0=v_sc[:],
                                    in1=x_sb[:, par, :], op=ALU.add)
                blk.vector(vwb)
                ve_wb = c.val("ve")
                v_evac_ve[0] = ve_wb
                v_evac_ve[1] = ve_wb
                x_tile_ve[par] = ve_wb

                def stx(sy, par=par, row0=row0, ve_wb=ve_wb):
                    sy.wait_ge(c.sem("ve"), ve_wb)
                    i = sy.dma_start(out=x_out[row0:row0 + 128, :],
                                     in_=xo_sb[:, par, :])
                    c.inc(i, f"so{par}", 16)
                blk.sync(stx)
                xo_tile_sy[par] = c.val(f"so{par}")

            ln_copy_pe_guard[0] = c.val("pe")

        so0_t, so1_t = c.val("so0"), c.val("so1")

        def fin(sy, so0_t=so0_t, so1_t=so1_t):
            sy.wait_ge(c.sem("so0"), so0_t)
            sy.wait_ge(c.sem("so1"), so1_t)
        blk.sync(fin)


def _declare_l0attn_drams(nc, pfx=""):
    return dict(
        wqkv=nc.dram_tensor(pfx + "wqkv", [D, 3 * D], F32,
                            kind="ExternalInput").ap(),
        bqk=nc.dram_tensor(pfx + "bqk", [128, 16], F32,
                           kind="ExternalInput").ap(),
        bv=nc.dram_tensor(pfx + "bv", [1, D], F32, kind="ExternalInput").ap(),
    )


def build_l0attn():
    nc = bass.Bass("TRN2", target_bir_lowering=False, debug=False)
    aps = _declare_l0attn_drams(nc)
    aps.update(_declare_shared_consts(nc))
    aps["x_in"] = nc.dram_tensor("x_in", [TOK, D], F32, kind="ExternalInput").ap()
    aps["x_out"] = nc.dram_tensor("x_out", [TOK, D], F32,
                                  kind="ExternalOutput").ap()
    with contextlib.ExitStack() as es:
        c = Ctx(nc, es)
        PB = es.enter_context(nc.psum_tensor("PB", [128, 2048], F32))
        PM = es.enter_context(nc.psum_tensor("PM", [128, 2, 512], F32))
        PA = es.enter_context(nc.psum_tensor("PA", [128, 2, 512], F32))
        sbase, _ = nc.bump_sbuf(nc.sbuf_bytes_remaining - 64)
        blk = es.enter_context(nc.Block())
        emit_l0attn(nc, c, es, blk, aps, "a_", sbase, PB, PM, PA)
    return nc


def _l0attn_maps(inputs):
    Wq = np.asarray(inputs["L0_attn_Wq"], np.float32)
    Wk = np.asarray(inputs["L0_attn_Wk"], np.float32)
    Wv = np.asarray(inputs["L0_attn_Wv"], np.float32)
    bq = np.asarray(inputs["L0_attn_bq"], np.float32)
    bk = np.asarray(inputs["L0_attn_bk"], np.float32)
    bv = np.asarray(inputs["L0_attn_bv"], np.float32)
    wqkv = np.concatenate([Wq, Wk, Wv], axis=1)
    bqk = np.stack([bq.reshape(8, 128).T, bk.reshape(8, 128).T],
                   axis=1).reshape(128, 16)
    return dict(
        wqkv=np.ascontiguousarray(wqkv),
        bqk=np.ascontiguousarray(bqk),
        bv=bv.reshape(1, D),
    )


def run_l0attn(x_full, inputs, nc=None):
    if nc is None:
        nc = build_l0attn()
    base = _l0attn_maps(inputs)
    base.update(_shared_const_values())
    xr = np.asarray(x_full, np.float32).reshape(N_CORES, TOK, D)
    in_maps = [dict(base, x_in=np.ascontiguousarray(xr[i])) for i in range(N_CORES)]
    res = run_bass_kernel_spmd(nc, in_maps, core_ids=list(range(N_CORES)))
    out = np.stack([res.results[i]["x_out"] for i in range(N_CORES)])
    return out.reshape(16, 2048, 1024)



def emit_l1ffn(nc, c, es, blk, aps, pfx, sbase, pb_big, pb_mm, pb_at):
    x_in = aps["x_in"]
    wqkvo = aps["wqkvo"]
    bqk = aps["bqk"]
    bv_b = aps["bv_b"]
    bo_b = aps["bo_b"]
    ln_b = aps["ln_b"]
    gb_b = aps["gb_b"]
    gw = aps["gw"]
    w1d = aps["w1d"]
    w2d = aps["w2d"]
    b1d = aps["b1d"]
    b2d = aps["b2d"]
    id_f = aps["id_f"]
    ones_bf = aps["ones_bf"]
    ones_f = aps["ones_f"]
    neg1_bf = aps["neg1_bf"]
    x_out = aps["x_out"]
    combT_dram = aps["combT_dram"]
    m_dram = aps["m_dram"]

    NHEAD = 16
    ASCALE = 0.125
    NMT = SEQ // 128

    if True:
        nA, nB, nC, nD = 32768, 32768, 32768, 33280
        arena = sbase
        offA, offB, offC = arena, arena + nA, arena + nA + nB
        offD, offE = offC + nC, offC + nC + nD
        offW = offE + 32768
        cur = [offW + 16384]
        sba = _cursor_alloc(nc, pfx, cur)
        at = lambda n, s, d, offset: nc.alloc_sbuf_tensor_at(pfx + n, s, d,
                                                             offset=offset)
        fmA = at("fmA", [128, 8, SEQ], BF16, offset=offA)
        pT = at("pT", [128, 2, 512], BF16, offset=offA)
        m_row = at("m_row", [1, SEQ], BF16, offset=offA + 2048)
        m16 = at("m16", [16, 128], F32, offset=offA + 6144)
        rz_row = at("rz_row", [1, 512], F32, offset=offA + 6656)
        rz_bc = at("rz_bc", [64, 512], F32, offset=offA + 8704)
        qT = at("qT", [128, 8, SEQ], BF16, offset=offB)
        yaccT = at("yaccT", [128, 8, 1024], F32, offset=offB)
        kT = at("kT", [128, 8, SEQ], BF16, offset=offC)
        hT = at("hT", [128, 16, 1024], BF16, offset=offC)
        v_aug = at("v_aug", [128, 16, 16, 65], BF16, offset=offD)
        w1e = at("w1e", [128, 8, INTER], BF16, offset=offD)
        xi_tm = at("xi_tm", [128, 16, 1024], BF16, offset=offE)
        wch = at("wch", [128, 2, 8, 512], BF16, offset=offW)
        w2ch = at("w2ch", [128, 2, 4, 1024], BF16, offset=offW)

        x_sb = sba("x_sb", [128, D])
        f32a = sba("f32a", [128, D])
        f32b = sba("f32b", [128, D])
        st = sba("st", [128, 16])
        mcol = sba("mcol", [128, 16])
        gwork = sba("gwork", [128, 40])
        cmb8 = sba("cmb8", [8, 128])
        gw_sb = sba("gw_sb", [128, 8, 8], BF16)
        gb_bc = sba("gb_bc", [128, 8])
        bqk_sb = sba("bqk_sb", [128, 16])
        b1_sb = sba("b1_sb", [128, 128])
        b2_sb = sba("b2_sb", [128, 64])
        bv_bc = sba("bv_bc", [128, D], BF16)
        bo_bc = sba("bo_bc", [128, D], BF16)
        ln_bc = sba("ln_bc", [128, 4, D], BF16)
        id_f_sb = sba("id_f_sb", [128, 128], F32)
        ones_bf_sb = sba("ones_bf_sb", [1, 128], BF16)
        ones_f_sb = sba("ones_f_sb", [1, 128], F32)
        neg1_sb = sba("neg1_sb", [1, 128], BF16)
        comb_row = sba("comb_row", [1, 512])
        comb_sb = sba("comb_sb", [128, 2, 512], BF16)

        def pre_sync(sy):
            for dst, srcap in ((bqk_sb, bqk), (b1_sb, b1d), (b2_sb, b2d),
                               (gb_bc, gb_b), (id_f_sb, id_f), (ones_f_sb, ones_f)):
                i = sy.dma_start(out=dst[:], in_=srcap[:])
                c.inc(i, "syc", 16)
        blk.sync(pre_sync)
        syc_v = c.val("syc")

        def pre_gp(g):
            for dst, srcap in ((gw_sb[:], gw.rearrange("(cc p) j -> p cc j", p=128)),
                               (bv_bc[:], bv_b[:]), (bo_bc[:], bo_b[:]),
                               (ln_bc[:], ln_b.rearrange("p (l j) -> p l j", l=4)),
                               (ones_bf_sb[:], ones_bf[:]), (neg1_sb[:], neg1_bf[:])):
                i = g.dma_start(out=dst, in_=srcap)
                c.inc(i, "sgw", 16)
        blk.gpsimd(pre_gp)
        sgw_v = c.val("sgw")

        def gate_t(t):
            t.wait_ge(c.sem("syc"), syc_v)
            t.wait_ge(c.sem("sgw"), sgw_v)
        blk.tensor(gate_t)

        def gate_v(v):
            v.wait_ge(c.sem("syc"), syc_v)
            v.wait_ge(c.sem("sgw"), sgw_v)
        blk.vector(gate_v)

        def gate_s(s):
            s.wait_ge(c.sem("syc"), syc_v)
            s.wait_ge(c.sem("sgw"), sgw_v)
        blk.scalar(gate_s)

        x_tile_ve = [0]
        x_st_sy = [0]
        tr_pe = [0]
        ln_st1_ve = [0]
        wch_pe = {0: 0, 1: 0}
        pT_pe = {0: 0, 1: 0}
        stbank_se = {0: 0, 1: 0}
        stbank_ve = {0: 0, 1: 0}
        atbank_ve = {0: 0, 1: 0}
        bigbank_se = {j: 0 for j in range(4)}
        bigbank_ve = {j: 0 for j in range(4)}
        m_row_pe = [0]
        smd_v = [0]
        comb_row_pe = [0]

        def wait_big(t, banks):
            for j in banks:
                if bigbank_se[j]:
                    t.wait_ge(c.sem("se"), bigbank_se[j])
                if bigbank_ve[j]:
                    t.wait_ge(c.sem("ve"), bigbank_ve[j])

        def ln_stats(get_in, se_guard_extra=0):
            pass

        def emit_stats(src_ap, wait_list_v, wait_list_s):
            def vstats(v, wl=tuple(wait_list_v)):
                v = c.chain(v, "ve")
                for s_, val in wl:
                    v.wait_ge(c.sem(s_), val)
                v.reduce_sum(st[:, 0:1], src_ap, axis=AX.X)
            blk.vector(vstats)

            def sstats(sc, wl=tuple(wait_list_s), prev=ln_st1_ve[0]):
                sc = c.chain(sc, "se")
                for s_, val in wl:
                    sc.wait_ge(c.sem(s_), val)
                if prev:
                    sc.wait_ge(c.sem("ve"), prev)
                sc.activation(out=f32a[:], in_=src_ap, func=AF.Square,
                              accum_out=st[:, 1:2])
            blk.scalar(sstats)
            se_sq = c.val("se")

            def vmath(v, se_sq=se_sq):
                v = c.chain(v, "ve")
                v.wait_ge(c.sem("se"), se_sq)
                v.tensor_scalar(out=st[:, 2:3], in0=st[:, 0:1], scalar1=1.0 / D,
                                scalar2=None, op0=ALU.mult)
                v.tensor_scalar(out=st[:, 3:4], in0=st[:, 1:2], scalar1=1.0 / D,
                                scalar2=None, op0=ALU.mult)
                v.tensor_tensor(out=st[:, 4:5], in0=st[:, 2:3], in1=st[:, 2:3],
                                op=ALU.mult)
                v.tensor_tensor(out=st[:, 5:6], in0=st[:, 3:4], in1=st[:, 4:5],
                                op=ALU.subtract)
                v.tensor_scalar(out=st[:, 5:6], in0=st[:, 5:6], scalar1=EPS,
                                scalar2=None, op0=ALU.add)
            blk.vector(vmath)
            ln_st1_ve[0] = c.val("ve")
            ve_var = c.val("ve")

            def srstd(sc, ve_var=ve_var):
                sc = c.chain(sc, "se")
                sc.wait_ge(c.sem("ve"), ve_var)
                sc.activation(out=st[:, 6:7], in_=st[:, 5:6], func=AF.Ln)
                sc.activation(out=st[:, 7:8], in_=st[:, 6:7], func=AF.Exp,
                              scale=-0.5)
            blk.scalar(srstd)
            return c.val("se")

        def emit_tr_to_fm(dst3, mi, src_f32, ve_wait):
            def ttr(t, ve_wait=ve_wait):
                t.wait_ge(c.sem("ve"), ve_wait)
                wait_big(t, (0, 1))
                for g in range(8):
                    i = t.transpose(pb_big[:, g * 128:(g + 1) * 128],
                                    src_f32[:, g * 128:(g + 1) * 128], id_f_sb[:])
                    c.inc(i, "pe", 1)
            blk.tensor(ttr)
            pe_tr = c.val("pe")
            tr_pe[0] = pe_tr

            def vcp(v, pe_tr=pe_tr, mi=mi):
                v = c.chain(v, "ve")
                v.wait_ge(c.sem("pe"), pe_tr)
                v.tensor_copy(out=dst3[:, :, mi * 128:(mi + 1) * 128],
                              in_=pb_big[:, 0:1024].rearrange("p (g j) -> p g j",
                                                              g=8))
            blk.vector(vcp)
            bigbank_ve[0] = bigbank_ve[1] = c.val("ve")
            return c.val("ve")

        for b in range(B_PER_CORE):
            base_row = b * SEQ

            for m in range(NMT):
                row0 = base_row + m * 128

                def ld(sy, row0=row0, xv=x_tile_ve[0], xs=x_st_sy[0]):
                    if xv:
                        sy.wait_ge(c.sem("ve"), xv)
                    if xs:
                        sy.wait_ge(c.sem("sxo"), xs)
                    i = sy.dma_start(out=x_sb[:], in_=x_in[row0:row0 + 128, :])
                    c.inc(i, "sx", 16)
                blk.sync(ld)
                sy_here = c.val("sx")

                se_r = emit_stats(x_sb[:], [("sx", sy_here)], [("sx", sy_here)])

                def vln(v, se_r=se_r, m=m, tpg=tr_pe[0]):
                    v = c.chain(v, "ve")
                    v.wait_ge(c.sem("se"), se_r)
                    if tpg:
                        v.wait_ge(c.sem("pe"), tpg)
                    v.tensor_scalar(out=f32b[:], in0=x_sb[:], scalar1=st[:, 2:3],
                                    scalar2=st[:, 7:8], op0=ALU.subtract,
                                    op1=ALU.mult)
                    v.tensor_copy(out=xi_tm.ap()[:, m, :], in_=f32b[:])
                blk.vector(vln)
                x_tile_ve[0] = c.val("ve")
                emit_tr_to_fm(fmA.ap(), m, f32b[:], c.val("ve"))
            ve_src_done = c.val("ve")

            def vz(v, ve_src_done=ve_src_done, peg=c.val("pe")):
                v = c.chain(v, "ve")
                if peg:
                    v.wait_ge(c.sem("pe"), peg)
                v.memset(v_aug.ap()[:], 1.0)
            blk.vector(vz)

            for kind in range(2):
                dst = qT if kind == 0 else kT
                for jb in range(8):
                    wpar = (kind * 8 + jb) % 2

                    def lw(g, kind=kind, jb=jb, wpar=wpar, wg=c.val("pe"),
                           sgw_v=sgw_v):
                        g.wait_ge(c.sem("sgw"), sgw_v)
                        if wg:
                            g.wait_ge(c.sem("pe"), wg)
                        i = g.dma_start(
                            out=wch.ap()[:, wpar, :, 0:128],
                            in_=wqkvo.rearrange("(cc p) j -> p cc j", p=128)
                            [:, :, kind * D + jb * 128:kind * D + (jb + 1) * 128])
                        c.inc(i, f"swc{wpar}", 16)
                    blk.gpsimd(lw)
                    sw_here = c.val(f"swc{wpar}")

                    for tc in range(4):
                        pp = tc % 2

                        def tqk(t, jb=jb, tc=tc, pp=pp, wpar=wpar, sw_here=sw_here,
                                sbv=stbank_se[pp], vbv=stbank_ve[pp],
                                ve_src_done=ve_src_done):
                            t.wait_ge(c.sem(f"swc{wpar}"), sw_here)
                            t.wait_ge(c.sem("ve"), ve_src_done)
                            if sbv:
                                t.wait_ge(c.sem("se"), sbv)
                            if vbv:
                                t.wait_ge(c.sem("ve"), vbv)
                            for g in range(8):
                                i = t.matmul(pb_mm[:, pp, :], wch.ap()[:, wpar, g, 0:128],
                                             fmA.ap()[:, g, tc * 512:(tc + 1) * 512],
                                             start=(g == 0), stop=(g == 7))
                                c.inc(i, "pe", 1)
                        blk.tensor(tqk)
                        pe_qk = c.val("pe")
                        wch_pe[wpar] = pe_qk

                        def sqk(sc, jb=jb, tc=tc, pp=pp, pe_qk=pe_qk, kind=kind,
                                dst=dst):
                            sc = c.chain(sc, "se")
                            sc.wait_ge(c.sem("pe"), pe_qk)
                            sc.activation(out=dst.ap()[:, jb, tc * 512:(tc + 1) * 512],
                                          in_=pb_mm[:, pp, :], func=AF.Identity,
                                          bias=bqk_sb[:, kind * 8 + jb:
                                                      kind * 8 + jb + 1])
                        blk.scalar(sqk)
                        stbank_se[pp] = c.val("se")

            ve_vaug = c.val("ve")
            for half in range(2):
                def lwv(g, half=half, wg=c.val("pe")):
                    if wg:
                        g.wait_ge(c.sem("pe"), wg)
                    i = g.dma_start(
                        out=wch.ap()[:, half, :, :],
                        in_=wqkvo.rearrange("(cc p) j -> p cc j", p=128)
                        [:, :, 2 * D + half * 512:2 * D + (half + 1) * 512])
                    c.inc(i, f"swc{half}", 16)
                blk.gpsimd(lwv)
            sw_v = (c.val("swc0"), c.val("swc1"))
            for m in range(NMT):
                for half in range(2):
                    pp = (m * 2 + half) % 2

                    def tv(t, m=m, half=half, pp=pp, sw_v=sw_v,
                           sbv=stbank_se[pp], vbv=stbank_ve[pp]):
                        t.wait_ge(c.sem(f"swc{half}"), sw_v[half])
                        if sbv:
                            t.wait_ge(c.sem("se"), sbv)
                        if vbv:
                            t.wait_ge(c.sem("ve"), vbv)
                        for g in range(8):
                            i = t.matmul(pb_mm[:, pp, :],
                                         fmA.ap()[:, g, m * 128:(m + 1) * 128],
                                         wch.ap()[:, half, g, :],
                                         start=(g == 0), stop=(g == 7))
                            c.inc(i, "pe", 1)
                    blk.tensor(tv)
                    pe_v = c.val("pe")
                    wch_pe[half] = pe_v

                    def vv(v, m=m, half=half, pp=pp, pe_v=pe_v, ve_vaug=ve_vaug):
                        v = c.chain(v, "ve")
                        v.wait_ge(c.sem("pe"), pe_v)
                        v.tensor_tensor(
                            out=v_aug.ap()[:, m, 8 * half:8 * (half + 1), 0:64],
                            in0=pb_mm[:, pp, :].rearrange("p (hh dd) -> p hh dd",
                                                          hh=8),
                            in1=bv_bc[:, half * 512:(half + 1) * 512].rearrange(
                                "p (hh dd) -> p hh dd", hh=8),
                            op=ALU.add)
                    blk.vector(vv)
                    stbank_ve[pp] = c.val("ve")
            ve_qkv = c.val("ve")
            se_qkv = c.val("se")

            for h in [0, 2, 4, 6, 8, 10, 12, 14, 1, 3, 5, 7, 9, 11, 13, 15]:
                hb = h // 2
                ho = 64 * (h % 2)
                for qi in range(NMT):
                    def ts1(t, h=h, qi=qi, hb=hb, ho=ho, se_qkv=se_qkv,
                            ve_qkv=ve_qkv):
                        t.wait_ge(c.sem("se"), se_qkv)
                        t.wait_ge(c.sem("ve"), ve_qkv)
                        wait_big(t, (0, 1, 2, 3))
                        for j in range(4):
                            i = t.matmul(pb_big[:, j * 512:(j + 1) * 512],
                                         qT.ap()[ho:ho + 64, hb,
                                                 qi * 128:(qi + 1) * 128],
                                         kT.ap()[ho:ho + 64, hb,
                                                 j * 512:(j + 1) * 512],
                                         start=True, stop=True)
                            c.inc(i, "pe", 1)
                    blk.tensor(ts1)
                    pe_s1 = c.val("pe")

                    def vm1(v, qi=qi, pe_s1=pe_s1):
                        v = c.chain(v, "ve")
                        v.wait_ge(c.sem("pe"), pe_s1)
                        v.reduce_max(mcol[:, qi:qi + 1], pb_big[:], axis=AX.X)
                    blk.vector(vm1)
                    for j in range(4):
                        bigbank_ve[j] = c.val("ve")
                ve_mcol = c.val("ve")

                def tmc(t, ve_mcol=ve_mcol):
                    t.wait_ge(c.sem("ve"), ve_mcol)
                    if stbank_se[0]:
                        t.wait_ge(c.sem("se"), stbank_se[0])
                    if stbank_ve[0]:
                        t.wait_ge(c.sem("ve"), stbank_ve[0])
                    i = t.transpose(pb_mm[0:16, 0, 0:128], mcol[:], id_f_sb[:])
                    c.inc(i, "pe", 1)
                blk.tensor(tmc)
                pe_mc = c.val("pe")

                def vmc(v, pe_mc=pe_mc, smv=smd_v[0]):
                    v = c.chain(v, "ve")
                    v.wait_ge(c.sem("pe"), pe_mc)
                    if smv:
                        v.wait_ge(c.sem("smd"), smv)
                    v.tensor_copy(out=m16.ap()[:], in_=pb_mm[0:16, 0, 0:128])
                blk.vector(vmc)
                ve_m16 = c.val("ve")
                stbank_ve[0] = ve_m16

                def gmd(g, ve_m16=ve_m16, mrp=m_row_pe[0]):
                    g.wait_ge(c.sem("ve"), ve_m16)
                    i = g.dma_start(out=m_dram[:, :], in_=m16.ap()[:])
                    c.inc(i, "smd", 16)
                    g.wait_ge(c.sem("smd"), c.val("smd"))
                    if mrp:
                        g.wait_ge(c.sem("pe"), mrp)
                    i = g.dma_start(out=m_row.ap()[:, :],
                                    in_=m_dram.rearrange("a e -> (a e)")[None, :])
                    c.inc(i, "smd", 16)
                blk.gpsimd(gmd)
                smd_v[0] = c.val("smd")
                sml_v = c.val("smd")

                for qc in range(4):
                    atp = qc % 2
                    for kb in range(NMT):
                        stp = kb % 2

                        def tst(t, qc=qc, kb=kb, stp=stp, atp=atp, hb=hb, ho=ho,
                                sml_v=sml_v, sbv=stbank_se[stp], vbv=stbank_ve[stp],
                                abv=atbank_ve[atp], first=(kb == 0)):
                            t.wait_ge(c.sem("smd"), sml_v)
                            if sbv:
                                t.wait_ge(c.sem("se"), sbv)
                            if vbv:
                                t.wait_ge(c.sem("ve"), vbv)
                            if first and abv:
                                t.wait_ge(c.sem("ve"), abv)
                            i = t.matmul(pb_mm[:, stp, :],
                                         kT.ap()[ho:ho + 64, hb,
                                                 kb * 128:(kb + 1) * 128],
                                         qT.ap()[ho:ho + 64, hb,
                                                 qc * 512:(qc + 1) * 512],
                                         start=True, stop=False)
                            c.inc(i, "pe", 1)
                            i = t.matmul(pb_mm[:, stp, :], neg1_sb[:],
                                         m_row.ap()[0:1, qc * 512:(qc + 1) * 512],
                                         start=False, stop=True)
                            c.inc(i, "pe", 1)
                        blk.tensor(tst)
                        pe_st = c.val("pe")
                        m_row_pe[0] = pe_st
                        stbank_ve[stp] = 0

                        def sexp(sc, kb=kb, stp=stp, pe_st=pe_st, pg=pT_pe[stp]):
                            sc = c.chain(sc, "se")
                            sc.wait_ge(c.sem("pe"), pe_st)
                            if pg:
                                sc.wait_ge(c.sem("pe"), pg)
                            sc.activation(out=pT.ap()[:, stp, :],
                                          in_=pb_mm[:, stp, :], func=AF.Exp,
                                          scale=ASCALE)
                        blk.scalar(sexp)
                        stbank_se[stp] = c.val("se")
                        se_p = c.val("se")

                        def tpv(t, h=h, kb=kb, stp=stp, atp=atp, se_p=se_p):
                            t.wait_ge(c.sem("se"), se_p)
                            i = t.matmul(pb_at[0:65, atp, :],
                                         v_aug.ap()[:, kb, h, :], pT.ap()[:, stp, :],
                                         start=(kb == 0), stop=(kb == NMT - 1))
                            c.inc(i, "pe", 1)
                        blk.tensor(tpv)
                        pT_pe[stp] = c.val("pe")
                    pe_pv = c.val("pe")

                    def vrz(v, atp=atp, pe_pv=pe_pv):
                        v = c.chain(v, "ve")
                        v.wait_ge(c.sem("pe"), pe_pv)
                        v.reciprocal(rz_row.ap()[:, :], pb_at[64:65, atp, :])
                    blk.vector(vrz)
                    ve_rz = c.val("ve")

                    def trz(t, ve_rz=ve_rz, sbv=stbank_se[0], vbv=stbank_ve[0]):
                        t.wait_ge(c.sem("ve"), ve_rz)
                        if sbv:
                            t.wait_ge(c.sem("se"), sbv)
                        if vbv:
                            t.wait_ge(c.sem("ve"), vbv)
                        i = t.matmul(pb_mm[0:64, 0, :], ones_f_sb[0:1, 0:64],
                                     rz_row.ap()[:, :], start=True, stop=True)
                        c.inc(i, "pe", 1)
                    blk.tensor(trz)
                    pe_rz = c.val("pe")

                    def vat(v, qc=qc, atp=atp, pe_rz=pe_rz, hb=hb, ho=ho):
                        v = c.chain(v, "ve")
                        v.wait_ge(c.sem("pe"), pe_rz)
                        v.tensor_copy(out=rz_bc.ap()[:, :], in_=pb_mm[0:64, 0, :])
                        v.tensor_tensor(out=qT.ap()[ho:ho + 64, hb,
                                                    qc * 512:(qc + 1) * 512],
                                        in0=pb_at[0:64, atp, :], in1=rz_bc.ap()[:, :],
                                        op=ALU.mult)
                    blk.vector(vat)
                    atbank_ve[atp] = c.val("ve")
                    stbank_ve[0] = c.val("ve")
            ve_attn = c.val("ve")

            for half in range(2):
                def lwo(g, half=half, wg=c.val("pe")):
                    if wg:
                        g.wait_ge(c.sem("pe"), wg)
                    i = g.dma_start(
                        out=wch.ap()[:, half, :, :],
                        in_=wqkvo.rearrange("(cc p) j -> p cc j", p=128)
                        [:, :, 3 * D + half * 512:3 * D + (half + 1) * 512])
                    c.inc(i, f"swc{half}", 16)
                blk.gpsimd(lwo)
            sw_o = (c.val("swc0"), c.val("swc1"))

            for m in range(NMT):
                for half in range(2):
                    def two(t, m=m, half=half, sw_o=sw_o, ve_attn=ve_attn,
                            sbv=stbank_se[half], vbv=stbank_ve[half]):
                        t.wait_ge(c.sem(f"swc{half}"), sw_o[half])
                        t.wait_ge(c.sem("ve"), ve_attn)
                        if sbv:
                            t.wait_ge(c.sem("se"), sbv)
                        if vbv:
                            t.wait_ge(c.sem("ve"), vbv)
                        for g in range(8):
                            i = t.matmul(pb_mm[:, half, :],
                                         qT.ap()[:, g, m * 128:(m + 1) * 128],
                                         wch.ap()[:, half, g, :],
                                         start=(g == 0), stop=(g == 7))
                            c.inc(i, "pe", 1)
                    blk.tensor(two)
                    wch_pe[half] = c.val("pe")
                pe_wo = c.val("pe")

                def vx1(v, m=m, pe_wo=pe_wo, tpg=tr_pe[0]):
                    v = c.chain(v, "ve")
                    v.wait_ge(c.sem("pe"), pe_wo)
                    if tpg:
                        v.wait_ge(c.sem("pe"), tpg)
                    for half in range(2):
                        v.tensor_tensor(out=f32b[:, half * 512:(half + 1) * 512],
                                        in0=pb_mm[:, half, :],
                                        in1=bo_bc[:, half * 512:(half + 1) * 512],
                                        op=ALU.add)
                    v.tensor_tensor(out=f32b[:], in0=f32b[:],
                                    in1=xi_tm.ap()[:, m, :], op=ALU.add)
                blk.vector(vx1)
                ve_x1 = c.val("ve")
                stbank_ve[0] = stbank_ve[1] = ve_x1

                se_r1 = emit_stats(f32b[:], [("ve", ve_x1)], [("ve", ve_x1)])

                def vxint(v, m=m, se_r1=se_r1):
                    v = c.chain(v, "ve")
                    v.wait_ge(c.sem("se"), se_r1)
                    v.tensor_scalar(out=f32b[:], in0=f32b[:], scalar1=st[:, 2:3],
                                    scalar2=st[:, 7:8], op0=ALU.subtract,
                                    op1=ALU.mult)
                    v.tensor_tensor(out=f32b[:], in0=f32b[:], in1=ln_bc[:, 0, :],
                                    op=ALU.mult)
                    v.tensor_tensor(out=f32b[:], in0=f32b[:], in1=ln_bc[:, 1, :],
                                    op=ALU.add)
                    v.tensor_copy(out=xi_tm.ap()[:, m, :], in_=f32b[:])
                blk.vector(vxint)
                emit_tr_to_fm(fmA.ap(), m, f32b[:], c.val("ve"))
            ve_xint = c.val("ve")

            comb_st = [0]
            for m in range(NMT):
                def tlog(t, m=m, ve_xint=ve_xint, sgw_v=sgw_v,
                         sbv=stbank_se[0], vbv=stbank_ve[0]):
                    t.wait_ge(c.sem("sgw"), sgw_v)
                    t.wait_ge(c.sem("ve"), ve_xint)
                    if sbv:
                        t.wait_ge(c.sem("se"), sbv)
                    if vbv:
                        t.wait_ge(c.sem("ve"), vbv)
                    for g in range(8):
                        i = t.matmul(pb_mm[:, 0, 0:8],
                                     fmA.ap()[:, g, m * 128:(m + 1) * 128],
                                     gw_sb[:, g, :], start=(g == 0), stop=(g == 7))
                        c.inc(i, "pe", 1)
                blk.tensor(tlog)
                pe_log = c.val("pe")

                def vsm1(v, pe_log=pe_log):
                    v = c.chain(v, "ve")
                    v.wait_ge(c.sem("pe"), pe_log)
                    v.tensor_tensor(out=gwork[:, 0:8], in0=pb_mm[:, 0, 0:8],
                                    in1=gb_bc[:], op=ALU.add)
                    v.reduce_max(st[:, 8:9], gwork[:, 0:8], axis=AX.X, negate=True)
                blk.vector(vsm1)
                ve_nm = c.val("ve")
                stbank_ve[0] = ve_nm

                def sexpg(sc, ve_nm=ve_nm):
                    sc = c.chain(sc, "se")
                    sc.wait_ge(c.sem("ve"), ve_nm)
                    sc.activation(out=gwork[:, 0:8], in_=gwork[:, 0:8],
                                  func=AF.Exp, bias=st[:, 8:9],
                                  accum_out=st[:, 9:10])
                blk.scalar(sexpg)
                se_eg = c.val("se")
                stbank_se[0] = se_eg

                def vsm2(v, se_eg=se_eg, tpg=tr_pe[0]):
                    v = c.chain(v, "ve")
                    v.wait_ge(c.sem("se"), se_eg)
                    if tpg:
                        v.wait_ge(c.sem("pe"), tpg)
                    v.reciprocal(st[:, 10:11], st[:, 9:10])
                    v.tensor_scalar(out=gwork[:, 8:16], in0=gwork[:, 0:8],
                                    scalar1=st[:, 10:11], scalar2=None, op0=ALU.mult)
                    v.reduce_max(st[:, 11:12], gwork[:, 8:16], axis=AX.X)
                    v.tensor_scalar(out=gwork[:, 24:32], in0=gwork[:, 8:16],
                                    scalar1=st[:, 11:12], scalar2=None,
                                    op0=ALU.is_equal)
                    v.scalar_tensor_tensor(out=gwork[:, 24:32], in0=gwork[:, 24:32],
                                           scalar=NEG_BIG, in1=gwork[:, 8:16],
                                           op0=ALU.mult, op1=ALU.add)
                    v.reduce_max(st[:, 12:13], gwork[:, 24:32], axis=AX.X)
                    v.tensor_scalar(out=gwork[:, 32:40], in0=gwork[:, 8:16],
                                    scalar1=st[:, 12:13], scalar2=None, op0=ALU.is_ge)
                    v.tensor_tensor(out=gwork[:, 16:24], in0=gwork[:, 8:16],
                                    in1=gwork[:, 32:40], op=ALU.mult)
                blk.vector(vsm2)
                ve_cb = c.val("ve")

                def tcb(t, ve_cb=ve_cb):
                    t.wait_ge(c.sem("ve"), ve_cb)
                    wait_big(t, (0,))
                    i = t.transpose(pb_big[0:8, 0:128], gwork[:, 16:24], id_f_sb[:])
                    c.inc(i, "pe", 1)
                blk.tensor(tcb)
                pe_cb = c.val("pe")
                tr_pe[0] = pe_cb

                def vcb(v, pe_cb=pe_cb, prev=comb_st[0]):
                    v = c.chain(v, "ve")
                    v.wait_ge(c.sem("pe"), pe_cb)
                    if prev:
                        v.wait_ge(c.sem("scb"), prev)
                    v.tensor_copy(out=cmb8[:], in_=pb_big[0:8, 0:128])
                blk.vector(vcb)
                ve_cbc = c.val("ve")
                bigbank_ve[0] = ve_cbc

                def stcb(sy, m=m, b=b, ve_cbc=ve_cbc):
                    sy.wait_ge(c.sem("ve"), ve_cbc)
                    col0 = b * SEQ + m * 128
                    i = sy.dma_start(out=combT_dram[:, col0:col0 + 128], in_=cmb8[:])
                    c.inc(i, "scb", 16)
                blk.sync(stcb)
                comb_st[0] = c.val("scb")
            scb_done = c.val("scb")

            h_reads_pe = [0]
            w1_reads_pe = [0]
            w2_reads_pe = {0: 0, 1: 0}
            psy_se = {j: 0 for j in range(8)}
            yacc_tr_pe = [0]

            def psy_bank(dgrp):
                if dgrp < 4:
                    return pb_big[:, dgrp * 512:(dgrp + 1) * 512]
                if dgrp < 6:
                    return pb_mm[:, dgrp - 4, :]
                return pb_at[:, dgrp - 6, :]

            for sblk in range(2):
                tok0 = b * SEQ + sblk * 1024

                def vzy(v, ytp=max(yacc_tr_pe[0], wch_pe[0], wch_pe[1])):
                    v = c.chain(v, "ve")
                    if ytp:
                        v.wait_ge(c.sem("pe"), ytp)
                    v.memset(yaccT.ap()[:], 0.0)
                blk.vector(vzy)
                ve_yz = c.val("ve")

                for e in range(8):
                    def lw1(g, e=e, wrp=max(w1_reads_pe[0], pT_pe[0], pT_pe[1]),
                            scd=scb_done):
                        g.wait_ge(c.sem("scb"), scd)
                        if wrp:
                            g.wait_ge(c.sem("pe"), wrp)
                        i = g.dma_start(out=w1e.ap()[:],
                                        in_=w1d[e].rearrange("(cc p) i -> p cc i",
                                                             p=128))
                        c.inc(i, "sw1", 16)
                    blk.gpsimd(lw1)
                    sw1_v = c.val("sw1")

                    for tc2 in range(2):
                        def ldcr(sy, e=e, tc2=tc2, tok0=tok0, crp=comb_row_pe[0],
                                 scd=scb_done):
                            sy.wait_ge(c.sem("scb"), scd)
                            if crp:
                                sy.wait_ge(c.sem("pe"), crp)
                            col0 = tok0 + tc2 * 512
                            i = sy.dma_start(out=comb_row[:],
                                             in_=combT_dram[e:e + 1,
                                                            col0:col0 + 512])
                            c.inc(i, "scr", 16)
                        blk.sync(ldcr)
                        scr_v = c.val("scr")

                        def tbc(t, tc2=tc2, scr_v=scr_v, sbv=stbank_se[0],
                                vbv=stbank_ve[0]):
                            t.wait_ge(c.sem("scr"), scr_v)
                            if sbv:
                                t.wait_ge(c.sem("se"), sbv)
                            if vbv:
                                t.wait_ge(c.sem("ve"), vbv)
                            i = t.matmul(pb_mm[:, 0, :], ones_f_sb[:], comb_row[:],
                                         start=True, stop=True)
                            c.inc(i, "pe", 1)
                        blk.tensor(tbc)
                        pe_bc = c.val("pe")
                        comb_row_pe[0] = pe_bc

                        def vbc(v, tc2=tc2, pe_bc=pe_bc):
                            v = c.chain(v, "ve")
                            v.wait_ge(c.sem("pe"), pe_bc)
                            v.tensor_copy(out=comb_sb[:, tc2, :], in_=pb_mm[:, 0, :])
                        blk.vector(vbc)
                        stbank_ve[0] = c.val("ve")
                    ve_compb = c.val("ve")

                    for tc2 in range(2):
                        tc = sblk * 2 + tc2
                        for ib in range(16):
                            pp = ib % 2

                            def tA(t, e=e, tc=tc, tc2=tc2, ib=ib, pp=pp,
                                   sw1_v=sw1_v, sbv=bigbank_se[pp],
                                   vbv=bigbank_ve[pp], vyz=ve_yz):
                                t.wait_ge(c.sem("sw1"), sw1_v)
                                t.wait_ge(c.sem("ve"), vyz)
                                if sbv:
                                    t.wait_ge(c.sem("se"), sbv)
                                if vbv:
                                    t.wait_ge(c.sem("ve"), vbv)
                                for g in range(8):
                                    i = t.matmul(pb_big[:, pp * 512:(pp + 1) * 512],
                                                 w1e.ap()[:, g, ib * 128:(ib + 1) * 128],
                                                 fmA.ap()[:, g, tc * 512:(tc + 1) * 512],
                                                 start=(g == 0), stop=(g == 7))
                                    c.inc(i, "pe", 1)
                            blk.tensor(tA)
                            pe_A = c.val("pe")
                            w1_reads_pe[0] = pe_A

                            def sA(sc, e=e, ib=ib, tc2=tc2, pp=pp, pe_A=pe_A,
                                   hrp=h_reads_pe[0]):
                                sc = c.chain(sc, "se")
                                sc.wait_ge(c.sem("pe"), pe_A)
                                if hrp:
                                    sc.wait_ge(c.sem("pe"), hrp)
                                sc.activation(out=hT.ap()[:, ib,
                                                          tc2 * 512:(tc2 + 1) * 512],
                                              in_=pb_big[:, pp * 512:(pp + 1) * 512],
                                              func=AF.Relu,
                                              bias=b1_sb[:, e * 16 + ib:
                                                         e * 16 + ib + 1])
                            blk.scalar(sA)
                            bigbank_se[pp] = c.val("se")
                    se_hdone = c.val("se")

                    for tc2 in range(2):
                        for icg in range(4):
                            wpar = icg % 2

                            def lw2(g, e=e, icg=icg, wpar=wpar,
                                    wrp=w2_reads_pe[wpar] if icg >= 2
                                    else c.val("pe")):
                                if wrp:
                                    g.wait_ge(c.sem("pe"), wrp)
                                i = g.dma_start(
                                    out=w2ch.ap()[:, wpar, :, :],
                                    in_=w2d[e].rearrange("(cc p) j -> p cc j", p=128)
                                    [:, icg * 4:(icg + 1) * 4, :])
                                c.inc(i, f"sw2{wpar}", 16)
                            blk.gpsimd(lw2)
                            sw2_v = c.val(f"sw2{wpar}")

                            def tB(t, e=e, tc2=tc2, icg=icg, wpar=wpar,
                                   sw2_v=sw2_v, se_hdone=se_hdone,
                                   psw=[psy_se[j] for j in range(8)],
                                   veg=c.val("ve")):
                                t.wait_ge(c.sem(f"sw2{wpar}"), sw2_v)
                                t.wait_ge(c.sem("se"), se_hdone)
                                if icg == 0:
                                    t.wait_ge(c.sem("ve"), veg)
                                    for j in range(8):
                                        if psw[j]:
                                            t.wait_ge(c.sem("se"), psw[j])
                                for ici in range(4):
                                    ic = icg * 4 + ici
                                    for dgrp in range(8):
                                        i = t.matmul(
                                            psy_bank(dgrp),
                                            w2ch.ap()[:, wpar, ici,
                                                      dgrp * 128:(dgrp + 1) * 128],
                                            hT.ap()[:, ic, tc2 * 512:(tc2 + 1) * 512],
                                            start=(ic == 0), stop=(ic == 15))
                                        c.inc(i, "pe", 1)
                            blk.tensor(tB)
                            pe_B = c.val("pe")
                            w2_reads_pe[wpar] = pe_B
                            h_reads_pe[0] = pe_B
                        pe_Ball = c.val("pe")

                        f32a_ve = {0: 0, 1: 0}
                        for dgrp in range(8):
                            def sev(sc, e=e, dgrp=dgrp, pe_Ball=pe_Ball,
                                    par=dgrp % 2, fav=None):
                                sc = c.chain(sc, "se")
                                sc.wait_ge(c.sem("pe"), pe_Ball)
                                if fav:
                                    sc.wait_ge(c.sem("ve"), fav)
                                sc.activation(out=f32a[:, par * 512:(par + 1) * 512],
                                              in_=psy_bank(dgrp), func=AF.Identity,
                                              bias=b2_sb[:, e * 8 + dgrp:
                                                         e * 8 + dgrp + 1])
                            sev.__defaults__ = sev.__defaults__[:-1] + (
                                f32a_ve[dgrp % 2],)
                            blk.scalar(sev)
                            psy_se[dgrp] = c.val("se")
                            se_ev = c.val("se")

                            def vev(v, e=e, tc2=tc2, dgrp=dgrp, se_ev=se_ev,
                                    par=dgrp % 2, ve_compb=ve_compb):
                                v = c.chain(v, "ve")
                                v.wait_ge(c.sem("se"), se_ev)
                                v.scalar_tensor_tensor(
                                    out=f32b[:, par * 512:(par + 1) * 512],
                                    in0=f32a[:, par * 512:(par + 1) * 512],
                                    scalar=1.0, in1=comb_sb[:, tc2, :],
                                    op0=ALU.mult, op1=ALU.mult)
                                v.tensor_tensor(
                                    out=yaccT.ap()[:, dgrp, tc2 * 512:(tc2 + 1) * 512],
                                    in0=yaccT.ap()[:, dgrp, tc2 * 512:(tc2 + 1) * 512],
                                    in1=f32b[:, par * 512:(par + 1) * 512],
                                    op=ALU.add)
                            blk.vector(vev)
                            f32a_ve[dgrp % 2] = c.val("ve")
                        ve_evd = c.val("ve")
                        stbank_ve[0] = stbank_ve[1] = ve_evd
                        atbank_ve[0] = atbank_ve[1] = ve_evd
                ve_moe = c.val("ve")

                for msb in range(8):
                    m = sblk * 8 + msb
                    row0 = b * SEQ + m * 128

                    def twb(t, msb=msb, ve_moe=ve_moe):
                        t.wait_ge(c.sem("ve"), ve_moe)
                        wait_big(t, (0, 1))
                        for g in range(8):
                            i = t.transpose(pb_big[:, g * 128:(g + 1) * 128],
                                            yaccT.ap()[:, g, msb * 128:(msb + 1) * 128],
                                            id_f_sb[:])
                            c.inc(i, "pe", 1)
                    blk.tensor(twb)
                    pe_wb = c.val("pe")
                    yacc_tr_pe[0] = pe_wb

                    def vx2(v, m=m, pe_wb=pe_wb, tpg=tr_pe[0]):
                        v = c.chain(v, "ve")
                        v.wait_ge(c.sem("pe"), pe_wb)
                        v.tensor_tensor(out=f32b[:],
                                        in0=pb_big[:, 0:1024],
                                        in1=xi_tm.ap()[:, m, :], op=ALU.add)
                    blk.vector(vx2)
                    ve_x2 = c.val("ve")
                    bigbank_ve[0] = bigbank_ve[1] = ve_x2

                    se_r2 = emit_stats(f32b[:], [("ve", ve_x2)], [("ve", ve_x2)])

                    def ldx2(sy, row0=row0, xv=x_tile_ve[0], xs=x_st_sy[0]):
                        if xv:
                            sy.wait_ge(c.sem("ve"), xv)
                        if xs:
                            sy.wait_ge(c.sem("sxo"), xs)
                        i = sy.dma_start(out=x_sb[:], in_=x_in[row0:row0 + 128, :])
                        c.inc(i, "sx", 16)
                    blk.sync(ldx2)
                    sy_x2 = c.val("sx")

                    def vout(v, se_r2=se_r2, sy_x2=sy_x2):
                        v = c.chain(v, "ve")
                        v.wait_ge(c.sem("se"), se_r2)
                        v.wait_ge(c.sem("sx"), sy_x2)
                        v.tensor_scalar(out=f32b[:], in0=f32b[:], scalar1=st[:, 2:3],
                                        scalar2=st[:, 7:8], op0=ALU.subtract,
                                        op1=ALU.mult)
                        v.tensor_tensor(out=f32b[:], in0=f32b[:], in1=ln_bc[:, 2, :],
                                        op=ALU.mult)
                        v.tensor_tensor(out=f32b[:], in0=f32b[:], in1=ln_bc[:, 3, :],
                                        op=ALU.add)
                        v.scalar_tensor_tensor(out=x_sb[:], in0=f32b[:], scalar=0.5,
                                               in1=x_sb[:], op0=ALU.mult,
                                               op1=ALU.add)
                    blk.vector(vout)
                    ve_out = c.val("ve")
                    x_tile_ve[0] = ve_out

                    def stx(sy, row0=row0, ve_out=ve_out):
                        sy.wait_ge(c.sem("ve"), ve_out)
                        i = sy.dma_start(out=x_out[row0:row0 + 128, :], in_=x_sb[:])
                        c.inc(i, "sxo", 16)
                    blk.sync(stx)
                    x_st_sy[0] = c.val("sxo")

        sxo_t = c.val("sxo")

        def fin(sy, sxo_t=sxo_t):
            sy.wait_ge(c.sem("sxo"), sxo_t)
        blk.sync(fin)


def _declare_l1ffn_drams(nc, pfx=""):
    return dict(
        wqkvo=nc.dram_tensor(pfx + "wqkvo", [D, 4 * D], F32,
                             kind="ExternalInput").ap(),
        bqk=nc.dram_tensor(pfx + "bqk", [128, 16], F32,
                           kind="ExternalInput").ap(),
        bv_b=nc.dram_tensor(pfx + "bv_b", [128, D], F32,
                            kind="ExternalInput").ap(),
        bo_b=nc.dram_tensor(pfx + "bo_b", [128, D], F32,
                            kind="ExternalInput").ap(),
        ln_b=nc.dram_tensor(pfx + "ln_b", [128, 4 * D], F32,
                            kind="ExternalInput").ap(),
        gb_b=nc.dram_tensor(pfx + "gb_b", [128, 8], F32,
                            kind="ExternalInput").ap(),
        gw=nc.dram_tensor(pfx + "gw", [D, 8], F32, kind="ExternalInput").ap(),
        w1d=nc.dram_tensor(pfx + "w1d", [8, D, INTER], F32,
                           kind="ExternalInput").ap(),
        w2d=nc.dram_tensor(pfx + "w2d", [8, INTER, D], F32,
                           kind="ExternalInput").ap(),
        b1d=nc.dram_tensor(pfx + "b1d", [128, 128], F32,
                           kind="ExternalInput").ap(),
        b2d=nc.dram_tensor(pfx + "b2d", [128, 64], F32,
                           kind="ExternalInput").ap(),
    )


def build_l1ffn():
    nc = bass.Bass("TRN2", target_bir_lowering=False, debug=False)
    aps = _declare_l1ffn_drams(nc)
    aps.update(_declare_shared_consts(nc))
    aps["x_in"] = nc.dram_tensor("x_in", [TOK, D], F32, kind="ExternalInput").ap()
    aps["x_out"] = nc.dram_tensor("x_out", [TOK, D], F32,
                                  kind="ExternalOutput").ap()
    aps["combT_dram"] = nc.dram_tensor("combT_dram", [8, TOK], F32,
                                       kind="Internal").ap()
    aps["m_dram"] = nc.dram_tensor("m_dram", [16, 128], F32,
                                   kind="Internal").ap()
    with contextlib.ExitStack() as es:
        c = Ctx(nc, es)
        PB = es.enter_context(nc.psum_tensor("PB", [128, 2048], F32))
        PM = es.enter_context(nc.psum_tensor("PM", [128, 2, 512], F32))
        PA = es.enter_context(nc.psum_tensor("PA", [128, 2, 512], F32))
        sbase, _ = nc.bump_sbuf(nc.sbuf_bytes_remaining - 64)
        blk = es.enter_context(nc.Block())
        emit_l1ffn(nc, c, es, blk, aps, "q_", sbase, PB, PM, PA)
    return nc


def _l1ffn_maps(inputs):
    Wq = np.asarray(inputs["L1_ffn_Wq"], np.float32)
    Wk = np.asarray(inputs["L1_ffn_Wk"], np.float32)
    Wv = np.asarray(inputs["L1_ffn_Wv"], np.float32)
    Wo = np.asarray(inputs["L1_ffn_Wo"], np.float32)
    bq = np.asarray(inputs["L1_ffn_bq"], np.float32)
    bk = np.asarray(inputs["L1_ffn_bk"], np.float32)
    bv = np.asarray(inputs["L1_ffn_bv"], np.float32)
    bo = np.asarray(inputs["L1_ffn_bo"], np.float32)
    b1 = np.asarray(inputs["L1_ffn_b1"], np.float32)
    b2 = np.asarray(inputs["L1_ffn_b2"], np.float32)
    lnq = np.stack([np.asarray(inputs["L1_ffn_ln1_g"], np.float32),
                    np.asarray(inputs["L1_ffn_ln1_b"], np.float32),
                    np.asarray(inputs["L1_ffn_ln2_g"], np.float32),
                    np.asarray(inputs["L1_ffn_ln2_b"], np.float32)])
    return dict(
        wqkvo=np.ascontiguousarray(np.concatenate([Wq, Wk, Wv, Wo], axis=1)),
        bqk=np.ascontiguousarray(
            np.stack([bq.reshape(8, 128).T, bk.reshape(8, 128).T],
                     axis=1).reshape(128, 16)),
        bv_b=np.ascontiguousarray(np.broadcast_to(bv, (128, D))),
        bo_b=np.ascontiguousarray(np.broadcast_to(bo, (128, D))),
        ln_b=np.ascontiguousarray(
            np.broadcast_to(lnq.reshape(1, 4 * D), (128, 4 * D))),
        gb_b=np.ascontiguousarray(
            np.broadcast_to(np.asarray(inputs["L1_ffn_gb"], np.float32), (128, 8))),
        gw=np.ascontiguousarray(np.asarray(inputs["L1_ffn_gW"], np.float32)),
        w1d=np.ascontiguousarray(np.asarray(inputs["L1_ffn_W1"], np.float32)),
        w2d=np.ascontiguousarray(np.asarray(inputs["L1_ffn_W2"], np.float32)),
        b1d=np.ascontiguousarray(b1.reshape(8, 16, 128).transpose(2, 0, 1)
                                 .reshape(128, 128)),
        b2d=np.ascontiguousarray(b2.reshape(8, 8, 128).transpose(2, 0, 1)
                                 .reshape(128, 64)),
    )


def run_l1ffn(x_full, inputs, nc=None):
    if nc is None:
        nc = build_l1ffn()
    base = _l1ffn_maps(inputs)
    base.update(_shared_const_values())
    xr = np.asarray(x_full, np.float32).reshape(N_CORES, TOK, D)
    in_maps = [dict(base, x_in=np.ascontiguousarray(xr[i])) for i in range(N_CORES)]
    res = run_bass_kernel_spmd(nc, in_maps, core_ids=list(range(N_CORES)))
    out = np.stack([res.results[i]["x_out"] for i in range(N_CORES)])
    return out.reshape(16, 2048, 1024)



def emit_l1attn(nc, c, es, blk, aps, pfx, sbase, pb_big, pb_mm, pb_at):
    x_in = aps["x_in"]
    w1T = aps["w1T"]
    w2T = aps["w2T"]
    inpT = aps["inpT"]
    xprojT_d = aps["xprojT_d"]
    dtprojT_d = aps["dtprojT_d"]
    outT_d = aps["outT_d"]
    A_d = aps["A_d"]
    Dcol_d = aps["Dcol_d"]
    convw_d = aps["convw_d"]
    convb_d = aps["convb_d"]
    dtpb_d = aps["dtpb_d"]
    b1pp_d = aps["b1pp_d"]
    b2pp_d = aps["b2pp_d"]
    lnbc_d = aps["lnbc_d"]
    id_f = aps["id_f"]
    id_bf = aps["id_bf"]
    ones_f = aps["ones_f"]
    ones_bfd = aps["ones_bf"]
    x_out = aps["x_out"]
    v_dram = aps["v_dram"]
    qx_dram = aps["qx_dram"]
    z_dram = aps["z_dram"]
    bc_dram = aps["bc_dram"]
    gq_dram = aps["gq_dram"]
    cc_in_h = aps["cc_in_h"]
    cc_out_h = aps["cc_out_h"]
    cc_in_d = cc_in_h.ap()
    cc_out_d = cc_out_h.ap()

    QCLIP = 128.0 - EPS

    if True:
        nRA, nRB, nRC = 65664, 24576, 24576
        RA, RB, RC = sbase, sbase + nRA, sbase + nRA + nRB
        cur = [sbase + nRA + nRB + nRC]
        sba = _cursor_alloc(nc, pfx, cur)
        at = lambda n, s, d, offset: nc.alloc_sbuf_tensor_at(pfx + n, s, d,
                                                             offset=offset)
        convbuf = at("convbuf", [128, 16, 2052], BF16, offset=RA)
        gatedT = at("gatedT", [128, 16, 512], BF16, offset=RA)
        u2f = at("u2f", [128, 8, 512], F32, offset=RA + 16384)
        u0_tc = at("u0_tc", [128, 4, D], F32, offset=RA + 32768)
        wf32 = at("wf32", [128, 2, 8, 128], F32, offset=RA + 49152)
        ws_bf = at("ws_bf", [128, 2, 8, 128], BF16, offset=RA + 57344)
        w2f = at("w2f", [128, 2, 1024], F32, offset=RB)
        w2s = at("w2s", [128, 2, 1024], BF16, offset=RB + 8192)
        wk_bf = at("wk_bf", [128, 2, 8, 256], BF16, offset=RB)
        dt_f32 = at("dt_f32", [128, SEQ], F32, offset=RB)
        dtx_bf = at("dtx_bf", [128, SEQ], BF16, offset=RB + 8192)
        a_bf = at("a_bf", [128, SEQ], BF16, offset=RB + 12288)
        h_bf = at("h_bf", [128, SEQ], BF16, offset=RB + 16384)
        outw = at("outw", [128, 2, 2, 1024], BF16, offset=RB)
        outf32 = at("outf32", [128, 8, 512], F32, offset=RB + 8192)
        t1T = at("t1T", [128, 8, 512], BF16, offset=RC)
        qx_t = at("qx_t", [128, D], BF16, offset=RC + 8192)
        zst = at("zst", [128, 512], BF16, offset=RC + 10240)
        bcbuf = at("bcbuf", [128, SEQ], F32, offset=RC)
        xprojT_sb = at("xprojT_sb", [128, 16, 96], BF16, offset=RC + 8192)
        y_f32 = at("y_f32", [128, SEQ], F32, offset=RC)
        dtb2 = at("dtb2", [128, SEQ], F32, offset=RB + 12288)
        zsil = at("zsil", [128, SEQ], BF16, offset=RC + 8192)
        brow = at("brow", [1, 2, SEQ], BF16, offset=RC + 12288)
        dtlowT = at("dtlowT", [64, SEQ], BF16, offset=RC + 20480)

        x_sb = sba("x_sb", [128, D])
        f32a = sba("f32a", [128, D])
        f32b = sba("f32b", [128, D])
        f32c = sba("f32c", [128, D])
        f32d = sba("f32d", [128, 512])
        st = sba("st", [128, 16])
        qpart = sba("qpart", [128, 4])
        sclpp = sba("sclpp", [128, 4])
        qloc = sba("qloc", [4, 1])
        scl = sba("scl", [1, 16])
        gval = sba("gval", [1, 128])
        cc_sb = sba("cc_sb", [128, 128])
        A_sb = sba("A_sb", [128, 256])
        Dcol = sba("Dcol", [128, 16])
        convw = sba("convw", [128, 64])
        convb = sba("convb", [128, 16])
        dtpb = sba("dtpb", [128, 16])
        b1pp = sba("b1pp", [128, 16])
        b2pp = sba("b2pp", [128, 8])
        lnbc = sba("lnbc", [128, 4, D], BF16)
        dtprojT_sb = sba("dtprojT_sb", [64, INTER], BF16)
        id_f_sb = sba("id_f_sb", [128, 128], F32)
        id_bf_sb = sba("id_bf_sb", [128, 128], BF16)
        ones_f_sb = sba("ones_f_sb", [1, 128], F32)
        ones_bf_sb = sba("ones_bf_sb", [1, 128], BF16)
        diag = sba("diag", [128, 4, 128], BF16)

        def psy_bank(j):
            if j < 4:
                return pb_big[:, j * 512:(j + 1) * 512]
            if j < 6:
                return pb_mm[:, j - 4, :]
            return pb_at[:, j - 6, :]

        def pre_sync(sy):
            for dst, srcap in ((A_sb[:], A_d[:]), (Dcol[:], Dcol_d[:]),
                               (convw[:], convw_d[:]), (convb[:], convb_d[:]),
                               (dtpb[:], dtpb_d[:]), (b1pp[:], b1pp_d[:]),
                               (b2pp[:], b2pp_d[:]), (id_f_sb[:], id_f[:]),
                               (id_bf_sb[:], id_bf[:]), (ones_f_sb[:], ones_f[:]),
                               (ones_bf_sb[:], ones_bfd[:])):
                i = sy.dma_start(out=dst, in_=srcap)
                c.inc(i, "syc", 16)
        blk.sync(pre_sync)

        def pre_gp(g):
            for dst, srcap in ((lnbc[:], lnbc_d.rearrange("p (l j) -> p l j", l=4)),
                               (dtprojT_sb[:], dtprojT_d[:])):
                i = g.dma_start(out=dst, in_=srcap)
                c.inc(i, "sgw", 16)
        blk.gpsimd(pre_gp)
        syc_v = c.val("syc")
        sgw_v = c.val("sgw")

        def pre_s(sc):
            sc = c.chain(sc, "se")
            sc.wait_ge(c.sem("syc"), syc_v)
            sc.activation(out=A_sb[:], in_=A_sb[:], func=AF.Exp)
        blk.scalar(pre_s)
        se_A = c.val("se")

        def pre_v(v, se_A=se_A):
            v = c.chain(v, "ve")
            v.wait_ge(c.sem("se"), se_A)
            v.wait_ge(c.sem("sgw"), sgw_v)
            v.tensor_scalar(out=A_sb[:], in0=A_sb[:], scalar1=-1.0, scalar2=None,
                            op0=ALU.mult)
            v.memset(qpart[:], 0.0)
            v.memset(cc_sb[:], 0.0)
        blk.vector(pre_v)

        c.barrier(blk)

        ln_st1_ve = [0]

        def emit_stats(src_ap, wv):
            def vst(v, wl=tuple(wv)):
                v = c.chain(v, "ve")
                for s_, val in wl:
                    v.wait_ge(c.sem(s_), val)
                v.reduce_sum(st[:, 0:1], src_ap, axis=AX.X)
            blk.vector(vst)

            def sst(sc, wl=tuple(wv), prev=ln_st1_ve[0]):
                sc = c.chain(sc, "se")
                for s_, val in wl:
                    sc.wait_ge(c.sem(s_), val)
                if prev:
                    sc.wait_ge(c.sem("ve"), prev)
                sc.activation(out=f32a[:], in_=src_ap, func=AF.Square,
                              accum_out=st[:, 1:2])
            blk.scalar(sst)
            se_sq = c.val("se")

            def vmt(v, se_sq=se_sq):
                v = c.chain(v, "ve")
                v.wait_ge(c.sem("se"), se_sq)
                v.tensor_scalar(out=st[:, 2:3], in0=st[:, 0:1], scalar1=1.0 / D,
                                scalar2=None, op0=ALU.mult)
                v.tensor_scalar(out=st[:, 3:4], in0=st[:, 1:2], scalar1=1.0 / D,
                                scalar2=None, op0=ALU.mult)
                v.tensor_tensor(out=st[:, 4:5], in0=st[:, 2:3], in1=st[:, 2:3],
                                op=ALU.mult)
                v.tensor_tensor(out=st[:, 5:6], in0=st[:, 3:4], in1=st[:, 4:5],
                                op=ALU.subtract)
                v.tensor_scalar(out=st[:, 5:6], in0=st[:, 5:6], scalar1=EPS,
                                scalar2=None, op0=ALU.add)
            blk.vector(vmt)
            ln_st1_ve[0] = c.val("ve")
            ve_var = c.val("ve")

            def srs(sc, ve_var=ve_var):
                sc = c.chain(sc, "se")
                sc.wait_ge(c.sem("ve"), ve_var)
                sc.activation(out=st[:, 6:7], in_=st[:, 5:6], func=AF.Ln)
                sc.activation(out=st[:, 7:8], in_=st[:, 6:7], func=AF.Exp,
                              scale=-0.5)
            blk.scalar(srs)
            return c.val("se")

        for tcg in range(8):
            tok0 = tcg * 512
            q_idx = tcg % 4
            for ti in range(4):
                row0 = tok0 + ti * 128

                def ld(sy, row0=row0, xv=c.val("ve")):
                    if xv:
                        sy.wait_ge(c.sem("ve"), xv)
                    i = sy.dma_start(out=x_sb[:], in_=x_in[row0:row0 + 128, :])
                    c.inc(i, "sx", 16)
                blk.sync(ld)
                sy_x = c.val("sx")

                se_r = emit_stats(x_sb[:], [("sx", sy_x)])

                def vu0(v, ti=ti, se_r=se_r):
                    v = c.chain(v, "ve")
                    v.wait_ge(c.sem("se"), se_r)
                    v.tensor_scalar(out=u0_tc.ap()[:, ti, :], in0=x_sb[:],
                                    scalar1=st[:, 2:3], scalar2=st[:, 7:8],
                                    op0=ALU.subtract, op1=ALU.mult)
                blk.vector(vu0)
                ve_u0 = c.val("ve")

                se_r2 = emit_stats(u0_tc.ap()[:, ti, :], [("ve", ve_u0)])

                def vt1(v, ti=ti, se_r2=se_r2, tpg=c.val("pe")):
                    v = c.chain(v, "ve")
                    v.wait_ge(c.sem("se"), se_r2)
                    if tpg:
                        v.wait_ge(c.sem("pe"), tpg)
                    v.tensor_scalar(out=f32b[:], in0=u0_tc.ap()[:, ti, :],
                                    scalar1=st[:, 2:3], scalar2=st[:, 7:8],
                                    op0=ALU.subtract, op1=ALU.mult)
                    v.tensor_tensor(out=f32b[:], in0=f32b[:], in1=lnbc[:, 0, :],
                                    op=ALU.mult)
                    v.tensor_tensor(out=f32b[:], in0=f32b[:], in1=lnbc[:, 1, :],
                                    op=ALU.add)
                blk.vector(vt1)
                ve_t1 = c.val("ve")

                def ttr(t, ve_t1=ve_t1, veg=c.val("ve")):
                    t.wait_ge(c.sem("ve"), ve_t1)
                    for g in range(8):
                        i = t.transpose(pb_big[:, g * 128:(g + 1) * 128],
                                        f32b[:, g * 128:(g + 1) * 128], id_f_sb[:])
                        c.inc(i, "pe", 1)
                blk.tensor(ttr)
                pe_tr = c.val("pe")

                def vcp(v, pe_tr=pe_tr, ti=ti):
                    v = c.chain(v, "ve")
                    v.wait_ge(c.sem("pe"), pe_tr)
                    v.tensor_copy(out=t1T.ap()[:, :, ti * 128:(ti + 1) * 128],
                                  in_=pb_big[:, 0:1024].rearrange(
                                      "p (g j) -> p g j", g=8))
                blk.vector(vcp)
            ve_t1T = c.val("ve")

            sgn_se = {0: 0, 1: 0}
            for ib in range(16):
                pp = ib % 2

                def lw1(g, ib=ib, pp=pp, wg=sgn_se[pp]):
                    if wg:
                        g.wait_ge(c.sem("se"), wg)
                    i = g.dma_start(out=wf32.ap()[:, pp, :, :],
                                    in_=w1T.rearrange("(cc p) i -> p cc i", p=128)
                                    [:, :, ib * 128:(ib + 1) * 128])
                    c.inc(i, f"sw{pp}", 16)
                blk.gpsimd(lw1)
                sw_v = c.val(f"sw{pp}")

                def ssg(sc, ib=ib, pp=pp, sw_v=sw_v, mmg=c.val("pe")):
                    sc = c.chain(sc, "se")
                    sc.wait_ge(c.sem(f"sw{pp}"), sw_v)
                    if mmg:
                        sc.wait_ge(c.sem("pe"), mmg)
                    sc.activation(out=ws_bf.ap()[:, pp, :, :],
                                  in_=wf32.ap()[:, pp, :, :], func=AF.Sign)
                blk.scalar(ssg)
                sgn_se[pp] = c.val("se")
                se_sg = c.val("se")

                def tg(t, ib=ib, pp=pp, se_sg=se_sg, ve_t1T=ve_t1T,
                       evg=c.val("ve")):
                    t.wait_ge(c.sem("se"), se_sg)
                    t.wait_ge(c.sem("ve"), ve_t1T)
                    if evg:
                        t.wait_ge(c.sem("ve"), evg)
                    for g in range(8):
                        i = t.matmul(pb_mm[:, pp, :], ws_bf.ap()[:, pp, g, :],
                                     t1T.ap()[:, g, :], start=(g == 0),
                                     stop=(g == 7))
                        c.inc(i, "pe", 1)
                blk.tensor(tg)
                pe_g = c.val("pe")

                def sgb(sc, ib=ib, pp=pp, pe_g=pe_g):
                    sc = c.chain(sc, "se")
                    sc.wait_ge(c.sem("pe"), pe_g)
                    sc.activation(out=f32d[:], in_=pb_mm[:, pp, :], func=AF.Identity,
                                  bias=b1pp[:, ib:ib + 1])
                blk.scalar(sgb)
                se_gb = c.val("se")

                def vgc(v, ib=ib, se_gb=se_gb):
                    v = c.chain(v, "ve")
                    v.wait_ge(c.sem("se"), se_gb)
                    v.tensor_scalar(out=gatedT.ap()[:, ib, :], in0=f32d[:],
                                    scalar1=1.0, scalar2=-1.0, op0=ALU.min,
                                    op1=ALU.max)
                blk.vector(vgc)
            ve_gated = c.val("ve")

            s2_se = {0: 0, 1: 0}
            for ic in range(16):
                pp = ic % 2

                def lw2(g, ic=ic, pp=pp, wg=s2_se[pp]):
                    if wg:
                        g.wait_ge(c.sem("se"), wg)
                    i = g.dma_start(out=w2f.ap()[:, pp, :],
                                    in_=w2T[ic * 128:(ic + 1) * 128, :])
                    c.inc(i, f"sv{pp}", 16)
                blk.gpsimd(lw2)
                sv_v = c.val(f"sv{pp}")

                def ss2(sc, ic=ic, pp=pp, sv_v=sv_v, mmg=c.val("pe")):
                    sc = c.chain(sc, "se")
                    sc.wait_ge(c.sem(f"sv{pp}"), sv_v)
                    if mmg:
                        sc.wait_ge(c.sem("pe"), mmg)
                    sc.activation(out=w2s.ap()[:, pp, :], in_=w2f.ap()[:, pp, :],
                                  func=AF.Sign)
                blk.scalar(ss2)
                s2_se[pp] = c.val("se")
                se_s2 = c.val("se")

                def tu2(t, ic=ic, pp=pp, se_s2=se_s2, ve_gated=ve_gated,
                        veg=c.val("ve"), seg=c.val("se")):
                    t.wait_ge(c.sem("se"), se_s2)
                    t.wait_ge(c.sem("ve"), ve_gated)
                    if ic == 0:
                        t.wait_ge(c.sem("ve"), veg)
                        t.wait_ge(c.sem("se"), seg)
                    for jb in range(8):
                        i = t.matmul(psy_bank(jb), w2s.ap()[:, pp, jb * 128:(jb + 1) * 128],
                                     gatedT.ap()[:, ic, :],
                                     start=(ic == 0), stop=(ic == 15))
                        c.inc(i, "pe", 1)
                blk.tensor(tu2)
            pe_u2 = c.val("pe")

            for jb in range(8):
                def su2(sc, jb=jb, pe_u2=pe_u2):
                    sc = c.chain(sc, "se")
                    sc.wait_ge(c.sem("pe"), pe_u2)
                    sc.activation(out=u2f.ap()[:, jb, :], in_=psy_bank(jb),
                                  func=AF.Identity, bias=b2pp[:, jb:jb + 1])
                blk.scalar(su2)
            se_u2 = c.val("se")

            for ti in range(4):
                row0 = tok0 + ti * 128

                def tvt(t, ti=ti, se_u2=se_u2, veg=c.val("ve")):
                    t.wait_ge(c.sem("se"), se_u2)
                    if veg:
                        t.wait_ge(c.sem("ve"), veg)
                    for g in range(8):
                        i = t.transpose(pb_big[:, g * 128:(g + 1) * 128],
                                        u2f.ap()[:, g, ti * 128:(ti + 1) * 128],
                                        id_f_sb[:])
                        c.inc(i, "pe", 1)
                blk.tensor(tvt)
                pe_vt = c.val("pe")

                def vvt(v, ti=ti, q_idx=q_idx, pe_vt=pe_vt, svv=c.val("svd")
                        if "svd" in c.sems else 0):
                    v = c.chain(v, "ve")
                    v.wait_ge(c.sem("pe"), pe_vt)
                    if svv:
                        v.wait_ge(c.sem("svd"), svv)
                    v.tensor_tensor(out=f32c[:], in0=pb_big[:, 0:1024],
                                    in1=u0_tc.ap()[:, ti, :], op=ALU.add)
                    v.reduce_max(st[:, 12:13], f32c[:], axis=AX.X,
                                 apply_absolute_value=True)
                    v.tensor_tensor(out=qpart[:, q_idx:q_idx + 1],
                                    in0=qpart[:, q_idx:q_idx + 1],
                                    in1=st[:, 12:13], op=ALU.max)
                blk.vector(vvt)
                ve_vt = c.val("ve")

                def stv(sy, row0=row0, ve_vt=ve_vt):
                    sy.wait_ge(c.sem("ve"), ve_vt)
                    i = sy.dma_start(out=v_dram[row0:row0 + 128, :], in_=f32c[:])
                    c.inc(i, "svd", 16)
                blk.sync(stv)

        c.barrier(blk)

        def tqp(t):
            i = t.transpose(pb_mm[0:4, 0, 0:128], qpart[:], id_f_sb[:])
            c.inc(i, "pe", 1)
        blk.tensor(tqp)
        pe_qp = c.val("pe")

        def vqp(v, pe_qp=pe_qp):
            v = c.chain(v, "ve")
            v.wait_ge(c.sem("pe"), pe_qp)
            v.reduce_max(qloc[:], pb_mm[0:4, 0, 0:128], axis=AX.X)
        blk.vector(vqp)
        ve_qp = c.val("ve")

        def syq(sy, ve_qp=ve_qp):
            sy.wait_ge(c.sem("ve"), ve_qp)
            i = sy.dma_start(out=gq_dram[:, :], in_=qloc[:])
            c.inc(i, "sq1", 16)
            sy.wait_ge(c.sem("sq1"), c.val("sq1"))
            i = sy.dma_start(out=gval[0:1, 0:4],
                             in_=gq_dram.rearrange("a e -> (a e)")[None, :])
            c.inc(i, "sq2", 16)
        blk.sync(syq)
        sq2_v = c.val("sq2")

        def vcc(v, sq2_v=sq2_v):
            v = c.chain(v, "ve")
            v.wait_ge(c.sem("sq2"), sq2_v)
            v.tensor_copy(out=cc_sb[0:1, 0:4], in_=gval[0:1, 0:4])
        blk.vector(vcc)
        ve_cc = c.val("ve")

        def gcc(g, ve_cc=ve_cc):
            g.wait_ge(c.sem("ve"), ve_cc)
            i = g.dma_start(out=cc_in_d[:, :], in_=cc_sb[:])
            c.inc(i, "sc1", 16)
            g.wait_ge(c.sem("sc1"), c.val("sc1"))
            i = g.collective_compute(
                "AllReduce", ALU.max, replica_groups=[list(range(8))],
                ins=[cc_in_h.ap().opt()], outs=[cc_out_h.ap().opt()])
            c.inc(i, "sc2", 1)
            g.wait_ge(c.sem("sc2"), c.val("sc2"))
            i = g.dma_start(out=gval[:, :], in_=cc_out_d[0:1, :])
            c.inc(i, "sc3", 16)
        blk.gpsimd(gcc)
        sc3_v = c.val("sc3")

        def vscl(v, sc3_v=sc3_v):
            v = c.chain(v, "ve")
            v.wait_ge(c.sem("sc3"), sc3_v)
            v.tensor_scalar(out=scl[0:1, 0:4], in0=gval[0:1, 0:4], scalar1=EPS,
                            scalar2=None, op0=ALU.add)
            v.reciprocal(scl[0:1, 4:8], scl[0:1, 0:4])
            v.tensor_scalar(out=scl[0:1, 0:4], in0=scl[0:1, 4:8], scalar1=128.0,
                            scalar2=None, op0=ALU.mult)
        blk.vector(vscl)
        ve_scl = c.val("ve")

        for q in range(4):
            def tsc(t, q=q, ve_scl=ve_scl):
                t.wait_ge(c.sem("ve"), ve_scl)
                i = t.matmul(pb_mm[:, 0, q:q + 1], ones_f_sb[:],
                             scl[0:1, q:q + 1], start=True, stop=True)
                c.inc(i, "pe", 1)
            blk.tensor(tsc)
        pe_scl = c.val("pe")

        def vsp(v, pe_scl=pe_scl):
            v = c.chain(v, "ve")
            v.wait_ge(c.sem("pe"), pe_scl)
            v.tensor_copy(out=sclpp[:], in_=pb_mm[:, 0, 0:4])
        blk.vector(vsp)

        c.barrier(blk)

        for m in range(NTILE):
            row0 = m * 128
            q_idx = (m % 16) // 4

            def ldv(sy, row0=row0, vg=c.val("ve")):
                if vg:
                    sy.wait_ge(c.sem("ve"), vg)
                i = sy.dma_start(out=x_sb[:], in_=v_dram[row0:row0 + 128, :])
                c.inc(i, "sx", 16)
            blk.sync(ldv)
            sy_v = c.val("sx")

            def vqx(v, q_idx=q_idx, sy_v=sy_v, qg=c.val("sqx") if "sqx" in c.sems
                    else 0):
                v = c.chain(v, "ve")
                v.wait_ge(c.sem("sx"), sy_v)
                if qg:
                    v.wait_ge(c.sem("sqx"), qg)
                v.tensor_scalar(out=f32c[:], in0=x_sb[:],
                                scalar1=sclpp[:, q_idx:q_idx + 1], scalar2=QCLIP,
                                op0=ALU.mult, op1=ALU.min)
                v.tensor_scalar(out=qx_t.ap()[:], in0=f32c[:], scalar1=-QCLIP,
                                scalar2=None, op0=ALU.max)
            blk.vector(vqx)
            ve_qx = c.val("ve")

            def stq(sy, row0=row0, ve_qx=ve_qx):
                sy.wait_ge(c.sem("ve"), ve_qx)
                i = sy.dma_start(out=qx_dram[row0:row0 + 128, :], in_=qx_t.ap()[:])
                c.inc(i, "sqx", 16)
            blk.sync(stq)

        c.barrier(blk)

        for b in range(B_PER_CORE):
            base_row = b * SEQ

            def vpad(v):
                v = c.chain(v, "ve")
                v.memset(convbuf.ap()[:, :, 0:3], 0.0)
            blk.vector(vpad)

            for tc in range(4):
                tok0 = base_row + tc * 512
                for ti in range(4):
                    row0 = tok0 + ti * 128

                    def ldq(sy, row0=row0, vg=c.val("ve")):
                        if vg:
                            sy.wait_ge(c.sem("ve"), vg)
                        i = sy.dma_start(out=qx_t.ap()[:],
                                         in_=qx_dram[row0:row0 + 128, :])
                        c.inc(i, "sx", 16)
                    blk.sync(ldq)
                    sy_q = c.val("sx")

                    se_r = emit_stats(qx_t.ap()[:], [("sx", sy_q)])

                    def vln2(v, se_r=se_r, tpg=c.val("pe")):
                        v = c.chain(v, "ve")
                        v.wait_ge(c.sem("se"), se_r)
                        if tpg:
                            v.wait_ge(c.sem("pe"), tpg)
                        v.tensor_scalar(out=f32b[:], in0=qx_t.ap()[:],
                                        scalar1=st[:, 2:3], scalar2=st[:, 7:8],
                                        op0=ALU.subtract, op1=ALU.mult)
                        v.tensor_tensor(out=f32b[:], in0=f32b[:],
                                        in1=lnbc[:, 2, :], op=ALU.mult)
                        v.tensor_tensor(out=f32b[:], in0=f32b[:],
                                        in1=lnbc[:, 3, :], op=ALU.add)
                    blk.vector(vln2)
                    ve_l2 = c.val("ve")

                    def ttru(t, ve_l2=ve_l2, ti=ti):
                        t.wait_ge(c.sem("ve"), ve_l2)
                        for g in range(8):
                            i = t.transpose(pb_big[:, g * 128:(g + 1) * 128],
                                            f32b[:, g * 128:(g + 1) * 128],
                                            id_f_sb[:])
                            c.inc(i, "pe", 1)
                    blk.tensor(ttru)
                    pe_tru = c.val("pe")

                    def vcpu(v, pe_tru=pe_tru, ti=ti):
                        v = c.chain(v, "ve")
                        v.wait_ge(c.sem("pe"), pe_tru)
                        v.tensor_copy(out=t1T.ap()[:, :, ti * 128:(ti + 1) * 128],
                                      in_=pb_big[:, 0:1024].rearrange(
                                          "p (g j) -> p g j", g=8))
                    blk.vector(vcpu)
                ve_uT = c.val("ve")

                for jbp in range(16):
                    pp = jbp % 2

                    def lwi(g, jbp=jbp, pp=pp, wg=c.val("pe")):
                        if wg:
                            g.wait_ge(c.sem("pe"), wg)
                        i = g.dma_start(
                            out=wk_bf.ap()[:, pp, :, :],
                            in_=inpT.rearrange("(cc p) j -> p cc j", p=128)
                            [:, :, jbp * 256:(jbp + 1) * 256])
                        c.inc(i, f"sw{pp}", 16)
                    blk.gpsimd(lwi)
                    swi_v = c.val(f"sw{pp}")

                    for jh in range(2):
                        jb = jbp * 2 + jh

                        def tip(t, jbp=jbp, jh=jh, pp=pp, swi_v=swi_v,
                                ve_uT=ve_uT, evg=c.val("se")):
                            t.wait_ge(c.sem(f"sw{pp}"), swi_v)
                            t.wait_ge(c.sem("ve"), ve_uT)
                            if evg:
                                t.wait_ge(c.sem("se"), evg)
                            for g in range(8):
                                i = t.matmul(pb_mm[:, jh, :],
                                             wk_bf.ap()[:, pp, g,
                                                        jh * 128:(jh + 1) * 128],
                                             t1T.ap()[:, g, :],
                                             start=(g == 0), stop=(g == 7))
                                c.inc(i, "pe", 1)
                        blk.tensor(tip)
                        pe_ip = c.val("pe")

                        if jb < 16:
                            def sxs(sc, jb=jb, jh=jh, tc=tc, pe_ip=pe_ip):
                                sc = c.chain(sc, "se")
                                sc.wait_ge(c.sem("pe"), pe_ip)
                                sc.activation(
                                    out=convbuf.ap()[:, jb,
                                                     3 + tc * 512:3 + (tc + 1) * 512],
                                    in_=pb_mm[:, jh, :], func=AF.Identity)
                            blk.scalar(sxs)
                        else:
                            def szz(sc, jh=jh, pe_ip=pe_ip, zg=c.val("szd")
                                    if "szd" in c.sems else 0):
                                sc = c.chain(sc, "se")
                                sc.wait_ge(c.sem("pe"), pe_ip)
                                if zg:
                                    sc.wait_ge(c.sem("szd"), zg)
                                sc.activation(out=zst.ap()[:], in_=pb_mm[:, jh, :],
                                              func=AF.Identity)
                            blk.scalar(szz)
                            se_z = c.val("se")

                            def stz(sy, jb=jb, tc=tc, b=b, se_z=se_z):
                                sy.wait_ge(c.sem("se"), se_z)
                                i = sy.dma_start(
                                    out=z_dram[b, jb - 16, :,
                                               tc * 512:(tc + 1) * 512],
                                    in_=zst.ap()[:])
                                c.inc(i, "szd", 16)
                            blk.sync(stz)

            c.barrier(blk)

            for cb in range(16):
                def vdg(v, cb=cb, dg=c.val("pe")):
                    v = c.chain(v, "ve")
                    if dg:
                        v.wait_ge(c.sem("pe"), dg)
                    for j in range(4):
                        v.tensor_scalar(out=diag[:, j, :], in0=id_bf_sb[:],
                                        scalar1=convw[:, cb * 4 + j:cb * 4 + j + 1],
                                        scalar2=None, op0=ALU.mult)
                blk.vector(vdg)
                ve_dg = c.val("ve")

                def tcv(t, cb=cb, ve_dg=ve_dg, eg=c.val("se")):
                    t.wait_ge(c.sem("ve"), ve_dg)
                    if eg:
                        t.wait_ge(c.sem("se"), eg)
                    for t4 in range(4):
                        for j in range(4):
                            i = t.matmul(pb_big[:, t4 * 512:(t4 + 1) * 512],
                                         diag[:, j, :],
                                         convbuf.ap()[:, cb,
                                                      t4 * 512 + j:t4 * 512 + j + 512],
                                         start=(j == 0), stop=(j == 3))
                            c.inc(i, "pe", 1)
                blk.tensor(tcv)
                pe_cv = c.val("pe")

                def scv(sc, cb=cb, pe_cv=pe_cv):
                    sc = c.chain(sc, "se")
                    sc.wait_ge(c.sem("pe"), pe_cv)
                    sc.activation(out=convbuf.ap()[:, cb, 3:3 + SEQ],
                                  in_=pb_big[:, :], func=AF.Silu,
                                  bias=convb[:, cb:cb + 1])
                blk.scalar(scv)

            c.barrier(blk)

            def lxp(g):
                i = g.dma_start(out=xprojT_sb[:],
                                in_=xprojT_d.rearrange("(cc p) j -> p cc j", p=128))
                c.inc(i, "sxp", 16)
            blk.gpsimd(lxp)
            sxp_v = c.val("sxp")

            for tc in range(4):
                def txp(t, tc=tc, eg=c.val("ve"), sxp_v=sxp_v):
                    t.wait_ge(c.sem("sxp"), sxp_v)
                    if tc == 0 and eg:
                        t.wait_ge(c.sem("ve"), eg)
                    for cb in range(16):
                        i = t.matmul(pb_big[0:96, tc * 512:(tc + 1) * 512],
                                     xprojT_sb[:, cb, :],
                                     convbuf.ap()[:, cb,
                                                  3 + tc * 512:3 + (tc + 1) * 512],
                                     start=(cb == 0), stop=(cb == 15))
                        c.inc(i, "pe", 1)
                blk.tensor(txp)
                pe_xp = c.val("pe")

                def vxp(v, tc=tc, pe_xp=pe_xp):
                    v = c.chain(v, "ve")
                    v.wait_ge(c.sem("pe"), pe_xp)
                    v.tensor_copy(out=dtlowT.ap()[:, tc * 512:(tc + 1) * 512],
                                  in_=pb_big[0:64, tc * 512:(tc + 1) * 512])
                    v.tensor_copy(out=bcbuf.ap()[64:96, tc * 512:(tc + 1) * 512],
                                  in_=pb_big[64:96, tc * 512:(tc + 1) * 512])
                blk.vector(vxp)
            ve_xp = c.val("ve")

            def stbc(sy, b=b, ve_xp=ve_xp):
                sy.wait_ge(c.sem("ve"), ve_xp)
                i = sy.dma_start(out=bc_dram[:, 0, :], in_=bcbuf.ap()[64:80, :])
                c.inc(i, "sbc", 16)
                i = sy.dma_start(out=bc_dram[:, 1, :], in_=bcbuf.ap()[80:96, :])
                c.inc(i, "sbc", 16)
                sy.wait_ge(c.sem("sbc"), c.val("sbc"))
            blk.sync(stbc)

            c.barrier(blk)

            for dblk in range(16):
                c.barrier(blk, engines=("tensor", "vector", "scalar", "gpsimd",
                                        "sync"))
                def tdt(t, dblk=dblk):
                    for tc in range(4):
                        i = t.matmul(pb_big[:, tc * 512:(tc + 1) * 512],
                                     dtprojT_sb[:, dblk * 128:(dblk + 1) * 128],
                                     dtlowT.ap()[:, tc * 512:(tc + 1) * 512],
                                     start=True, stop=True)
                        c.inc(i, "pe", 1)
                blk.tensor(tdt)
                pe_dt = c.val("pe")

                def sdt(sc, dblk=dblk, pe_dt=pe_dt):
                    sc = c.chain(sc, "se")
                    sc.wait_ge(c.sem("pe"), pe_dt)
                    sc.activation(out=dt_f32.ap()[:], in_=pb_big[:, :],
                                  func=AF.Identity, bias=dtpb[:, dblk:dblk + 1])
                    sc.activation(out=y_f32.ap()[:], in_=dt_f32.ap()[:], func=AF.Abs)
                    sc.activation(out=y_f32.ap()[:], in_=y_f32.ap()[:], func=AF.Exp,
                                  scale=-1.0)
                    sc.activation(out=y_f32.ap()[:], in_=y_f32.ap()[:], func=AF.Ln,
                                  bias=1.0)
                    sc.activation(out=dtb2.ap()[:], in_=dt_f32.ap()[:], func=AF.Relu)
                blk.scalar(sdt)
                se_dt = c.val("se")

                def vdt(v, dblk=dblk, se_dt=se_dt):
                    v = c.chain(v, "ve")
                    v.wait_ge(c.sem("se"), se_dt)
                    v.tensor_tensor(out=dt_f32.ap()[:], in0=y_f32.ap()[:],
                                    in1=dtb2.ap()[:], op=ALU.add)
                    v.tensor_tensor(out=dtx_bf.ap()[:], in0=dt_f32.ap()[:],
                                    in1=convbuf.ap()[:, dblk, 3:3 + SEQ],
                                    op=ALU.mult)
                    v.tensor_scalar(out=y_f32.ap()[:],
                                    in0=convbuf.ap()[:, dblk, 3:3 + SEQ],
                                    scalar1=Dcol[:, dblk:dblk + 1], scalar2=None,
                                    op0=ALU.mult)
                blk.vector(vdt)
                ve_dt = c.val("ve")

                for s in range(16):
                    def ldr(g, s=s, bg=c.val("pe")):
                        if bg:
                            g.wait_ge(c.sem("pe"), bg)
                        i = g.dma_start(out=brow.ap()[:, :, :],
                                        in_=bc_dram[s, :, :][None, :, :])
                        c.inc(i, "sbr", 16)
                    blk.gpsimd(ldr)
                    sbr_v = c.val("sbr")

                    def tbc(t, s=s, sbr_v=sbr_v, vg=c.val("ve")):
                        t.wait_ge(c.sem("sbr"), sbr_v)
                        if vg:
                            t.wait_ge(c.sem("ve"), vg)
                        for qc in range(4):
                            i = t.matmul(pb_big[:, qc * 512:(qc + 1) * 512],
                                         ones_bf_sb[:],
                                         brow.ap()[0:1, 0, qc * 512:(qc + 1) * 512],
                                         start=True, stop=True)
                            c.inc(i, "pe", 1)
                        for qc in range(4):
                            bank = pb_mm if qc < 2 else pb_at
                            i = t.matmul(bank[:, qc % 2, :], ones_bf_sb[:],
                                         brow.ap()[0:1, 1, qc * 512:(qc + 1) * 512],
                                         start=True, stop=True)
                            c.inc(i, "pe", 1)
                    blk.tensor(tbc)
                    pe_bc = c.val("pe")

                    def sa(sc, s=s, dblk=dblk, ve_dt=ve_dt, scg=c.val("ve")):
                        sc = c.chain(sc, "se")
                        sc.wait_ge(c.sem("ve"), ve_dt)
                        if scg:
                            sc.wait_ge(c.sem("ve"), scg)
                        sc.activation(out=a_bf.ap()[:], in_=dt_f32.ap()[:],
                                      func=AF.Exp,
                                      scale=A_sb[:, dblk * 16 + s:dblk * 16 + s + 1])
                    blk.scalar(sa)
                    se_a = c.val("se")

                    def vsc(v, s=s, pe_bc=pe_bc, se_a=se_a):
                        v = c.chain(v, "ve")
                        v.wait_ge(c.sem("pe"), pe_bc)
                        v.wait_ge(c.sem("se"), se_a)
                        v.tensor_tensor(out=h_bf.ap()[:], in0=dtx_bf.ap()[:],
                                        in1=pb_big[:, :], op=ALU.mult)
                        v.tensor_tensor_scan(out=h_bf.ap()[:], data0=a_bf.ap()[:],
                                             data1=h_bf.ap()[:], initial=0.0,
                                             op0=ALU.mult, op1=ALU.add)
                        for qc in range(4):
                            bank = pb_mm if qc < 2 else pb_at
                            v.tensor_tensor(out=a_bf.ap()[:, qc * 512:(qc + 1) * 512],
                                            in0=h_bf.ap()[:, qc * 512:(qc + 1) * 512],
                                            in1=bank[:, qc % 2, :], op=ALU.mult)
                        v.tensor_tensor(out=y_f32.ap()[:], in0=y_f32.ap()[:],
                                        in1=a_bf.ap()[:], op=ALU.add)
                    blk.vector(vsc)

                def ldz(g, b=b, dblk=dblk, zg=c.val("se")):
                    if zg:
                        g.wait_ge(c.sem("se"), zg)
                    i = g.dma_start(out=zsil.ap()[:], in_=z_dram[b, dblk, :, :])
                    c.inc(i, "szl", 16)
                blk.gpsimd(ldz)
                szl_v = c.val("szl")

                def ssz(sc, szl_v=szl_v, vg=c.val("ve")):
                    sc = c.chain(sc, "se")
                    sc.wait_ge(c.sem("szl"), szl_v)
                    sc.wait_ge(c.sem("ve"), vg)
                    sc.activation(out=zsil.ap()[:], in_=zsil.ap()[:], func=AF.Silu)
                blk.scalar(ssz)
                se_sz = c.val("se")

                def vyz(v, dblk=dblk, se_sz=se_sz):
                    v = c.chain(v, "ve")
                    v.wait_ge(c.sem("se"), se_sz)
                    v.tensor_tensor(out=convbuf.ap()[:, dblk, 3:3 + SEQ],
                                    in0=y_f32.ap()[:], in1=zsil.ap()[:],
                                    op=ALU.mult)
                blk.vector(vyz)

            c.barrier(blk)

            ev_se = [0]
            for tc in range(4):
                tok0 = base_row + tc * 512
                for icg in range(8):
                    pp = icg % 2

                    def lwo(g, icg=icg, pp=pp, wg=c.val("pe")):
                        if wg:
                            g.wait_ge(c.sem("pe"), wg)
                        i = g.dma_start(
                            out=outw.ap()[:, pp, :, :],
                            in_=outT_d.rearrange("(cc p) j -> p cc j", p=128)
                            [:, icg * 2:(icg + 1) * 2, :])
                        c.inc(i, f"sw{pp}", 16)
                    blk.gpsimd(lwo)
                    swo_v = c.val(f"sw{pp}")

                    def top(t, tc=tc, icg=icg, pp=pp, swo_v=swo_v,
                            eg=ev_se[0], veg=c.val("ve")):
                        t.wait_ge(c.sem(f"sw{pp}"), swo_v)
                        if icg == 0:
                            if eg:
                                t.wait_ge(c.sem("se"), eg)
                            if veg:
                                t.wait_ge(c.sem("ve"), veg)
                        for ici in range(2):
                            ic = icg * 2 + ici
                            for jb in range(8):
                                i = t.matmul(
                                    psy_bank(jb),
                                    outw.ap()[:, pp, ici, jb * 128:(jb + 1) * 128],
                                    convbuf.ap()[:, ic,
                                                 3 + tc * 512:3 + (tc + 1) * 512],
                                    start=(ic == 0), stop=(ic == 15))
                                c.inc(i, "pe", 1)
                    blk.tensor(top)
                pe_op = c.val("pe")

                for jb in range(8):
                    def sop(sc, jb=jb, pe_op=pe_op):
                        sc = c.chain(sc, "se")
                        sc.wait_ge(c.sem("pe"), pe_op)
                        sc.activation(out=outf32.ap()[:, jb, :], in_=psy_bank(jb),
                                      func=AF.Identity)
                    blk.scalar(sop)
                ev_se[0] = c.val("se")
                se_op = c.val("se")

                for ti in range(4):
                    row0 = tok0 + ti * 128

                    def tto(t, ti=ti, se_op=se_op, veg=c.val("ve")):
                        t.wait_ge(c.sem("se"), se_op)
                        if veg:
                            t.wait_ge(c.sem("ve"), veg)
                        for g in range(8):
                            i = t.transpose(pb_big[:, g * 128:(g + 1) * 128],
                                            outf32.ap()[:, g, ti * 128:(ti + 1) * 128],
                                            id_f_sb[:])
                            c.inc(i, "pe", 1)
                    blk.tensor(tto)
                    pe_to = c.val("pe")

                    def ldq2(sy, row0=row0, vg=c.val("ve"),
                             og=c.val("sxo")):
                        if vg:
                            sy.wait_ge(c.sem("ve"), vg)
                        if og:
                            sy.wait_ge(c.sem("sxo"), og)
                        i = sy.dma_start(out=qx_t.ap()[:],
                                         in_=qx_dram[row0:row0 + 128, :])
                        c.inc(i, "sx", 16)
                        i = sy.dma_start(out=x_sb[:], in_=x_in[row0:row0 + 128, :])
                        c.inc(i, "sx", 16)
                    blk.sync(ldq2)
                    sy_f = c.val("sx")

                    def vfin(v, pe_to=pe_to, sy_f=sy_f,
                             og=c.val("sxo") if "sxo" in c.sems else 0):
                        v = c.chain(v, "ve")
                        v.wait_ge(c.sem("pe"), pe_to)
                        v.wait_ge(c.sem("sx"), sy_f)
                        if og:
                            v.wait_ge(c.sem("sxo"), og)
                        v.tensor_tensor(out=f32c[:], in0=pb_big[:, 0:1024],
                                        in1=qx_t.ap()[:], op=ALU.add)
                        v.tensor_tensor(out=x_sb[:], in0=f32c[:], in1=x_sb[:],
                                        op=ALU.add)
                    blk.vector(vfin)
                    ve_f = c.val("ve")

                    def sto(sy, row0=row0, ve_f=ve_f):
                        sy.wait_ge(c.sem("ve"), ve_f)
                        i = sy.dma_start(out=x_out[row0:row0 + 128, :], in_=x_sb[:])
                        c.inc(i, "sxo", 16)
                    blk.sync(sto)

        sxo_t = c.val("sxo")

        def fin(sy, sxo_t=sxo_t):
            sy.wait_ge(c.sem("sxo"), sxo_t)
        blk.sync(fin)


def _declare_l1attn_drams(nc, pfx=""):
    aps = dict(
        w1T=nc.dram_tensor(pfx + "w1T", [D, INTER], F32,
                           kind="ExternalInput").ap(),
        w2T=nc.dram_tensor(pfx + "w2T", [INTER, D], F32,
                           kind="ExternalInput").ap(),
        inpT=nc.dram_tensor(pfx + "inpT", [D, 2 * INTER], F32,
                            kind="ExternalInput").ap(),
        xprojT_d=nc.dram_tensor(pfx + "xprojT_d", [INTER, 96], F32,
                                kind="ExternalInput").ap(),
        dtprojT_d=nc.dram_tensor(pfx + "dtprojT_d", [64, INTER], F32,
                                 kind="ExternalInput").ap(),
        outT_d=nc.dram_tensor(pfx + "outT_d", [INTER, D], F32,
                              kind="ExternalInput").ap(),
        A_d=nc.dram_tensor(pfx + "A_d", [128, 256], F32,
                           kind="ExternalInput").ap(),
        Dcol_d=nc.dram_tensor(pfx + "Dcol_d", [128, 16], F32,
                              kind="ExternalInput").ap(),
        convw_d=nc.dram_tensor(pfx + "convw_d", [128, 64], F32,
                               kind="ExternalInput").ap(),
        convb_d=nc.dram_tensor(pfx + "convb_d", [128, 16], F32,
                               kind="ExternalInput").ap(),
        dtpb_d=nc.dram_tensor(pfx + "dtpb_d", [128, 16], F32,
                              kind="ExternalInput").ap(),
        b1pp_d=nc.dram_tensor(pfx + "b1pp_d", [128, 16], F32,
                              kind="ExternalInput").ap(),
        b2pp_d=nc.dram_tensor(pfx + "b2pp_d", [128, 8], F32,
                              kind="ExternalInput").ap(),
        lnbc_d=nc.dram_tensor(pfx + "lnbc_d", [128, 4 * D], F32,
                              kind="ExternalInput").ap(),
    )
    aps["v_dram"] = nc.dram_tensor(pfx + "v_dram", [TOK, D], F32,
                                   kind="Internal").ap()
    aps["qx_dram"] = nc.dram_tensor(pfx + "qx_dram", [TOK, D], BF16,
                                    kind="Internal").ap()
    aps["z_dram"] = nc.dram_tensor(pfx + "z_dram", [2, 16, 128, SEQ], BF16,
                                   kind="Internal").ap()
    aps["bc_dram"] = nc.dram_tensor(pfx + "bc_dram", [16, 2, SEQ], F32,
                                    kind="Internal").ap()
    aps["gq_dram"] = nc.dram_tensor(pfx + "gq_dram", [4, 1], F32,
                                    kind="Internal").ap()
    aps["cc_in_h"] = nc.dram_tensor(pfx + "cc_in_h", [128, 128], F32,
                                    kind="Internal")
    aps["cc_out_h"] = nc.dram_tensor(pfx + "cc_out_h", [128, 128], F32,
                                     kind="Internal")
    return aps


def build_l1attn():
    nc = bass.Bass("TRN2", target_bir_lowering=False, debug=False)
    aps = _declare_l1attn_drams(nc)
    aps.update(_declare_shared_consts(nc))
    aps["x_in"] = nc.dram_tensor("x_in", [TOK, D], F32, kind="ExternalInput").ap()
    aps["x_out"] = nc.dram_tensor("x_out", [TOK, D], F32,
                                  kind="ExternalOutput").ap()
    with contextlib.ExitStack() as es:
        c = Ctx(nc, es)
        PB = es.enter_context(nc.psum_tensor("PB", [128, 2048], F32))
        PM = es.enter_context(nc.psum_tensor("PM", [128, 2, 512], F32))
        PA = es.enter_context(nc.psum_tensor("PA", [128, 2, 512], F32))
        sbase, _ = nc.bump_sbuf(nc.sbuf_bytes_remaining - 64)
        blk = es.enter_context(nc.Block())
        emit_l1attn(nc, c, es, blk, aps, "m_", sbase, PB, PM, PA)
    return nc


def _l1attn_maps(inputs):
    W1 = np.asarray(inputs["L1_attn_W1"], np.float32)
    W2 = np.asarray(inputs["L1_attn_W2"], np.float32)
    inp = np.asarray(inputs["L1_attn_in_proj_w"], np.float32)
    xproj = np.asarray(inputs["L1_attn_x_proj_w"], np.float32)
    dtproj = np.asarray(inputs["L1_attn_dt_proj_w"], np.float32)
    outp = np.asarray(inputs["L1_attn_out_proj_w"], np.float32)
    A_log = np.asarray(inputs["L1_attn_A_log"], np.float32)
    Dp = np.asarray(inputs["L1_attn_D_param"], np.float32)
    convw = np.asarray(inputs["L1_attn_conv_w"], np.float32)
    convb = np.asarray(inputs["L1_attn_conv_b"], np.float32)
    dtpb = np.asarray(inputs["L1_attn_dt_proj_b"], np.float32)
    b1 = np.asarray(inputs["L1_attn_b1"], np.float32)
    b2 = np.asarray(inputs["L1_attn_b2"], np.float32)
    lnq = np.stack([np.asarray(inputs["L1_attn_ln1_g"], np.float32),
                    np.asarray(inputs["L1_attn_ln1_b"], np.float32),
                    np.asarray(inputs["L1_attn_ln2_g"], np.float32),
                    np.asarray(inputs["L1_attn_ln2_b"], np.float32)])

    def pp(vec, nb):
        return np.ascontiguousarray(vec.reshape(nb, 128).T)

    return dict(
        w1T=np.ascontiguousarray(W1.T),
        w2T=np.ascontiguousarray(W2.T),
        inpT=np.ascontiguousarray(inp.T),
        xprojT_d=np.ascontiguousarray(xproj.T),
        dtprojT_d=np.ascontiguousarray(dtproj.T),
        outT_d=np.ascontiguousarray(outp.T),
        A_d=np.ascontiguousarray(
            A_log.reshape(16, 128, 16).transpose(1, 0, 2).reshape(128, 256)),
        Dcol_d=pp(Dp, 16),
        convw_d=np.ascontiguousarray(
            convw.reshape(16, 128, 4).transpose(1, 0, 2).reshape(128, 64)),
        convb_d=pp(convb, 16),
        dtpb_d=pp(dtpb, 16),
        b1pp_d=pp(b1, 16),
        b2pp_d=pp(b2, 8),
        lnbc_d=np.ascontiguousarray(
            np.broadcast_to(lnq.reshape(1, 4 * D), (128, 4 * D))),
    )


def run_l1attn(x_full, inputs, nc=None):
    if nc is None:
        nc = build_l1attn()
    base = _l1attn_maps(inputs)
    base.update(_shared_const_values())
    xr = np.asarray(x_full, np.float32).reshape(N_CORES, TOK, D)
    in_maps = [dict(base, x_in=np.ascontiguousarray(xr[i])) for i in range(N_CORES)]
    res = run_bass_kernel_spmd(nc, in_maps, core_ids=list(range(N_CORES)))
    out = np.stack([res.results[i]["x_out"] for i in range(N_CORES)])
    return out.reshape(16, 2048, 1024)



def _l0ffn_maps(inputs):
    w1 = inputs["L0_ffn_w1"]
    w2 = inputs["L0_ffn_w2"]
    w3 = inputs["L0_ffn_w3"]
    sw1 = inputs["L0_ffn_sw1"]
    sw2 = inputs["L0_ffn_sw2"]
    sw3 = inputs["L0_ffn_sw3"]
    wA = np.zeros((9, D, 2 * INTER), np.float32)
    w2a = np.zeros((9, INTER, D), np.float32)
    for e in range(8):
        wA[e, :, :INTER] = np.asarray(w1[e]).T
        wA[e, :, INTER:] = np.asarray(w3[e]).T
        w2a[e] = np.asarray(w2[e]).T
    wA[8, :, :INTER] = np.asarray(sw1).T
    wA[8, :, INTER:] = np.asarray(sw3).T
    w2a[8] = np.asarray(sw2).T
    base = dict(
        wA=np.ascontiguousarray(wA),
        w2=np.ascontiguousarray(w2a),
        gw=np.ascontiguousarray(np.asarray(inputs["L0_ffn_gate_w"]).T),
        gbias=np.asarray(inputs["L0_ffn_gate_bias"]).reshape(1, 8).astype(np.float32),
    )
    return base


def run_l0ffn(x_full, inputs, nc=None):
    if nc is None:
        nc = build_l0ffn()
    base = _l0ffn_maps(inputs)
    base.update(_shared_const_values())
    xr = np.asarray(x_full, np.float32).reshape(N_CORES, TOK, D)
    in_maps = [dict(base, x_in=np.ascontiguousarray(xr[i])) for i in range(N_CORES)]
    res = run_bass_kernel_spmd(nc, in_maps, core_ids=list(range(N_CORES)))
    out = np.stack([res.results[i]["x_out"] for i in range(N_CORES)])
    return out.reshape(16, 2048, 1024)



def build_fused():
    nc = bass.Bass("TRN2", target_bir_lowering=False, debug=False)
    shared = _declare_shared_consts(nc)
    a0f = _declare_l0ffn_drams(nc, "l0f_")
    a0a = _declare_l0attn_drams(nc, "l0a_")
    a1f = _declare_l1ffn_drams(nc, "l1f_")
    a1a = _declare_l1attn_drams(nc, "l1a_")
    for d in (a0f, a0a, a1f, a1a):
        d.update(shared)
    combT = nc.dram_tensor("combT_dram", [8, TOK], F32, kind="Internal").ap()
    m_dram = nc.dram_tensor("m_dram", [16, 128], F32, kind="Internal").ap()
    a0f["combT_dram"] = combT
    a1f["combT_dram"] = combT
    a1f["m_dram"] = m_dram

    x0 = nc.dram_tensor("x_in", [TOK, D], F32, kind="ExternalInput").ap()
    xb = [nc.dram_tensor(f"xb{i}", [TOK, D], F32, kind="Internal").ap()
          for i in range(5)]
    xout = nc.dram_tensor("x_out", [TOK, D], F32, kind="ExternalOutput").ap()
    chain = [x0] + xb + [xout]

    seq = [(emit_l0ffn, a0f, "s1_"), (emit_l0attn, a0a, "s2_"),
           (emit_l0ffn, a0f, "s3_"), (emit_l1ffn, a1f, "s4_"),
           (emit_l1attn, a1a, "s5_"), (emit_l1ffn, a1f, "s6_")]

    with contextlib.ExitStack() as es:
        c = Ctx(nc, es)
        PB = es.enter_context(nc.psum_tensor("PB", [128, 2048], F32))
        PM = es.enter_context(nc.psum_tensor("PM", [128, 2, 512], F32))
        PA = es.enter_context(nc.psum_tensor("PA", [128, 2, 512], F32))
        sbase, _ = nc.bump_sbuf(nc.sbuf_bytes_remaining - 64)
        blk = es.enter_context(nc.Block())
        for i, (emit, base_aps, pfx) in enumerate(seq):
            aps = dict(base_aps)
            aps["x_in"] = chain[i]
            aps["x_out"] = chain[i + 1]
            emit(nc, c, es, blk, aps, pfx, sbase, PB, PM, PA)
            c.barrier(blk)
    return nc


def _fused_maps(inputs):
    base = {}
    base.update({"l0f_" + k: v for k, v in _l0ffn_maps(inputs).items()})
    base.update({"l0a_" + k: v for k, v in _l0attn_maps(inputs).items()})
    base.update({"l1f_" + k: v for k, v in _l1ffn_maps(inputs).items()})
    base.update({"l1a_" + k: v for k, v in _l1attn_maps(inputs).items()})
    base.update(_shared_const_values())
    return base


def run_fused(x_full, inputs, nc=None):
    if nc is None:
        nc = build_fused()
    base = _fused_maps(inputs)
    xr = np.asarray(x_full, np.float32).reshape(N_CORES, TOK, D)
    in_maps = [dict(base, x_in=np.ascontiguousarray(xr[i]))
               for i in range(N_CORES)]
    res = run_bass_kernel_spmd(nc, in_maps, core_ids=list(range(N_CORES)))
    out = np.stack([res.results[i]["x_out"] for i in range(N_CORES)])
    return out.reshape(16, 2048, 1024)


_prog_cache = {}


def _prog(name):
    if name not in _prog_cache:
        _prog_cache[name] = {"l0ffn": build_l0ffn, "l0attn": build_l0attn,
                             "l1ffn": build_l1ffn, "l1attn": build_l1attn}[name]()
    return _prog_cache[name]


def kernel(**inputs):
    if "fused" not in _prog_cache:
        _prog_cache["fused"] = build_fused()
    x = np.asarray(inputs["L0_attn_X"], np.float32)
    out = run_fused(x, inputs, nc=_prog_cache["fused"])
    return out.astype(np.float32)


# version 06
# speedup vs baseline: 1.0210x; 1.0210x over previous
import contextlib
import numpy as np
import ml_dtypes
import concourse.bass as bass
import concourse.mybir as mybir
from concourse.bass_utils import run_bass_kernel_spmd

F32 = mybir.dt.float32
BF16 = mybir.dt.bfloat16
U8 = mybir.dt.uint8
AF = mybir.ActivationFunctionType
ALU = mybir.AluOpType
AX = mybir.AxisListType

N_CORES = 8
B_PER_CORE = 2
SEQ = 2048
TOK = B_PER_CORE * SEQ
D = 1024
INTER = 2048
NTILE = TOK // 128
EPS = 1e-5
NEG_BIG = -1e30


class Ctx:

    def __init__(self, nc, es):
        self.nc = nc
        self.es = es
        self.sems = {}
        self.waited = {}

    def sem(self, name):
        if name not in self.sems:
            h = self.es.enter_context(self.nc.semaphore(name))
            self.sems[name] = [h, 0]
        return self.sems[name][0]

    def bump(self, name, n=1):
        self.sems[name][1] += n
        return self.sems[name][1]

    def val(self, name):
        if name not in self.sems:
            return 0
        return self.sems[name][1]

    def inc(self, instr, name, n=1):
        instr.then_inc(self.sem(name), n)
        return self.bump(name, n)

    def chain(self, eng, name):
        return _Chained(eng, self, name)

    def wrap_block(self, blk):
        return _BlockWrap(blk, self)

    def barrier(self, blk, engines=("tensor", "vector", "scalar", "gpsimd",
                                    "sync")):
        snap = [(name, rec[1]) for name, rec in self.sems.items() if rec[1] > 0]

        def mk(engname):
            def f(e):
                for name, val in snap:
                    e.wait_ge(self.sem(name), val)
            return f
        for engname in engines:
            getattr(blk, engname)(mk(engname))


class _WaitFilter:

    def __init__(self, eng, ctx, skey):
        object.__setattr__(self, "_eng", eng)
        object.__setattr__(self, "_ctx", ctx)
        object.__setattr__(self, "_skey", skey)

    def wait_ge(self, sem, val):
        seen = self._ctx.waited.setdefault(self._skey, {})
        if seen.get(sem.num, 0) >= val:
            return None
        seen[sem.num] = val
        return self._eng.wait_ge(sem, val)

    def __getattr__(self, a):
        return getattr(self._eng, a)


class _BlockWrap:
    def __init__(self, blk, ctx):
        self._blk = blk
        self._ctx = ctx

    def _mk(self, engname):
        blk, ctx = self._blk, self._ctx

        def reg(f):
            getattr(blk, engname)(lambda e: f(_WaitFilter(e, ctx, engname)))
        return reg

    def __getattr__(self, a):
        if a in ("tensor", "vector", "scalar", "gpsimd", "sync"):
            return self._mk(a)
        return getattr(self._blk, a)


class _Chained:

    _PASSTHROUGH = {"wait_ge", "wait_op", "sem_inc", "nop", "drain"}

    def __init__(self, eng, ctx, name):
        self._eng = eng
        self._ctx = ctx
        self._name = name

    def __getattr__(self, attr):
        fn = getattr(self._eng, attr)
        if attr in self._PASSTHROUGH or not callable(fn):
            return fn
        ctx, name, eng = self._ctx, self._name, self._eng

        def wrapped(*a, **k):
            eng.wait_ge(ctx.sem(name), ctx.val(name))
            i = fn(*a, **k)
            ctx.inc(i, name)
            return i

        return wrapped


def _host_bf16(x):
    return np.ascontiguousarray(x).astype(ml_dtypes.bfloat16)




_DT_BYTES = {F32: 4, BF16: 2, U8: 1}


def _cursor_alloc(nc, pfx, cur):
    def sb(n, s, d=F32):
        nb = _DT_BYTES[d]
        for x in s[1:]:
            nb *= x
        t = nc.alloc_sbuf_tensor_at(pfx + n, s, d, offset=cur[0])
        cur[0] += (nb + 31) // 32 * 32
        return t
    return sb


def _declare_shared_consts(nc):
    return dict(
        id_bf=nc.dram_tensor("id_bf", [128, 128], BF16, kind="ExternalInput").ap(),
        id_f=nc.dram_tensor("id_f", [128, 128], F32, kind="ExternalInput").ap(),
        ones_bf=nc.dram_tensor("ones_bf", [1, 128], BF16,
                               kind="ExternalInput").ap(),
        ones_f=nc.dram_tensor("ones_f", [1, 128], F32, kind="ExternalInput").ap(),
        neg1_bf=nc.dram_tensor("neg1_bf", [1, 128], BF16,
                               kind="ExternalInput").ap(),
        tril=nc.dram_tensor("tril", [128, 128], F32, kind="ExternalInput").ap(),
    )


_SHARED_CONST_VALUES = None


def _shared_const_values():
    global _SHARED_CONST_VALUES
    if _SHARED_CONST_VALUES is None:
        _SHARED_CONST_VALUES = dict(
            id_bf=np.eye(128, dtype=ml_dtypes.bfloat16),
            id_f=np.eye(128, dtype=np.float32),
            ones_bf=np.ones((1, 128), ml_dtypes.bfloat16),
            ones_f=np.ones((1, 128), np.float32),
            neg1_bf=np.full((1, 128), -1.0, ml_dtypes.bfloat16),
            tril=np.ascontiguousarray(np.tril(np.ones((128, 128), np.float32))),
        )
    return _SHARED_CONST_VALUES


def emit_l0ffn(nc, c, es, blk, aps, pfx, sbase, PB, PM, PA):
    x_in = aps["x_in"]
    wA = aps["wA"]
    w2 = aps["w2"]
    gw = aps["gw"]
    gbias = aps["gbias"]
    id_bf = aps["id_bf"]
    id_f = aps["id_f"]
    ones_bf = aps["ones_bf"]
    ones_f = aps["ones_f"]
    x_out = aps["x_out"]
    combT_dram = aps["combT_dram"]

    SBTOK = 1024
    NSB = TOK // SBTOK
    NMT = SBTOK // 128
    NTC = SBTOK // 512

    if True:
        cur = [sbase]
        sb = _cursor_alloc(nc, pfx, cur)

        x_sb = sb("x_sb", [128, 2, D])
        xo_sb = sb("xo_sb", [128, 2, D])
        sq_sc = sb("sq_sc", [128, D])
        st = sb("st", [128, 16])
        ln_xT = sb("ln_xT", [128, 8, SBTOK], BF16)
        ln_bf = sb("ln_bf", [128, D], BF16)
        gw_sb = sb("gw_sb", [128, 8, 8], BF16)
        gb_bc = sb("gb_bc", [128, 8])
        neg_t = sb("neg_t", [128, 8])
        gwork = sb("gwork", [128, 40])
        gmask = sb("gmask", [128, 8], U8)
        comb_f = sb("comb_f", [128, 8])
        combT8 = sb("combT8", [8, 128])
        comb_row = sb("comb_row", [1, 512])
        comb_sb = sb("comb_sb", [128, 512])
        wA_sb = sb("wA_sb", [128, 8, 2 * INTER], BF16)
        w2_sb = sb("w2_sb", [128, 16, D], BF16)
        hT_sb = sb("hT_sb", [128, 16, 512], BF16)
        hs_sc = sb("hs_sc", [128, 2, 512], BF16)
        yev = sb("yev", [128, 512])
        yacc = sb("yacc", [128, 8, SBTOK])
        id_bf_sb = sb("id_bf_sb", [128, 128], BF16)
        id_f_sb = sb("id_f_sb", [128, 128], F32)
        ones_bf_sb = sb("ones_bf_sb", [1, 128], BF16)
        ones_f_sb = sb("ones_f_sb", [1, 128], F32)
        gb_row = sb("gb_row", [1, 8])

        ps_a = [PB[:, i * 512:(i + 1) * 512] for i in range(4)]
        ps_y = [PM[:, i, :] for i in range(2)]
        ps_mix = PA[:, 0, :]
        ps_tr = PA[:, 1, :].bitcast(BF16).rearrange("p (g j) -> p g j", g=8)

        def pre_sync(sy):
            for dst, srcap in ((gb_row, gbias), (id_bf_sb, id_bf), (id_f_sb, id_f),
                               (ones_bf_sb, ones_bf), (ones_f_sb, ones_f)):
                i = sy.dma_start(out=dst[:], in_=srcap[:])
                c.inc(i, "syc", 16)
        blk.sync(pre_sync)

        def pre_gp(g):
            i1 = g.dma_start(out=gw_sb[:], in_=gw.rearrange("(c p) j -> p c j", p=128))
            c.inc(i1, "sgw", 16)
        blk.gpsimd(pre_gp)

        pre_sy = c.val("syc")
        sgw_v = c.val("sgw")

        def pre_t(t):
            t.wait_ge(c.sem("syc"), pre_sy)
            i = t.matmul(ps_mix[:, 0:8], ones_f_sb[:], gb_row[:], start=True, stop=True)
            c.inc(i, "pe", 1)
        blk.tensor(pre_t)
        pe_gb = c.val("pe")

        def pre_v(v):
            v = c.chain(v, "ve")
            v.wait_ge(c.sem("pe"), pe_gb)
            i = v.tensor_copy(out=gb_bc[:], in_=ps_mix[:, 0:8])
            i = v.memset(neg_t[:], NEG_BIG)
        blk.vector(pre_v)
        ve_pre = c.val("ve")

        last_lnx_reads_pe = [0]
        last_wA_reads_pe = [0]
        last_w2_reads_pe = [0]
        last_h_reads_pe = [0]
        x_tile_ve = {}
        xo_tile_sy = {}
        ln_tr_pe = []
        ln_st1_ve = [0]
        comb_tr_pe = []
        comb_st_sy = []
        comb_row_pe = [0]

        for s in range(NSB):
            tok0 = s * SBTOK
            ln_copy_ve = []
            for m in range(NMT):
                par = m % 2
                row0 = tok0 + m * 128
                def ld(sy, par=par, row0=row0, m=m):
                    if x_tile_ve.get(par) is not None:
                        sy.wait_ge(c.sem("ve"), x_tile_ve[par])
                    i = sy.dma_start(out=x_sb[:, par, :], in_=x_in[row0:row0 + 128, :])
                    c.inc(i, f"sx{par}", 16)
                blk.sync(ld)
                sy_here = c.val(f"sx{par}")

                def vstats(v, par=par, sy_here=sy_here):
                    v = c.chain(v, "ve")
                    v.wait_ge(c.sem(f"sx{par}"), sy_here)
                    i = v.reduce_sum(st[:, 0:1], x_sb[:, par, :], axis=AX.X)
                blk.vector(vstats)
                ve_sum = c.val("ve")

                def sstats(sc, par=par, sy_here=sy_here,
                           prev_math=ln_st1_ve[0]):
                    sc = c.chain(sc, "se")
                    sc.wait_ge(c.sem(f"sx{par}"), sy_here)
                    if prev_math:
                        sc.wait_ge(c.sem("ve"), prev_math)
                    i = sc.activation(out=sq_sc[:], in_=x_sb[:, par, :], func=AF.Square,
                                      accum_out=st[:, 1:2])
                blk.scalar(sstats)
                se_sq = c.val("se")

                def vmath(v, par=par, se_sq=se_sq):
                    v = c.chain(v, "ve")
                    v.wait_ge(c.sem("se"), se_sq)
                    i = v.tensor_scalar(out=st[:, 2:3], in0=st[:, 0:1], scalar1=1.0 / D,
                                        scalar2=None, op0=ALU.mult)
                    i = v.tensor_scalar(out=st[:, 3:4], in0=st[:, 1:2], scalar1=1.0 / D,
                                        scalar2=None, op0=ALU.mult)
                    i = v.tensor_tensor(out=st[:, 4:5], in0=st[:, 2:3], in1=st[:, 2:3],
                                        op=ALU.mult)
                    i = v.tensor_tensor(out=st[:, 5:6], in0=st[:, 3:4], in1=st[:, 4:5],
                                        op=ALU.subtract)
                    i = v.tensor_scalar(out=st[:, 5:6], in0=st[:, 5:6], scalar1=EPS,
                                        scalar2=None, op0=ALU.add)
                blk.vector(vmath)
                ve_var = c.val("ve")
                ln_st1_ve[0] = ve_var

                def srstd(sc, ve_var=ve_var):
                    sc = c.chain(sc, "se")
                    sc.wait_ge(c.sem("ve"), ve_var)
                    i = sc.activation(out=st[:, 6:7], in_=st[:, 5:6], func=AF.Ln)
                    i = sc.activation(out=st[:, 7:8], in_=st[:, 6:7], func=AF.Exp,
                                      scale=-0.5)
                blk.scalar(srstd)
                se_rstd = c.val("se")

                def vln(v, par=par, se_rstd=se_rstd):
                    v = c.chain(v, "ve")
                    v.wait_ge(c.sem("se"), se_rstd)
                    if ln_tr_pe:
                        v.wait_ge(c.sem("pe"), ln_tr_pe[-1])
                    i = v.tensor_scalar(out=ln_bf[:], in0=x_sb[:, par, :],
                                        scalar1=st[:, 2:3], scalar2=st[:, 7:8],
                                        op0=ALU.subtract, op1=ALU.mult)
                blk.vector(vln)
                ve_ln = c.val("ve")
                x_tile_ve[par] = ve_ln

                def ttr(t, ve_ln=ve_ln, m=m):
                    t.wait_ge(c.sem("ve"), ve_ln)
                    if ln_copy_ve:
                        t.wait_ge(c.sem("ve"), ln_copy_ve[-1])
                    for g in range(8):
                        i = t.transpose(ps_tr[:, g, :], ln_bf[:, g * 128:(g + 1) * 128],
                                        id_bf_sb[:])
                        c.inc(i, "pe", 1)
                blk.tensor(ttr)
                pe_tr = c.val("pe")
                ln_tr_pe.append(pe_tr)

                def vcp(v, pe_tr=pe_tr, m=m):
                    v = c.chain(v, "ve")
                    v.wait_ge(c.sem("pe"), pe_tr)
                    i = v.tensor_copy(out=ln_xT[:, :, m * 128:(m + 1) * 128],
                                      in_=ps_tr[:, :, :])
                blk.vector(vcp)
                ln_copy_ve.append(c.val("ve"))

            ve_ln_done = c.val("ve")

            comb_tr_ve = []
            for m in range(NMT):
                def tlog(t, m=m, ve_ln_done=ve_ln_done):
                    t.wait_ge(c.sem("sgw"), sgw_v)
                    t.wait_ge(c.sem("ve"), ve_ln_done)
                    if comb_tr_ve:
                        t.wait_ge(c.sem("ve"), comb_tr_ve[-1])
                    for g in range(8):
                        i = t.matmul(ps_mix[:, 0:8],
                                     ln_xT[:, g, m * 128:(m + 1) * 128],
                                     gw_sb[:, g, :], start=(g == 0), stop=(g == 7))
                        c.inc(i, "pe", 1)
                blk.tensor(tlog)
                pe_log = c.val("pe")

                def vsm1(v, pe_log=pe_log):
                    v = c.chain(v, "ve")
                    v.wait_ge(c.sem("pe"), pe_log)
                    i = v.reduce_max(st[:, 8:9], ps_mix[:, 0:8], axis=AX.X, negate=True)
                blk.vector(vsm1)
                ve_nm = c.val("ve")

                def sexp(sc, ve_nm=ve_nm):
                    sc = c.chain(sc, "se")
                    sc.wait_ge(c.sem("ve"), ve_nm)
                    i = sc.activation(out=gwork[:, 0:8], in_=ps_mix[:, 0:8], func=AF.Exp,
                                      bias=st[:, 8:9], accum_out=st[:, 9:10])
                blk.scalar(sexp)
                se_exp = c.val("se")

                def vsm2(v, se_exp=se_exp, m=m):
                    v = c.chain(v, "ve")
                    v.wait_ge(c.sem("se"), se_exp)
                    if comb_tr_pe:
                        v.wait_ge(c.sem("pe"), comb_tr_pe[-1])
                    i = v.reciprocal(st[:, 10:11], st[:, 9:10])
                    i = v.tensor_scalar(out=gwork[:, 8:16], in0=gwork[:, 0:8],
                                        scalar1=st[:, 10:11], scalar2=None, op0=ALU.mult)
                    i = v.tensor_tensor(out=gwork[:, 16:24], in0=gwork[:, 8:16],
                                        in1=gb_bc[:], op=ALU.add)
                    i = v.reduce_max(st[:, 11:12], gwork[:, 16:24], axis=AX.X)
                    i = v.tensor_scalar(out=gwork[:, 24:32], in0=gwork[:, 16:24],
                                        scalar1=st[:, 11:12], scalar2=None,
                                        op0=ALU.is_equal)
                    i = v.scalar_tensor_tensor(out=gwork[:, 24:32],
                                               in0=gwork[:, 24:32], scalar=NEG_BIG,
                                               in1=gwork[:, 16:24],
                                               op0=ALU.mult, op1=ALU.add)
                    i = v.reduce_max(st[:, 12:13], gwork[:, 24:32], axis=AX.X)
                    i = v.tensor_scalar(out=gwork[:, 32:40], in0=gwork[:, 16:24],
                                        scalar1=st[:, 12:13], scalar2=None, op0=ALU.is_ge)
                    i = v.tensor_tensor(out=comb_f[:], in0=gwork[:, 8:16],
                                        in1=gwork[:, 32:40], op=ALU.mult)
                blk.vector(vsm2)
                ve_cb = c.val("ve")

                def tcb(t, ve_cb=ve_cb):
                    t.wait_ge(c.sem("ve"), ve_cb)
                    i = t.transpose(ps_mix[0:8, 0:128], comb_f[:], id_f_sb[:])
                    c.inc(i, "pe", 1)
                blk.tensor(tcb)
                pe_cb = c.val("pe")
                comb_tr_pe.append(pe_cb)

                def vcb(v, pe_cb=pe_cb, m=m):
                    v = c.chain(v, "ve")
                    v.wait_ge(c.sem("pe"), pe_cb)
                    if comb_st_sy:
                        v.wait_ge(c.sem("scb"), comb_st_sy[-1])
                    i = v.tensor_copy(out=combT8[:], in_=ps_mix[0:8, 0:128])
                blk.vector(vcb)
                comb_tr_ve.append(c.val("ve"))
                ve_cbc = c.val("ve")

                def stcb(sy, m=m, tok0=tok0, ve_cbc=ve_cbc):
                    sy.wait_ge(c.sem("ve"), ve_cbc)
                    col0 = tok0 + m * 128
                    i = sy.dma_start(out=combT_dram[:, col0:col0 + 128],
                                     in_=combT8[:])
                    c.inc(i, "scb", 16)
                blk.sync(stcb)
                comb_st_sy.append(c.val("scb"))

            ve_gate_done = c.val("ve")

            def vz(v, ve_gate_done=ve_gate_done):
                v = c.chain(v, "ve")
                v.wait_ge(c.sem("pe"), last_lnx_reads_pe[0])
                i = v.memset(yacc[:], 0.0)
            blk.vector(vz)
            ve_yz = c.val("ve")

            h_write_ve = [0]
            evac_ve = {0: 0, 1: 0}
            silu_se = {}
            hmul_ve = {}
            for e in range(9):
                def lw(g, e=e):
                    g.wait_ge(c.sem("pe"), last_wA_reads_pe[0])
                    for h2 in range(2):
                        i = g.dma_start(
                            out=wA_sb[:, 4 * h2:4 * (h2 + 1), :],
                            in_=wA[e].rearrange("(c p) i -> p c i", p=128)[:, 4 * h2:4 * (h2 + 1), :])
                        c.inc(i, "swa", 16)
                blk.gpsimd(lw)
                gp_wA = c.val("swa")

                def lw2(g, e=e):
                    g.wait_ge(c.sem("pe"), last_w2_reads_pe[0])
                    i = g.dma_start(out=w2_sb[:],
                                    in_=w2[e].rearrange("(c p) j -> p c j", p=128))
                    c.inc(i, "sw2", 16)
                blk.gpsimd(lw2)
                gp_w2 = c.val("sw2")

                for tc in range(NTC):
                    if e < 9 - 1:
                        def ldcr(sy, e=e, tc=tc, tok0=tok0):
                            sy.wait_ge(c.sem("scb"), comb_st_sy[-1])
                            sy.wait_ge(c.sem("pe"), comb_row_pe[0])
                            col0 = tok0 + tc * 512
                            i = sy.dma_start(out=comb_row[:],
                                             in_=combT_dram[e:e + 1, col0:col0 + 512])
                            c.inc(i, "scr", 16)
                        blk.sync(ldcr)
                        sy_cr = c.val("scr")

                        def tbc(t, sy_cr=sy_cr, ve_gate_done=ve_gate_done):
                            t.wait_ge(c.sem("scr"), sy_cr)
                            t.wait_ge(c.sem("ve"), ve_gate_done)
                            i = t.matmul(ps_mix[:, 0:512], ones_f_sb[:],
                                         comb_row[:], start=True, stop=True)
                            c.inc(i, "pe", 1)
                        blk.tensor(tbc)
                        pe_bc = c.val("pe")
                        comb_row_pe[0] = pe_bc

                        def vbc(v, pe_bc=pe_bc):
                            v = c.chain(v, "ve")
                            v.wait_ge(c.sem("pe"), pe_bc)
                            i = v.tensor_copy(out=comb_sb[:], in_=ps_mix[:, 0:512])
                        blk.vector(vbc)
                    ve_comb = c.val("ve")

                    for ib in range(16):
                        pp = ib % 2
                        bank_g1, bank_g3 = ps_a[2 * pp], ps_a[2 * pp + 1]

                        def tA(t, e=e, tc=tc, ib=ib, pp=pp, gp_wA=gp_wA,
                               bank_g1=bank_g1, bank_g3=bank_g3):
                            t.wait_ge(c.sem("swa"), gp_wA)
                            if silu_se.get(pp):
                                t.wait_ge(c.sem("se"), silu_se[pp])
                            if hmul_ve.get(pp):
                                t.wait_ge(c.sem("ve"), hmul_ve[pp])
                            for g in range(8):
                                i = t.matmul(bank_g1[:],
                                             wA_sb[:, g, ib * 128:(ib + 1) * 128],
                                             ln_xT[:, g, tc * 512:(tc + 1) * 512],
                                             start=(g == 0), stop=(g == 7))
                                c.inc(i, "pe", 1)
                            for g in range(8):
                                i = t.matmul(bank_g3[:],
                                             wA_sb[:, g, INTER + ib * 128:INTER + (ib + 1) * 128],
                                             ln_xT[:, g, tc * 512:(tc + 1) * 512],
                                             start=(g == 0), stop=(g == 7))
                                c.inc(i, "pe", 1)
                        blk.tensor(tA)
                        pe_A = c.val("pe")
                        last_lnx_reads_pe[0] = pe_A
                        last_wA_reads_pe[0] = pe_A

                        def sA(sc, pe_A=pe_A, pp=pp, bank_g1=bank_g1,
                               hmv=hmul_ve.get(pp)):
                            sc = c.chain(sc, "se")
                            sc.wait_ge(c.sem("pe"), pe_A)
                            if hmv:
                                sc.wait_ge(c.sem("ve"), hmv)
                            i = sc.activation(out=hs_sc[:, pp, :], in_=bank_g1[:],
                                              func=AF.Silu)
                        blk.scalar(sA)
                        silu_se[pp] = c.val("se")

                        def vA(v, pe_A=pe_A, ib=ib, pp=pp, bank_g3=bank_g3,
                               se_h=silu_se[pp]):
                            v = c.chain(v, "ve")
                            v.wait_ge(c.sem("se"), se_h)
                            v.wait_ge(c.sem("pe"), pe_A)
                            v.wait_ge(c.sem("pe"), last_h_reads_pe[0])
                            i = v.tensor_tensor(out=hT_sb[:, ib, :],
                                                in0=hs_sc[:, pp, :],
                                                in1=bank_g3[:], op=ALU.mult)
                        blk.vector(vA)
                        hmul_ve[pp] = c.val("ve")
                    h_write_ve[0] = c.val("ve")

                    for g in range(8):
                        pp = g % 2
                        bank_y = ps_y[pp]

                        def tB(t, e=e, tc=tc, g=g, pp=pp, gp_w2=gp_w2,
                               hv=h_write_ve[0], bank_y=bank_y):
                            t.wait_ge(c.sem("sw2"), gp_w2)
                            t.wait_ge(c.sem("ve"), hv)
                            if evac_ve[pp]:
                                t.wait_ge(c.sem("ve"), evac_ve[pp])
                            for ic in range(16):
                                i = t.matmul(bank_y[:],
                                             w2_sb[:, ic, g * 128:(g + 1) * 128],
                                             hT_sb[:, ic, :],
                                             start=(ic == 0), stop=(ic == 15))
                                c.inc(i, "pe", 1)
                        blk.tensor(tB)
                        pe_B = c.val("pe")
                        last_h_reads_pe[0] = pe_B
                        last_w2_reads_pe[0] = pe_B

                        def vB(v, e=e, tc=tc, g=g, pe_B=pe_B, ve_comb=ve_comb,
                               bank_y=bank_y):
                            v = c.chain(v, "ve")
                            v.wait_ge(c.sem("pe"), pe_B)
                            if e < 8:
                                i = v.tensor_tensor(out=yev[:], in0=bank_y[:],
                                                    in1=comb_sb[:], op=ALU.mult)
                                i = v.tensor_tensor(
                                    out=yacc[:, g, tc * 512:(tc + 1) * 512],
                                    in0=yacc[:, g, tc * 512:(tc + 1) * 512],
                                    in1=yev[:], op=ALU.add)
                            else:
                                i = v.tensor_tensor(
                                    out=yacc[:, g, tc * 512:(tc + 1) * 512],
                                    in0=yacc[:, g, tc * 512:(tc + 1) * 512],
                                    in1=bank_y[:], op=ALU.add)
                        blk.vector(vB)
                        evac_ve[pp] = c.val("ve")

            ve_experts_done = c.val("ve")

            wb_ve_list = []
            for m in range(NMT):
                par = m % 2
                row0 = tok0 + m * 128

                def ldx(sy, par=par, row0=row0, m=m):
                    if x_tile_ve.get(par) is not None:
                        sy.wait_ge(c.sem("ve"), x_tile_ve[par])
                    i = sy.dma_start(out=x_sb[:, par, :], in_=x_in[row0:row0 + 128, :])
                    c.inc(i, f"sx{par}", 16)
                blk.sync(ldx)
                sy_x = c.val(f"sx{par}")

                def twb(t, m=m, ve_experts_done=ve_experts_done):
                    t.wait_ge(c.sem("ve"), ve_experts_done)
                    if wb_ve_list:
                        t.wait_ge(c.sem("ve"), wb_ve_list[-1])
                    for g in range(8):
                        bank = ps_a[0] if g < 4 else ps_a[1]
                        i = t.transpose(bank[:, (g % 4) * 128:(g % 4 + 1) * 128],
                                        yacc[:, g, m * 128:(m + 1) * 128], id_f_sb[:])
                        c.inc(i, "pe", 1)
                blk.tensor(twb)
                pe_wb = c.val("pe")

                def vwb(v, par=par, m=m, pe_wb=pe_wb, sy_x=sy_x):
                    v = c.chain(v, "ve")
                    v.wait_ge(c.sem("pe"), pe_wb)
                    v.wait_ge(c.sem(f"sx{par}"), sy_x)
                    if xo_tile_sy.get(par) is not None:
                        v.wait_ge(c.sem(f"so{par}"), xo_tile_sy[par])
                    i = v.scalar_tensor_tensor(out=xo_sb[:, par, 0:512],
                                               in0=ps_a[0][:], scalar=0.5,
                                               in1=x_sb[:, par, 0:512],
                                               op0=ALU.mult, op1=ALU.add)
                    i = v.scalar_tensor_tensor(out=xo_sb[:, par, 512:1024],
                                               in0=ps_a[1][:], scalar=0.5,
                                               in1=x_sb[:, par, 512:1024],
                                               op0=ALU.mult, op1=ALU.add)
                blk.vector(vwb)
                ve_wb = c.val("ve")
                wb_ve_list.append(ve_wb)
                x_tile_ve[par] = ve_wb

                def stx(sy, par=par, row0=row0, ve_wb=ve_wb):
                    sy.wait_ge(c.sem("ve"), ve_wb)
                    i = sy.dma_start(out=x_out[row0:row0 + 128, :], in_=xo_sb[:, par, :])
                    c.inc(i, f"so{par}", 16)
                blk.sync(stx)
                xo_tile_sy[par] = c.val(f"so{par}")

            wb_pe = c.val("pe")
            last_lnx_reads_pe[0] = max(last_lnx_reads_pe[0], wb_pe)
            hmul_ve[0] = c.val("ve")
            hmul_ve[1] = c.val("ve")
            silu_se[0] = c.val("se")
            silu_se[1] = c.val("se")

        so0_t, so1_t = c.val("so0"), c.val("so1")

        def fin(sy, so0_t=so0_t, so1_t=so1_t):
            sy.wait_ge(c.sem("so0"), so0_t)
            sy.wait_ge(c.sem("so1"), so1_t)
        blk.sync(fin)


def _declare_l0ffn_drams(nc, pfx=""):
    aps = dict(
        wA=nc.dram_tensor(pfx + "wA", [9, D, 2 * INTER], F32,
                          kind="ExternalInput").ap(),
        w2=nc.dram_tensor(pfx + "w2", [9, INTER, D], F32,
                          kind="ExternalInput").ap(),
        gw=nc.dram_tensor(pfx + "gw", [D, 8], F32, kind="ExternalInput").ap(),
        gbias=nc.dram_tensor(pfx + "gbias", [1, 8], F32,
                             kind="ExternalInput").ap(),
    )
    return aps


def build_l0ffn():
    nc = bass.Bass("TRN2", target_bir_lowering=False, debug=False)
    aps = _declare_l0ffn_drams(nc)
    aps.update(_declare_shared_consts(nc))
    aps["x_in"] = nc.dram_tensor("x_in", [TOK, D], F32, kind="ExternalInput").ap()
    aps["x_out"] = nc.dram_tensor("x_out", [TOK, D], F32,
                                  kind="ExternalOutput").ap()
    aps["combT_dram"] = nc.dram_tensor("combT_dram", [8, TOK], F32,
                                       kind="Internal").ap()
    with contextlib.ExitStack() as es:
        c = Ctx(nc, es)
        PB = es.enter_context(nc.psum_tensor("PB", [128, 2048], F32))
        PM = es.enter_context(nc.psum_tensor("PM", [128, 2, 512], F32))
        PA = es.enter_context(nc.psum_tensor("PA", [128, 2, 512], F32))
        sbase, _ = nc.bump_sbuf(nc.sbuf_bytes_remaining - 64)
        blk = c.wrap_block(es.enter_context(nc.Block()))
        emit_l0ffn(nc, c, es, blk, aps, "f_", sbase, PB, PM, PA)
    return nc



def emit_l0attn(nc, c, es, blk, aps, pfx, sbase, PB, PM, PA):
    x_in = aps["x_in"]
    wqkv = aps["wqkv"]
    bqk = aps["bqk"]
    bv = aps["bv"]
    id_bf = aps["id_bf"]
    tril = aps["tril"]
    ones_f = aps["ones_f"]
    x_out = aps["x_out"]

    NQT = SEQ // 128
    SCALE = 1.0 / 32.0

    if True:
        cur = [sbase]
        sb = _cursor_alloc(nc, pfx, cur)

        x_sb = sb("x_sb", [128, 2, D])
        xo_sb = sb("xo_sb", [128, 2, D])
        sq_sc = sb("sq_sc", [128, D])
        st = sb("st", [128, 16])
        ln_bf = sb("ln_bf", [128, D], BF16)
        lnxT = sb("lnxT", [128, 8, SEQ], BF16)
        qT = sb("qT", [128, 8, SEQ], BF16)
        kT = sb("kT", [128, 8, SEQ], BF16)
        s_row = sb("s_row", [128, SEQ])
        p_buf = sb("p_buf", [128, SEQ], BF16)
        w_sb = sb("w_sb", [128, 8, 3 * D], BF16)
        bqk_sb = sb("bqk_sb", [128, 16])
        bv_row = sb("bv_row", [1, D])
        bv_bc = sb("bv_bc", [128, D])
        v_sc = sb("v_sc", [128, D])
        id_bf_sb = sb("id_bf_sb", [128, 128], BF16)
        tril_sb = sb("tril_sb", [128, 128], F32)
        ones_f_sb = sb("ones_f_sb", [1, 128], F32)

        ps_s = PB[:, :]
        ps_q = [PM[:, i, :] for i in range(2)]
        ps_tr = PA[:, 1, :].bitcast(BF16).rearrange("p (g j) -> p g j", g=8)

        def pre_sync(sy):
            for dst, srcap in ((bqk_sb, bqk), (bv_row, bv), (id_bf_sb, id_bf),
                               (tril_sb, tril), (ones_f_sb, ones_f)):
                i = sy.dma_start(out=dst[:], in_=srcap[:])
                c.inc(i, "syc", 16)
        blk.sync(pre_sync)
        syc_v = c.val("syc")

        def pre_gp(g):
            i = g.dma_start(out=w_sb[:],
                            in_=wqkv.rearrange("(c p) j -> p c j", p=128))
            c.inc(i, "swq", 16)
        blk.gpsimd(pre_gp)
        swq_v = c.val("swq")

        def pre_t(t):
            t.wait_ge(c.sem("syc"), syc_v)
            for h in range(2):
                i = t.matmul(ps_q[h][:], ones_f_sb[:], bv_row[0:1, h * 512:(h + 1) * 512],
                             start=True, stop=True)
                c.inc(i, "pe", 1)
        blk.tensor(pre_t)
        pe_bv = c.val("pe")

        def pre_v(v):
            v = c.chain(v, "ve")
            v.wait_ge(c.sem("pe"), pe_bv)
            for h in range(2):
                v.tensor_copy(out=bv_bc[:, h * 512:(h + 1) * 512], in_=ps_q[h][:])
        blk.vector(pre_v)
        ve_bvbc = c.val("ve")

        x_tile_ve = {}
        xo_tile_sy = {}
        ln_tr_pe = []
        ln_st1_ve = [0]
        ln_copy_pe_guard = [0]
        qk_evac_se = {0: 0, 1: 0}
        v_evac_ve = {0: 0, 1: 0}
        sbank_evac_ve = {j: 0 for j in range(4)}
        exp_se = [0]
        pmul_ve = [0]

        for b in range(B_PER_CORE):
            base_row = b * SEQ
            ln_copy_ve = []
            for m in range(NQT):
                par = m % 2
                row0 = base_row + m * 128

                def ld(sy, par=par, row0=row0):
                    if x_tile_ve.get(par) is not None:
                        sy.wait_ge(c.sem("ve"), x_tile_ve[par])
                    i = sy.dma_start(out=x_sb[:, par, :], in_=x_in[row0:row0 + 128, :])
                    c.inc(i, f"sx{par}", 16)
                blk.sync(ld)
                sy_here = c.val(f"sx{par}")

                def vstats(v, par=par, sy_here=sy_here):
                    v = c.chain(v, "ve")
                    v.wait_ge(c.sem(f"sx{par}"), sy_here)
                    v.reduce_sum(st[:, 0:1], x_sb[:, par, :], axis=AX.X)
                blk.vector(vstats)

                def sstats(sc, par=par, sy_here=sy_here, prev_math=ln_st1_ve[0]):
                    sc = c.chain(sc, "se")
                    sc.wait_ge(c.sem(f"sx{par}"), sy_here)
                    if prev_math:
                        sc.wait_ge(c.sem("ve"), prev_math)
                    sc.activation(out=sq_sc[:], in_=x_sb[:, par, :], func=AF.Square,
                                  accum_out=st[:, 1:2])
                blk.scalar(sstats)
                se_sq = c.val("se")

                def vmath(v, se_sq=se_sq):
                    v = c.chain(v, "ve")
                    v.wait_ge(c.sem("se"), se_sq)
                    v.tensor_scalar(out=st[:, 2:3], in0=st[:, 0:1], scalar1=1.0 / D,
                                    scalar2=None, op0=ALU.mult)
                    v.tensor_scalar(out=st[:, 3:4], in0=st[:, 1:2], scalar1=1.0 / D,
                                    scalar2=None, op0=ALU.mult)
                    v.tensor_tensor(out=st[:, 4:5], in0=st[:, 2:3], in1=st[:, 2:3],
                                    op=ALU.mult)
                    v.tensor_tensor(out=st[:, 5:6], in0=st[:, 3:4], in1=st[:, 4:5],
                                    op=ALU.subtract)
                    v.tensor_scalar(out=st[:, 5:6], in0=st[:, 5:6], scalar1=EPS,
                                    scalar2=None, op0=ALU.add)
                blk.vector(vmath)
                ve_var = c.val("ve")
                ln_st1_ve[0] = ve_var

                def srstd(sc, ve_var=ve_var):
                    sc = c.chain(sc, "se")
                    sc.wait_ge(c.sem("ve"), ve_var)
                    sc.activation(out=st[:, 6:7], in_=st[:, 5:6], func=AF.Ln)
                    sc.activation(out=st[:, 7:8], in_=st[:, 6:7], func=AF.Exp,
                                  scale=-0.5)
                blk.scalar(srstd)
                se_rstd = c.val("se")

                def vln(v, par=par, se_rstd=se_rstd):
                    v = c.chain(v, "ve")
                    v.wait_ge(c.sem("se"), se_rstd)
                    if ln_tr_pe:
                        v.wait_ge(c.sem("pe"), ln_tr_pe[-1])
                    v.tensor_scalar(out=ln_bf[:], in0=x_sb[:, par, :],
                                    scalar1=st[:, 2:3], scalar2=st[:, 7:8],
                                    op0=ALU.subtract, op1=ALU.mult)
                blk.vector(vln)
                ve_ln = c.val("ve")
                x_tile_ve[par] = ve_ln

                def ttr(t, ve_ln=ve_ln):
                    t.wait_ge(c.sem("ve"), ve_ln)
                    if ln_copy_ve:
                        t.wait_ge(c.sem("ve"), ln_copy_ve[-1])
                    for g in range(8):
                        i = t.transpose(ps_tr[:, g, :], ln_bf[:, g * 128:(g + 1) * 128],
                                        id_bf_sb[:])
                        c.inc(i, "pe", 1)
                blk.tensor(ttr)
                pe_tr = c.val("pe")
                ln_tr_pe.append(pe_tr)

                def vcp(v, pe_tr=pe_tr, m=m):
                    v = c.chain(v, "ve")
                    v.wait_ge(c.sem("pe"), pe_tr)
                    v.tensor_copy(out=lnxT[:, :, m * 128:(m + 1) * 128],
                                  in_=ps_tr[:, :, :])
                blk.vector(vcp)
                ln_copy_ve.append(c.val("ve"))
            ve_ln_done = c.val("ve")

            for kind in range(2):
                dst = qT if kind == 0 else kT
                for jb in range(8):
                    for tc in range(4):
                        pp = (jb * 4 + tc) % 2
                        bank = ps_q[pp]

                        def tqk(t, kind=kind, jb=jb, tc=tc, pp=pp, bank=bank,
                                ve_ln_done=ve_ln_done):
                            t.wait_ge(c.sem("swq"), swq_v)
                            t.wait_ge(c.sem("ve"), ve_ln_done)
                            if qk_evac_se[pp]:
                                t.wait_ge(c.sem("se"), qk_evac_se[pp])
                            if v_evac_ve[pp]:
                                t.wait_ge(c.sem("ve"), v_evac_ve[pp])
                            for g in range(8):
                                i = t.matmul(bank[:],
                                             w_sb[:, g, kind * D + jb * 128:
                                                  kind * D + (jb + 1) * 128],
                                             lnxT[:, g, tc * 512:(tc + 1) * 512],
                                             start=(g == 0), stop=(g == 7))
                                c.inc(i, "pe", 1)
                        blk.tensor(tqk)
                        pe_qk = c.val("pe")
                        ln_copy_pe_guard[0] = pe_qk

                        def sqk(sc, kind=kind, jb=jb, tc=tc, bank=bank, pe_qk=pe_qk,
                                dst=dst):
                            sc = c.chain(sc, "se")
                            sc.wait_ge(c.sem("pe"), pe_qk)
                            sc.activation(out=dst[:, jb, tc * 512:(tc + 1) * 512],
                                          in_=bank[:], func=AF.Identity,
                                          bias=bqk_sb[:, kind * 8 + jb:
                                                      kind * 8 + jb + 1])
                        blk.scalar(sqk)
                        qk_evac_se[pp] = c.val("se")
            qkprod_se = c.val("se")

            for qi in range(NQT):
                par = qi % 2
                row0 = base_row + qi * 128
                kend = (qi + 1) * 128
                nchunk = (kend + 511) // 512

                def vmemset(v, qi=qi, se_prev=exp_se[0]):
                    v = c.chain(v, "ve")
                    if se_prev:
                        v.wait_ge(c.sem("se"), se_prev)
                    v.memset(s_row[:], 0.0)
                blk.vector(vmemset)
                ve_ms = c.val("ve")

                def tsc(t, qi=qi, kend=kend, nchunk=nchunk, ve_ms=ve_ms,
                        qkprod_se=qkprod_se):
                    t.wait_ge(c.sem("se"), qkprod_se)
                    for j in range(nchunk):
                        cols = min(512, kend - j * 512)
                        if sbank_evac_ve[j]:
                            t.wait_ge(c.sem("ve"), sbank_evac_ve[j])
                        for g in range(8):
                            i = t.matmul(ps_s[:, j * 512:j * 512 + cols],
                                         qT[:, g, qi * 128:(qi + 1) * 128],
                                         kT[:, g, j * 512:j * 512 + cols],
                                         start=(g == 0), stop=(g == 7))
                            c.inc(i, "pe", 1)
                blk.tensor(tsc)
                pe_s = c.val("pe")

                def vevac(v, qi=qi, kend=kend, nchunk=nchunk, pe_s=pe_s,
                          ve_ms=ve_ms):
                    v = c.chain(v, "ve")
                    v.wait_ge(c.sem("pe"), pe_s)
                    for j in range(nchunk):
                        cols = min(512, kend - j * 512)
                        if cols > 128:
                            v.tensor_scalar(out=s_row[:, j * 512:j * 512 + cols - 128],
                                            in0=ps_s[:, j * 512:j * 512 + cols - 128],
                                            scalar1=SCALE, scalar2=None, op0=ALU.mult)
                        d0 = j * 512 + cols - 128
                        v.scalar_tensor_tensor(out=s_row[:, d0:d0 + 128],
                                               in0=ps_s[:, d0:d0 + 128],
                                               scalar=SCALE, in1=tril_sb[:],
                                               op0=ALU.mult, op1=ALU.mult)
                    v.reduce_max(st[:, 8:9], s_row[:], axis=AX.X, negate=True)
                blk.vector(vevac)
                ve_sev = c.val("ve")
                for j in range(nchunk):
                    sbank_evac_ve[j] = ve_sev

                def sexp(sc, ve_sev=ve_sev, pm=pmul_ve[0]):
                    sc = c.chain(sc, "se")
                    sc.wait_ge(c.sem("ve"), ve_sev)
                    if pm:
                        sc.wait_ge(c.sem("ve"), pm)
                    sc.activation(out=p_buf[:], in_=s_row[:], func=AF.Exp,
                                  bias=st[:, 8:9], accum_out=st[:, 9:10])
                blk.scalar(sexp)
                exp_se[0] = c.val("se")
                se_e = c.val("se")

                def ldx(sy, par=par, row0=row0):
                    if x_tile_ve.get(par) is not None:
                        sy.wait_ge(c.sem("ve"), x_tile_ve[par])
                    i = sy.dma_start(out=x_sb[:, par, :], in_=x_in[row0:row0 + 128, :])
                    c.inc(i, f"sx{par}", 16)
                blk.sync(ldx)
                sy_x = c.val(f"sx{par}")

                def vsum(v, se_e=se_e):
                    v = c.chain(v, "ve")
                    v.wait_ge(c.sem("se"), se_e)
                    v.reciprocal(st[:, 10:11], st[:, 9:10])
                    v.tensor_scalar(out=p_buf[:], in0=p_buf[:], scalar1=st[:, 10:11],
                                    scalar2=None, op0=ALU.mult, op1=ALU.add,
                                    accum_out=st[:, 11:12])
                blk.vector(vsum)
                pmul_ve[0] = c.val("ve")
                ve_r = c.val("ve")

                def tv(t, qi=qi, ve_r=ve_r):
                    for h in range(2):
                        bank = ps_q[h]
                        if qk_evac_se[h]:
                            t.wait_ge(c.sem("se"), qk_evac_se[h])
                        if v_evac_ve[h]:
                            t.wait_ge(c.sem("ve"), v_evac_ve[h])
                        for g in range(8):
                            i = t.matmul(bank[:],
                                         lnxT[:, g, qi * 128:(qi + 1) * 128],
                                         w_sb[:, g, 2 * D + h * 512:
                                              2 * D + (h + 1) * 512],
                                         start=(g == 0), stop=(g == 7))
                            c.inc(i, "pe", 1)
                blk.tensor(tv)
                pe_v = c.val("pe")
                qk_evac_se[0] = 0
                qk_evac_se[1] = 0

                def vwb(v, par=par, pe_v=pe_v, sy_x=sy_x, ve_r=ve_r):
                    v = c.chain(v, "ve")
                    v.wait_ge(c.sem("pe"), pe_v)
                    v.wait_ge(c.sem(f"sx{par}"), sy_x)
                    if xo_tile_sy.get(par) is not None:
                        v.wait_ge(c.sem(f"so{par}"), xo_tile_sy[par])
                    for h in range(2):
                        v.tensor_tensor(out=v_sc[:, h * 512:(h + 1) * 512],
                                        in0=ps_q[h][:],
                                        in1=bv_bc[:, h * 512:(h + 1) * 512],
                                        op=ALU.add)
                    v.tensor_scalar(out=v_sc[:], in0=v_sc[:], scalar1=st[:, 11:12],
                                    scalar2=None, op0=ALU.mult)
                    v.tensor_tensor(out=xo_sb[:, par, :], in0=v_sc[:],
                                    in1=x_sb[:, par, :], op=ALU.add)
                blk.vector(vwb)
                ve_wb = c.val("ve")
                v_evac_ve[0] = ve_wb
                v_evac_ve[1] = ve_wb
                x_tile_ve[par] = ve_wb

                def stx(sy, par=par, row0=row0, ve_wb=ve_wb):
                    sy.wait_ge(c.sem("ve"), ve_wb)
                    i = sy.dma_start(out=x_out[row0:row0 + 128, :],
                                     in_=xo_sb[:, par, :])
                    c.inc(i, f"so{par}", 16)
                blk.sync(stx)
                xo_tile_sy[par] = c.val(f"so{par}")

            ln_copy_pe_guard[0] = c.val("pe")

        so0_t, so1_t = c.val("so0"), c.val("so1")

        def fin(sy, so0_t=so0_t, so1_t=so1_t):
            sy.wait_ge(c.sem("so0"), so0_t)
            sy.wait_ge(c.sem("so1"), so1_t)
        blk.sync(fin)


def _declare_l0attn_drams(nc, pfx=""):
    return dict(
        wqkv=nc.dram_tensor(pfx + "wqkv", [D, 3 * D], F32,
                            kind="ExternalInput").ap(),
        bqk=nc.dram_tensor(pfx + "bqk", [128, 16], F32,
                           kind="ExternalInput").ap(),
        bv=nc.dram_tensor(pfx + "bv", [1, D], F32, kind="ExternalInput").ap(),
    )


def build_l0attn():
    nc = bass.Bass("TRN2", target_bir_lowering=False, debug=False)
    aps = _declare_l0attn_drams(nc)
    aps.update(_declare_shared_consts(nc))
    aps["x_in"] = nc.dram_tensor("x_in", [TOK, D], F32, kind="ExternalInput").ap()
    aps["x_out"] = nc.dram_tensor("x_out", [TOK, D], F32,
                                  kind="ExternalOutput").ap()
    with contextlib.ExitStack() as es:
        c = Ctx(nc, es)
        PB = es.enter_context(nc.psum_tensor("PB", [128, 2048], F32))
        PM = es.enter_context(nc.psum_tensor("PM", [128, 2, 512], F32))
        PA = es.enter_context(nc.psum_tensor("PA", [128, 2, 512], F32))
        sbase, _ = nc.bump_sbuf(nc.sbuf_bytes_remaining - 64)
        blk = c.wrap_block(es.enter_context(nc.Block()))
        emit_l0attn(nc, c, es, blk, aps, "a_", sbase, PB, PM, PA)
    return nc


def _l0attn_maps(inputs):
    Wq = np.asarray(inputs["L0_attn_Wq"], np.float32)
    Wk = np.asarray(inputs["L0_attn_Wk"], np.float32)
    Wv = np.asarray(inputs["L0_attn_Wv"], np.float32)
    bq = np.asarray(inputs["L0_attn_bq"], np.float32)
    bk = np.asarray(inputs["L0_attn_bk"], np.float32)
    bv = np.asarray(inputs["L0_attn_bv"], np.float32)
    wqkv = np.concatenate([Wq, Wk, Wv], axis=1)
    bqk = np.stack([bq.reshape(8, 128).T, bk.reshape(8, 128).T],
                   axis=1).reshape(128, 16)
    return dict(
        wqkv=np.ascontiguousarray(wqkv),
        bqk=np.ascontiguousarray(bqk),
        bv=bv.reshape(1, D),
    )


def run_l0attn(x_full, inputs, nc=None):
    if nc is None:
        nc = build_l0attn()
    base = _l0attn_maps(inputs)
    base.update(_shared_const_values())
    xr = np.asarray(x_full, np.float32).reshape(N_CORES, TOK, D)
    in_maps = [dict(base, x_in=np.ascontiguousarray(xr[i])) for i in range(N_CORES)]
    res = run_bass_kernel_spmd(nc, in_maps, core_ids=list(range(N_CORES)))
    out = np.stack([res.results[i]["x_out"] for i in range(N_CORES)])
    return out.reshape(16, 2048, 1024)



def emit_l1ffn(nc, c, es, blk, aps, pfx, sbase, pb_big, pb_mm, pb_at):
    x_in = aps["x_in"]
    wqkvo = aps["wqkvo"]
    bqk = aps["bqk"]
    bv_b = aps["bv_b"]
    bo_b = aps["bo_b"]
    ln_b = aps["ln_b"]
    gb_b = aps["gb_b"]
    gw = aps["gw"]
    w1d = aps["w1d"]
    w2d = aps["w2d"]
    b1d = aps["b1d"]
    b2d = aps["b2d"]
    id_f = aps["id_f"]
    ones_bf = aps["ones_bf"]
    ones_f = aps["ones_f"]
    neg1_bf = aps["neg1_bf"]
    x_out = aps["x_out"]
    combT_dram = aps["combT_dram"]
    m_dram = aps["m_dram"]

    NHEAD = 16
    ASCALE = 0.125
    NMT = SEQ // 128

    if True:
        nA, nB, nC, nD = 32768, 32768, 32768, 33280
        arena = sbase
        offA, offB, offC = arena, arena + nA, arena + nA + nB
        offD, offE = offC + nC, offC + nC + nD
        offW = offE + 32768
        cur = [offW + 16384]
        sba = _cursor_alloc(nc, pfx, cur)
        at = lambda n, s, d, offset: nc.alloc_sbuf_tensor_at(pfx + n, s, d,
                                                             offset=offset)
        fmA = at("fmA", [128, 8, SEQ], BF16, offset=offA)
        pT = at("pT", [128, 2, 512], BF16, offset=offA)
        m_row = at("m_row", [1, SEQ], BF16, offset=offA + 2048)
        m16 = at("m16", [16, 128], F32, offset=offA + 6144)
        rz_row = at("rz_row", [1, 512], F32, offset=offA + 6656)
        rz_bc = at("rz_bc", [64, 512], F32, offset=offA + 8704)
        qT = at("qT", [128, 8, SEQ], BF16, offset=offB)
        yaccT = at("yaccT", [128, 8, 1024], F32, offset=offB)
        kT = at("kT", [128, 8, SEQ], BF16, offset=offC)
        hT = at("hT", [128, 16, 1024], BF16, offset=offC)
        v_aug = at("v_aug", [128, 16, 16, 65], BF16, offset=offD)
        w1e = at("w1e", [128, 8, INTER], BF16, offset=offD)
        xi_tm = at("xi_tm", [128, 16, 1024], BF16, offset=offE)
        wch = at("wch", [128, 2, 8, 512], BF16, offset=offW)
        w2ch = at("w2ch", [128, 2, 4, 1024], BF16, offset=offW)

        x_sb = sba("x_sb", [128, D])
        f32a = sba("f32a", [128, D])
        f32b = sba("f32b", [128, D])
        st = sba("st", [128, 16])
        mcol = sba("mcol", [128, 16])
        gwork = sba("gwork", [128, 40])
        cmb8 = sba("cmb8", [8, 128])
        gw_sb = sba("gw_sb", [128, 8, 8], BF16)
        gb_bc = sba("gb_bc", [128, 8])
        bqk_sb = sba("bqk_sb", [128, 16])
        b1_sb = sba("b1_sb", [128, 128])
        b2_sb = sba("b2_sb", [128, 64])
        bv_bc = sba("bv_bc", [128, D], BF16)
        bo_bc = sba("bo_bc", [128, D], BF16)
        ln_bc = sba("ln_bc", [128, 4, D], BF16)
        id_f_sb = sba("id_f_sb", [128, 128], F32)
        ones_bf_sb = sba("ones_bf_sb", [1, 128], BF16)
        ones_f_sb = sba("ones_f_sb", [1, 128], F32)
        neg1_sb = sba("neg1_sb", [1, 128], BF16)
        comb_row = sba("comb_row", [1, 512])
        comb_sb = sba("comb_sb", [128, 2, 512], BF16)

        def pre_sync(sy):
            for dst, srcap in ((bqk_sb, bqk), (b1_sb, b1d), (b2_sb, b2d),
                               (gb_bc, gb_b), (id_f_sb, id_f), (ones_f_sb, ones_f)):
                i = sy.dma_start(out=dst[:], in_=srcap[:])
                c.inc(i, "syc", 16)
        blk.sync(pre_sync)
        syc_v = c.val("syc")

        def pre_gp(g):
            for dst, srcap in ((gw_sb[:], gw.rearrange("(cc p) j -> p cc j", p=128)),
                               (bv_bc[:], bv_b[:]), (bo_bc[:], bo_b[:]),
                               (ln_bc[:], ln_b.rearrange("p (l j) -> p l j", l=4)),
                               (ones_bf_sb[:], ones_bf[:]), (neg1_sb[:], neg1_bf[:])):
                i = g.dma_start(out=dst, in_=srcap)
                c.inc(i, "sgw", 16)
        blk.gpsimd(pre_gp)
        sgw_v = c.val("sgw")

        def gate_t(t):
            t.wait_ge(c.sem("syc"), syc_v)
            t.wait_ge(c.sem("sgw"), sgw_v)
        blk.tensor(gate_t)

        def gate_v(v):
            v.wait_ge(c.sem("syc"), syc_v)
            v.wait_ge(c.sem("sgw"), sgw_v)
        blk.vector(gate_v)

        def gate_s(s):
            s.wait_ge(c.sem("syc"), syc_v)
            s.wait_ge(c.sem("sgw"), sgw_v)
        blk.scalar(gate_s)

        x_tile_ve = [0]
        x_st_sy = [0]
        tr_pe = [0]
        ln_st1_ve = [0]
        wch_pe = {0: 0, 1: 0}
        pT_pe = {0: 0, 1: 0}
        stbank_se = {0: 0, 1: 0}
        stbank_ve = {0: 0, 1: 0}
        atbank_ve = {0: 0, 1: 0}
        bigbank_se = {j: 0 for j in range(4)}
        bigbank_ve = {j: 0 for j in range(4)}
        m_row_pe = [0]
        smd_v = [0]
        comb_row_pe = [0]

        def wait_big(t, banks):
            for j in banks:
                if bigbank_se[j]:
                    t.wait_ge(c.sem("se"), bigbank_se[j])
                if bigbank_ve[j]:
                    t.wait_ge(c.sem("ve"), bigbank_ve[j])

        def ln_stats(get_in, se_guard_extra=0):
            pass

        def emit_stats(src_ap, wait_list_v, wait_list_s):
            def vstats(v, wl=tuple(wait_list_v)):
                v = c.chain(v, "ve")
                for s_, val in wl:
                    v.wait_ge(c.sem(s_), val)
                v.reduce_sum(st[:, 0:1], src_ap, axis=AX.X)
            blk.vector(vstats)

            def sstats(sc, wl=tuple(wait_list_s), prev=ln_st1_ve[0]):
                sc = c.chain(sc, "se")
                for s_, val in wl:
                    sc.wait_ge(c.sem(s_), val)
                if prev:
                    sc.wait_ge(c.sem("ve"), prev)
                sc.activation(out=f32a[:], in_=src_ap, func=AF.Square,
                              accum_out=st[:, 1:2])
            blk.scalar(sstats)
            se_sq = c.val("se")

            def vmath(v, se_sq=se_sq):
                v = c.chain(v, "ve")
                v.wait_ge(c.sem("se"), se_sq)
                v.tensor_scalar(out=st[:, 2:3], in0=st[:, 0:1], scalar1=1.0 / D,
                                scalar2=None, op0=ALU.mult)
                v.tensor_scalar(out=st[:, 3:4], in0=st[:, 1:2], scalar1=1.0 / D,
                                scalar2=None, op0=ALU.mult)
                v.tensor_tensor(out=st[:, 4:5], in0=st[:, 2:3], in1=st[:, 2:3],
                                op=ALU.mult)
                v.tensor_tensor(out=st[:, 5:6], in0=st[:, 3:4], in1=st[:, 4:5],
                                op=ALU.subtract)
                v.tensor_scalar(out=st[:, 5:6], in0=st[:, 5:6], scalar1=EPS,
                                scalar2=None, op0=ALU.add)
            blk.vector(vmath)
            ln_st1_ve[0] = c.val("ve")
            ve_var = c.val("ve")

            def srstd(sc, ve_var=ve_var):
                sc = c.chain(sc, "se")
                sc.wait_ge(c.sem("ve"), ve_var)
                sc.activation(out=st[:, 6:7], in_=st[:, 5:6], func=AF.Ln)
                sc.activation(out=st[:, 7:8], in_=st[:, 6:7], func=AF.Exp,
                              scale=-0.5)
            blk.scalar(srstd)
            return c.val("se")

        def emit_tr_to_fm(dst3, mi, src_f32, ve_wait):
            def ttr(t, ve_wait=ve_wait):
                t.wait_ge(c.sem("ve"), ve_wait)
                wait_big(t, (0, 1))
                for g in range(8):
                    i = t.transpose(pb_big[:, g * 128:(g + 1) * 128],
                                    src_f32[:, g * 128:(g + 1) * 128], id_f_sb[:])
                    c.inc(i, "pe", 1)
            blk.tensor(ttr)
            pe_tr = c.val("pe")
            tr_pe[0] = pe_tr

            def vcp(v, pe_tr=pe_tr, mi=mi):
                v = c.chain(v, "ve")
                v.wait_ge(c.sem("pe"), pe_tr)
                v.tensor_copy(out=dst3[:, :, mi * 128:(mi + 1) * 128],
                              in_=pb_big[:, 0:1024].rearrange("p (g j) -> p g j",
                                                              g=8))
            blk.vector(vcp)
            bigbank_ve[0] = bigbank_ve[1] = c.val("ve")
            return c.val("ve")

        for b in range(B_PER_CORE):
            base_row = b * SEQ

            for m in range(NMT):
                row0 = base_row + m * 128

                def ld(sy, row0=row0, xv=x_tile_ve[0], xs=x_st_sy[0]):
                    if xv:
                        sy.wait_ge(c.sem("ve"), xv)
                    if xs:
                        sy.wait_ge(c.sem("sxo"), xs)
                    i = sy.dma_start(out=x_sb[:], in_=x_in[row0:row0 + 128, :])
                    c.inc(i, "sx", 16)
                blk.sync(ld)
                sy_here = c.val("sx")

                se_r = emit_stats(x_sb[:], [("sx", sy_here)], [("sx", sy_here)])

                def vln(v, se_r=se_r, m=m, tpg=tr_pe[0]):
                    v = c.chain(v, "ve")
                    v.wait_ge(c.sem("se"), se_r)
                    if tpg:
                        v.wait_ge(c.sem("pe"), tpg)
                    v.tensor_scalar(out=f32b[:], in0=x_sb[:], scalar1=st[:, 2:3],
                                    scalar2=st[:, 7:8], op0=ALU.subtract,
                                    op1=ALU.mult)
                    v.tensor_copy(out=xi_tm.ap()[:, m, :], in_=f32b[:])
                blk.vector(vln)
                x_tile_ve[0] = c.val("ve")
                emit_tr_to_fm(fmA.ap(), m, f32b[:], c.val("ve"))
            ve_src_done = c.val("ve")

            def vz(v, ve_src_done=ve_src_done, peg=c.val("pe")):
                v = c.chain(v, "ve")
                if peg:
                    v.wait_ge(c.sem("pe"), peg)
                v.memset(v_aug.ap()[:], 1.0)
            blk.vector(vz)

            for kind in range(2):
                dst = qT if kind == 0 else kT
                for jb in range(8):
                    wpar = (kind * 8 + jb) % 2

                    def lw(g, kind=kind, jb=jb, wpar=wpar, wg=c.val("pe"),
                           sgw_v=sgw_v):
                        g.wait_ge(c.sem("sgw"), sgw_v)
                        if wg:
                            g.wait_ge(c.sem("pe"), wg)
                        i = g.dma_start(
                            out=wch.ap()[:, wpar, :, 0:128],
                            in_=wqkvo.rearrange("(cc p) j -> p cc j", p=128)
                            [:, :, kind * D + jb * 128:kind * D + (jb + 1) * 128])
                        c.inc(i, f"swc{wpar}", 16)
                    blk.gpsimd(lw)
                    sw_here = c.val(f"swc{wpar}")

                    for tc in range(4):
                        pp = tc % 2

                        def tqk(t, jb=jb, tc=tc, pp=pp, wpar=wpar, sw_here=sw_here,
                                sbv=stbank_se[pp], vbv=stbank_ve[pp],
                                ve_src_done=ve_src_done):
                            t.wait_ge(c.sem(f"swc{wpar}"), sw_here)
                            t.wait_ge(c.sem("ve"), ve_src_done)
                            if sbv:
                                t.wait_ge(c.sem("se"), sbv)
                            if vbv:
                                t.wait_ge(c.sem("ve"), vbv)
                            for g in range(8):
                                i = t.matmul(pb_mm[:, pp, :], wch.ap()[:, wpar, g, 0:128],
                                             fmA.ap()[:, g, tc * 512:(tc + 1) * 512],
                                             start=(g == 0), stop=(g == 7))
                                c.inc(i, "pe", 1)
                        blk.tensor(tqk)
                        pe_qk = c.val("pe")
                        wch_pe[wpar] = pe_qk

                        def sqk(sc, jb=jb, tc=tc, pp=pp, pe_qk=pe_qk, kind=kind,
                                dst=dst):
                            sc = c.chain(sc, "se")
                            sc.wait_ge(c.sem("pe"), pe_qk)
                            sc.activation(out=dst.ap()[:, jb, tc * 512:(tc + 1) * 512],
                                          in_=pb_mm[:, pp, :], func=AF.Identity,
                                          bias=bqk_sb[:, kind * 8 + jb:
                                                      kind * 8 + jb + 1])
                        blk.scalar(sqk)
                        stbank_se[pp] = c.val("se")

            ve_vaug = c.val("ve")
            for half in range(2):
                def lwv(g, half=half, wg=c.val("pe")):
                    if wg:
                        g.wait_ge(c.sem("pe"), wg)
                    i = g.dma_start(
                        out=wch.ap()[:, half, :, :],
                        in_=wqkvo.rearrange("(cc p) j -> p cc j", p=128)
                        [:, :, 2 * D + half * 512:2 * D + (half + 1) * 512])
                    c.inc(i, f"swc{half}", 16)
                blk.gpsimd(lwv)
            sw_v = (c.val("swc0"), c.val("swc1"))
            for m in range(NMT):
                for half in range(2):
                    pp = (m * 2 + half) % 2

                    def tv(t, m=m, half=half, pp=pp, sw_v=sw_v,
                           sbv=stbank_se[pp], vbv=stbank_ve[pp]):
                        t.wait_ge(c.sem(f"swc{half}"), sw_v[half])
                        if sbv:
                            t.wait_ge(c.sem("se"), sbv)
                        if vbv:
                            t.wait_ge(c.sem("ve"), vbv)
                        for g in range(8):
                            i = t.matmul(pb_mm[:, pp, :],
                                         fmA.ap()[:, g, m * 128:(m + 1) * 128],
                                         wch.ap()[:, half, g, :],
                                         start=(g == 0), stop=(g == 7))
                            c.inc(i, "pe", 1)
                    blk.tensor(tv)
                    pe_v = c.val("pe")
                    wch_pe[half] = pe_v

                    def vv(v, m=m, half=half, pp=pp, pe_v=pe_v, ve_vaug=ve_vaug):
                        v = c.chain(v, "ve")
                        v.wait_ge(c.sem("pe"), pe_v)
                        v.tensor_tensor(
                            out=v_aug.ap()[:, m, 8 * half:8 * (half + 1), 0:64],
                            in0=pb_mm[:, pp, :].rearrange("p (hh dd) -> p hh dd",
                                                          hh=8),
                            in1=bv_bc[:, half * 512:(half + 1) * 512].rearrange(
                                "p (hh dd) -> p hh dd", hh=8),
                            op=ALU.add)
                    blk.vector(vv)
                    stbank_ve[pp] = c.val("ve")
            ve_qkv = c.val("ve")
            se_qkv = c.val("se")

            for h in [0, 2, 4, 6, 8, 10, 12, 14, 1, 3, 5, 7, 9, 11, 13, 15]:
                hb = h // 2
                ho = 64 * (h % 2)
                for qi in range(NMT):
                    def ts1(t, h=h, qi=qi, hb=hb, ho=ho, se_qkv=se_qkv,
                            ve_qkv=ve_qkv):
                        t.wait_ge(c.sem("se"), se_qkv)
                        t.wait_ge(c.sem("ve"), ve_qkv)
                        wait_big(t, (0, 1, 2, 3))
                        for j in range(4):
                            i = t.matmul(pb_big[:, j * 512:(j + 1) * 512],
                                         qT.ap()[ho:ho + 64, hb,
                                                 qi * 128:(qi + 1) * 128],
                                         kT.ap()[ho:ho + 64, hb,
                                                 j * 512:(j + 1) * 512],
                                         start=True, stop=True)
                            c.inc(i, "pe", 1)
                    blk.tensor(ts1)
                    pe_s1 = c.val("pe")

                    def vm1(v, qi=qi, pe_s1=pe_s1):
                        v = c.chain(v, "ve")
                        v.wait_ge(c.sem("pe"), pe_s1)
                        v.reduce_max(mcol[:, qi:qi + 1], pb_big[:], axis=AX.X)
                    blk.vector(vm1)
                    for j in range(4):
                        bigbank_ve[j] = c.val("ve")
                ve_mcol = c.val("ve")

                def tmc(t, ve_mcol=ve_mcol):
                    t.wait_ge(c.sem("ve"), ve_mcol)
                    if stbank_se[0]:
                        t.wait_ge(c.sem("se"), stbank_se[0])
                    if stbank_ve[0]:
                        t.wait_ge(c.sem("ve"), stbank_ve[0])
                    i = t.transpose(pb_mm[0:16, 0, 0:128], mcol[:], id_f_sb[:])
                    c.inc(i, "pe", 1)
                blk.tensor(tmc)
                pe_mc = c.val("pe")

                def vmc(v, pe_mc=pe_mc, smv=smd_v[0]):
                    v = c.chain(v, "ve")
                    v.wait_ge(c.sem("pe"), pe_mc)
                    if smv:
                        v.wait_ge(c.sem("smd"), smv)
                    v.tensor_copy(out=m16.ap()[:], in_=pb_mm[0:16, 0, 0:128])
                blk.vector(vmc)
                ve_m16 = c.val("ve")
                stbank_ve[0] = ve_m16

                def gmd(g, ve_m16=ve_m16, mrp=m_row_pe[0]):
                    g.wait_ge(c.sem("ve"), ve_m16)
                    i = g.dma_start(out=m_dram[:, :], in_=m16.ap()[:])
                    c.inc(i, "smd", 16)
                    g.wait_ge(c.sem("smd"), c.val("smd"))
                    if mrp:
                        g.wait_ge(c.sem("pe"), mrp)
                    i = g.dma_start(out=m_row.ap()[:, :],
                                    in_=m_dram.rearrange("a e -> (a e)")[None, :])
                    c.inc(i, "smd", 16)
                blk.gpsimd(gmd)
                smd_v[0] = c.val("smd")
                sml_v = c.val("smd")

                for qc in range(4):
                    atp = qc % 2
                    for kb in range(NMT):
                        stp = kb % 2

                        def tst(t, qc=qc, kb=kb, stp=stp, atp=atp, hb=hb, ho=ho,
                                sml_v=sml_v, sbv=stbank_se[stp], vbv=stbank_ve[stp],
                                abv=atbank_ve[atp], first=(kb == 0)):
                            t.wait_ge(c.sem("smd"), sml_v)
                            if sbv:
                                t.wait_ge(c.sem("se"), sbv)
                            if vbv:
                                t.wait_ge(c.sem("ve"), vbv)
                            if first and abv:
                                t.wait_ge(c.sem("ve"), abv)
                            i = t.matmul(pb_mm[:, stp, :],
                                         kT.ap()[ho:ho + 64, hb,
                                                 kb * 128:(kb + 1) * 128],
                                         qT.ap()[ho:ho + 64, hb,
                                                 qc * 512:(qc + 1) * 512],
                                         start=True, stop=False)
                            c.inc(i, "pe", 1)
                            i = t.matmul(pb_mm[:, stp, :], neg1_sb[:],
                                         m_row.ap()[0:1, qc * 512:(qc + 1) * 512],
                                         start=False, stop=True)
                            c.inc(i, "pe", 1)
                        blk.tensor(tst)
                        pe_st = c.val("pe")
                        m_row_pe[0] = pe_st
                        stbank_ve[stp] = 0

                        def sexp(sc, kb=kb, stp=stp, pe_st=pe_st, pg=pT_pe[stp]):
                            sc = c.chain(sc, "se")
                            sc.wait_ge(c.sem("pe"), pe_st)
                            if pg:
                                sc.wait_ge(c.sem("pe"), pg)
                            sc.activation(out=pT.ap()[:, stp, :],
                                          in_=pb_mm[:, stp, :], func=AF.Exp,
                                          scale=ASCALE)
                        blk.scalar(sexp)
                        stbank_se[stp] = c.val("se")
                        se_p = c.val("se")

                        def tpv(t, h=h, kb=kb, stp=stp, atp=atp, se_p=se_p):
                            t.wait_ge(c.sem("se"), se_p)
                            i = t.matmul(pb_at[0:65, atp, :],
                                         v_aug.ap()[:, kb, h, :], pT.ap()[:, stp, :],
                                         start=(kb == 0), stop=(kb == NMT - 1))
                            c.inc(i, "pe", 1)
                        blk.tensor(tpv)
                        pT_pe[stp] = c.val("pe")
                    pe_pv = c.val("pe")

                    def vrz(v, atp=atp, pe_pv=pe_pv):
                        v = c.chain(v, "ve")
                        v.wait_ge(c.sem("pe"), pe_pv)
                        v.reciprocal(rz_row.ap()[:, :], pb_at[64:65, atp, :])
                    blk.vector(vrz)
                    ve_rz = c.val("ve")

                    def trz(t, ve_rz=ve_rz, sbv=stbank_se[0], vbv=stbank_ve[0]):
                        t.wait_ge(c.sem("ve"), ve_rz)
                        if sbv:
                            t.wait_ge(c.sem("se"), sbv)
                        if vbv:
                            t.wait_ge(c.sem("ve"), vbv)
                        i = t.matmul(pb_mm[0:64, 0, :], ones_f_sb[0:1, 0:64],
                                     rz_row.ap()[:, :], start=True, stop=True)
                        c.inc(i, "pe", 1)
                    blk.tensor(trz)
                    pe_rz = c.val("pe")

                    def vat(v, qc=qc, atp=atp, pe_rz=pe_rz, hb=hb, ho=ho):
                        v = c.chain(v, "ve")
                        v.wait_ge(c.sem("pe"), pe_rz)
                        v.tensor_copy(out=rz_bc.ap()[:, :], in_=pb_mm[0:64, 0, :])
                        v.tensor_tensor(out=qT.ap()[ho:ho + 64, hb,
                                                    qc * 512:(qc + 1) * 512],
                                        in0=pb_at[0:64, atp, :], in1=rz_bc.ap()[:, :],
                                        op=ALU.mult)
                    blk.vector(vat)
                    atbank_ve[atp] = c.val("ve")
                    stbank_ve[0] = c.val("ve")
            ve_attn = c.val("ve")

            for half in range(2):
                def lwo(g, half=half, wg=c.val("pe")):
                    if wg:
                        g.wait_ge(c.sem("pe"), wg)
                    i = g.dma_start(
                        out=wch.ap()[:, half, :, :],
                        in_=wqkvo.rearrange("(cc p) j -> p cc j", p=128)
                        [:, :, 3 * D + half * 512:3 * D + (half + 1) * 512])
                    c.inc(i, f"swc{half}", 16)
                blk.gpsimd(lwo)
            sw_o = (c.val("swc0"), c.val("swc1"))

            for m in range(NMT):
                for half in range(2):
                    def two(t, m=m, half=half, sw_o=sw_o, ve_attn=ve_attn,
                            sbv=stbank_se[half], vbv=stbank_ve[half]):
                        t.wait_ge(c.sem(f"swc{half}"), sw_o[half])
                        t.wait_ge(c.sem("ve"), ve_attn)
                        if sbv:
                            t.wait_ge(c.sem("se"), sbv)
                        if vbv:
                            t.wait_ge(c.sem("ve"), vbv)
                        for g in range(8):
                            i = t.matmul(pb_mm[:, half, :],
                                         qT.ap()[:, g, m * 128:(m + 1) * 128],
                                         wch.ap()[:, half, g, :],
                                         start=(g == 0), stop=(g == 7))
                            c.inc(i, "pe", 1)
                    blk.tensor(two)
                    wch_pe[half] = c.val("pe")
                pe_wo = c.val("pe")

                def vx1(v, m=m, pe_wo=pe_wo, tpg=tr_pe[0]):
                    v = c.chain(v, "ve")
                    v.wait_ge(c.sem("pe"), pe_wo)
                    if tpg:
                        v.wait_ge(c.sem("pe"), tpg)
                    for half in range(2):
                        v.tensor_tensor(out=f32b[:, half * 512:(half + 1) * 512],
                                        in0=pb_mm[:, half, :],
                                        in1=bo_bc[:, half * 512:(half + 1) * 512],
                                        op=ALU.add)
                    v.tensor_tensor(out=f32b[:], in0=f32b[:],
                                    in1=xi_tm.ap()[:, m, :], op=ALU.add)
                blk.vector(vx1)
                ve_x1 = c.val("ve")
                stbank_ve[0] = stbank_ve[1] = ve_x1

                se_r1 = emit_stats(f32b[:], [("ve", ve_x1)], [("ve", ve_x1)])

                def vxint(v, m=m, se_r1=se_r1):
                    v = c.chain(v, "ve")
                    v.wait_ge(c.sem("se"), se_r1)
                    v.tensor_scalar(out=f32b[:], in0=f32b[:], scalar1=st[:, 2:3],
                                    scalar2=st[:, 7:8], op0=ALU.subtract,
                                    op1=ALU.mult)
                    v.tensor_tensor(out=f32b[:], in0=f32b[:], in1=ln_bc[:, 0, :],
                                    op=ALU.mult)
                    v.tensor_tensor(out=f32b[:], in0=f32b[:], in1=ln_bc[:, 1, :],
                                    op=ALU.add)
                    v.tensor_copy(out=xi_tm.ap()[:, m, :], in_=f32b[:])
                blk.vector(vxint)
                emit_tr_to_fm(fmA.ap(), m, f32b[:], c.val("ve"))
            ve_xint = c.val("ve")

            comb_st = [0]
            for m in range(NMT):
                def tlog(t, m=m, ve_xint=ve_xint, sgw_v=sgw_v,
                         sbv=stbank_se[0], vbv=stbank_ve[0]):
                    t.wait_ge(c.sem("sgw"), sgw_v)
                    t.wait_ge(c.sem("ve"), ve_xint)
                    if sbv:
                        t.wait_ge(c.sem("se"), sbv)
                    if vbv:
                        t.wait_ge(c.sem("ve"), vbv)
                    for g in range(8):
                        i = t.matmul(pb_mm[:, 0, 0:8],
                                     fmA.ap()[:, g, m * 128:(m + 1) * 128],
                                     gw_sb[:, g, :], start=(g == 0), stop=(g == 7))
                        c.inc(i, "pe", 1)
                blk.tensor(tlog)
                pe_log = c.val("pe")

                def vsm1(v, pe_log=pe_log):
                    v = c.chain(v, "ve")
                    v.wait_ge(c.sem("pe"), pe_log)
                    v.tensor_tensor(out=gwork[:, 0:8], in0=pb_mm[:, 0, 0:8],
                                    in1=gb_bc[:], op=ALU.add)
                    v.reduce_max(st[:, 8:9], gwork[:, 0:8], axis=AX.X, negate=True)
                blk.vector(vsm1)
                ve_nm = c.val("ve")
                stbank_ve[0] = ve_nm

                def sexpg(sc, ve_nm=ve_nm):
                    sc = c.chain(sc, "se")
                    sc.wait_ge(c.sem("ve"), ve_nm)
                    sc.activation(out=gwork[:, 0:8], in_=gwork[:, 0:8],
                                  func=AF.Exp, bias=st[:, 8:9],
                                  accum_out=st[:, 9:10])
                blk.scalar(sexpg)
                se_eg = c.val("se")
                stbank_se[0] = se_eg

                def vsm2(v, se_eg=se_eg, tpg=tr_pe[0]):
                    v = c.chain(v, "ve")
                    v.wait_ge(c.sem("se"), se_eg)
                    if tpg:
                        v.wait_ge(c.sem("pe"), tpg)
                    v.reciprocal(st[:, 10:11], st[:, 9:10])
                    v.tensor_scalar(out=gwork[:, 8:16], in0=gwork[:, 0:8],
                                    scalar1=st[:, 10:11], scalar2=None, op0=ALU.mult)
                    v.reduce_max(st[:, 11:12], gwork[:, 8:16], axis=AX.X)
                    v.tensor_scalar(out=gwork[:, 24:32], in0=gwork[:, 8:16],
                                    scalar1=st[:, 11:12], scalar2=None,
                                    op0=ALU.is_equal)
                    v.scalar_tensor_tensor(out=gwork[:, 24:32], in0=gwork[:, 24:32],
                                           scalar=NEG_BIG, in1=gwork[:, 8:16],
                                           op0=ALU.mult, op1=ALU.add)
                    v.reduce_max(st[:, 12:13], gwork[:, 24:32], axis=AX.X)
                    v.tensor_scalar(out=gwork[:, 32:40], in0=gwork[:, 8:16],
                                    scalar1=st[:, 12:13], scalar2=None, op0=ALU.is_ge)
                    v.tensor_tensor(out=gwork[:, 16:24], in0=gwork[:, 8:16],
                                    in1=gwork[:, 32:40], op=ALU.mult)
                blk.vector(vsm2)
                ve_cb = c.val("ve")

                def tcb(t, ve_cb=ve_cb):
                    t.wait_ge(c.sem("ve"), ve_cb)
                    wait_big(t, (0,))
                    i = t.transpose(pb_big[0:8, 0:128], gwork[:, 16:24], id_f_sb[:])
                    c.inc(i, "pe", 1)
                blk.tensor(tcb)
                pe_cb = c.val("pe")
                tr_pe[0] = pe_cb

                def vcb(v, pe_cb=pe_cb, prev=comb_st[0]):
                    v = c.chain(v, "ve")
                    v.wait_ge(c.sem("pe"), pe_cb)
                    if prev:
                        v.wait_ge(c.sem("scb"), prev)
                    v.tensor_copy(out=cmb8[:], in_=pb_big[0:8, 0:128])
                blk.vector(vcb)
                ve_cbc = c.val("ve")
                bigbank_ve[0] = ve_cbc

                def stcb(sy, m=m, b=b, ve_cbc=ve_cbc):
                    sy.wait_ge(c.sem("ve"), ve_cbc)
                    col0 = b * SEQ + m * 128
                    i = sy.dma_start(out=combT_dram[:, col0:col0 + 128], in_=cmb8[:])
                    c.inc(i, "scb", 16)
                blk.sync(stcb)
                comb_st[0] = c.val("scb")
            scb_done = c.val("scb")

            h_reads_pe = [0]
            w1_reads_pe = [0]
            w2_reads_pe = {0: 0, 1: 0}
            psy_se = {j: 0 for j in range(8)}
            yacc_tr_pe = [0]

            def psy_bank(dgrp):
                if dgrp < 4:
                    return pb_big[:, dgrp * 512:(dgrp + 1) * 512]
                if dgrp < 6:
                    return pb_mm[:, dgrp - 4, :]
                return pb_at[:, dgrp - 6, :]

            for sblk in range(2):
                tok0 = b * SEQ + sblk * 1024

                def vzy(v, ytp=max(yacc_tr_pe[0], wch_pe[0], wch_pe[1])):
                    v = c.chain(v, "ve")
                    if ytp:
                        v.wait_ge(c.sem("pe"), ytp)
                    v.memset(yaccT.ap()[:], 0.0)
                blk.vector(vzy)
                ve_yz = c.val("ve")

                for e in range(8):
                    def lw1(g, e=e, wrp=max(w1_reads_pe[0], pT_pe[0], pT_pe[1]),
                            scd=scb_done):
                        g.wait_ge(c.sem("scb"), scd)
                        if wrp:
                            g.wait_ge(c.sem("pe"), wrp)
                        i = g.dma_start(out=w1e.ap()[:],
                                        in_=w1d[e].rearrange("(cc p) i -> p cc i",
                                                             p=128))
                        c.inc(i, "sw1", 16)
                    blk.gpsimd(lw1)
                    sw1_v = c.val("sw1")

                    for tc2 in range(2):
                        def ldcr(sy, e=e, tc2=tc2, tok0=tok0, crp=comb_row_pe[0],
                                 scd=scb_done):
                            sy.wait_ge(c.sem("scb"), scd)
                            if crp:
                                sy.wait_ge(c.sem("pe"), crp)
                            col0 = tok0 + tc2 * 512
                            i = sy.dma_start(out=comb_row[:],
                                             in_=combT_dram[e:e + 1,
                                                            col0:col0 + 512])
                            c.inc(i, "scr", 16)
                        blk.sync(ldcr)
                        scr_v = c.val("scr")

                        def tbc(t, tc2=tc2, scr_v=scr_v, sbv=stbank_se[0],
                                vbv=stbank_ve[0]):
                            t.wait_ge(c.sem("scr"), scr_v)
                            if sbv:
                                t.wait_ge(c.sem("se"), sbv)
                            if vbv:
                                t.wait_ge(c.sem("ve"), vbv)
                            i = t.matmul(pb_mm[:, 0, :], ones_f_sb[:], comb_row[:],
                                         start=True, stop=True)
                            c.inc(i, "pe", 1)
                        blk.tensor(tbc)
                        pe_bc = c.val("pe")
                        comb_row_pe[0] = pe_bc

                        def vbc(v, tc2=tc2, pe_bc=pe_bc):
                            v = c.chain(v, "ve")
                            v.wait_ge(c.sem("pe"), pe_bc)
                            v.tensor_copy(out=comb_sb[:, tc2, :], in_=pb_mm[:, 0, :])
                        blk.vector(vbc)
                        stbank_ve[0] = c.val("ve")
                    ve_compb = c.val("ve")

                    for tc2 in range(2):
                        tc = sblk * 2 + tc2
                        for ib in range(16):
                            pp = ib % 2

                            def tA(t, e=e, tc=tc, tc2=tc2, ib=ib, pp=pp,
                                   sw1_v=sw1_v, sbv=bigbank_se[pp],
                                   vbv=bigbank_ve[pp], vyz=ve_yz):
                                t.wait_ge(c.sem("sw1"), sw1_v)
                                t.wait_ge(c.sem("ve"), vyz)
                                if sbv:
                                    t.wait_ge(c.sem("se"), sbv)
                                if vbv:
                                    t.wait_ge(c.sem("ve"), vbv)
                                for g in range(8):
                                    i = t.matmul(pb_big[:, pp * 512:(pp + 1) * 512],
                                                 w1e.ap()[:, g, ib * 128:(ib + 1) * 128],
                                                 fmA.ap()[:, g, tc * 512:(tc + 1) * 512],
                                                 start=(g == 0), stop=(g == 7))
                                    c.inc(i, "pe", 1)
                            blk.tensor(tA)
                            pe_A = c.val("pe")
                            w1_reads_pe[0] = pe_A

                            def sA(sc, e=e, ib=ib, tc2=tc2, pp=pp, pe_A=pe_A,
                                   hrp=h_reads_pe[0]):
                                sc = c.chain(sc, "se")
                                sc.wait_ge(c.sem("pe"), pe_A)
                                if hrp:
                                    sc.wait_ge(c.sem("pe"), hrp)
                                sc.activation(out=hT.ap()[:, ib,
                                                          tc2 * 512:(tc2 + 1) * 512],
                                              in_=pb_big[:, pp * 512:(pp + 1) * 512],
                                              func=AF.Relu,
                                              bias=b1_sb[:, e * 16 + ib:
                                                         e * 16 + ib + 1])
                            blk.scalar(sA)
                            bigbank_se[pp] = c.val("se")
                    se_hdone = c.val("se")

                    for tc2 in range(2):
                        for icg in range(4):
                            wpar = icg % 2

                            def lw2(g, e=e, icg=icg, wpar=wpar,
                                    wrp=w2_reads_pe[wpar] if icg >= 2
                                    else c.val("pe")):
                                if wrp:
                                    g.wait_ge(c.sem("pe"), wrp)
                                i = g.dma_start(
                                    out=w2ch.ap()[:, wpar, :, :],
                                    in_=w2d[e].rearrange("(cc p) j -> p cc j", p=128)
                                    [:, icg * 4:(icg + 1) * 4, :])
                                c.inc(i, f"sw2{wpar}", 16)
                            blk.gpsimd(lw2)
                            sw2_v = c.val(f"sw2{wpar}")

                            def tB(t, e=e, tc2=tc2, icg=icg, wpar=wpar,
                                   sw2_v=sw2_v, se_hdone=se_hdone,
                                   psw=[psy_se[j] for j in range(8)],
                                   veg=c.val("ve")):
                                t.wait_ge(c.sem(f"sw2{wpar}"), sw2_v)
                                t.wait_ge(c.sem("se"), se_hdone)
                                if icg == 0:
                                    t.wait_ge(c.sem("ve"), veg)
                                    for j in range(8):
                                        if psw[j]:
                                            t.wait_ge(c.sem("se"), psw[j])
                                for ici in range(4):
                                    ic = icg * 4 + ici
                                    for dgrp in range(8):
                                        i = t.matmul(
                                            psy_bank(dgrp),
                                            w2ch.ap()[:, wpar, ici,
                                                      dgrp * 128:(dgrp + 1) * 128],
                                            hT.ap()[:, ic, tc2 * 512:(tc2 + 1) * 512],
                                            start=(ic == 0), stop=(ic == 15))
                                        c.inc(i, "pe", 1)
                            blk.tensor(tB)
                            pe_B = c.val("pe")
                            w2_reads_pe[wpar] = pe_B
                            h_reads_pe[0] = pe_B
                        pe_Ball = c.val("pe")

                        f32a_ve = {0: 0, 1: 0}
                        for dgrp in range(8):
                            def sev(sc, e=e, dgrp=dgrp, pe_Ball=pe_Ball,
                                    par=dgrp % 2, fav=None):
                                sc = c.chain(sc, "se")
                                sc.wait_ge(c.sem("pe"), pe_Ball)
                                if fav:
                                    sc.wait_ge(c.sem("ve"), fav)
                                sc.activation(out=f32a[:, par * 512:(par + 1) * 512],
                                              in_=psy_bank(dgrp), func=AF.Identity,
                                              bias=b2_sb[:, e * 8 + dgrp:
                                                         e * 8 + dgrp + 1])
                            sev.__defaults__ = sev.__defaults__[:-1] + (
                                f32a_ve[dgrp % 2],)
                            blk.scalar(sev)
                            psy_se[dgrp] = c.val("se")
                            se_ev = c.val("se")

                            def vev(v, e=e, tc2=tc2, dgrp=dgrp, se_ev=se_ev,
                                    par=dgrp % 2, ve_compb=ve_compb):
                                v = c.chain(v, "ve")
                                v.wait_ge(c.sem("se"), se_ev)
                                v.scalar_tensor_tensor(
                                    out=f32b[:, par * 512:(par + 1) * 512],
                                    in0=f32a[:, par * 512:(par + 1) * 512],
                                    scalar=1.0, in1=comb_sb[:, tc2, :],
                                    op0=ALU.mult, op1=ALU.mult)
                                v.tensor_tensor(
                                    out=yaccT.ap()[:, dgrp, tc2 * 512:(tc2 + 1) * 512],
                                    in0=yaccT.ap()[:, dgrp, tc2 * 512:(tc2 + 1) * 512],
                                    in1=f32b[:, par * 512:(par + 1) * 512],
                                    op=ALU.add)
                            blk.vector(vev)
                            f32a_ve[dgrp % 2] = c.val("ve")
                        ve_evd = c.val("ve")
                        stbank_ve[0] = stbank_ve[1] = ve_evd
                        atbank_ve[0] = atbank_ve[1] = ve_evd
                ve_moe = c.val("ve")

                for msb in range(8):
                    m = sblk * 8 + msb
                    row0 = b * SEQ + m * 128

                    def twb(t, msb=msb, ve_moe=ve_moe):
                        t.wait_ge(c.sem("ve"), ve_moe)
                        wait_big(t, (0, 1))
                        for g in range(8):
                            i = t.transpose(pb_big[:, g * 128:(g + 1) * 128],
                                            yaccT.ap()[:, g, msb * 128:(msb + 1) * 128],
                                            id_f_sb[:])
                            c.inc(i, "pe", 1)
                    blk.tensor(twb)
                    pe_wb = c.val("pe")
                    yacc_tr_pe[0] = pe_wb

                    def vx2(v, m=m, pe_wb=pe_wb, tpg=tr_pe[0]):
                        v = c.chain(v, "ve")
                        v.wait_ge(c.sem("pe"), pe_wb)
                        v.tensor_tensor(out=f32b[:],
                                        in0=pb_big[:, 0:1024],
                                        in1=xi_tm.ap()[:, m, :], op=ALU.add)
                    blk.vector(vx2)
                    ve_x2 = c.val("ve")
                    bigbank_ve[0] = bigbank_ve[1] = ve_x2

                    se_r2 = emit_stats(f32b[:], [("ve", ve_x2)], [("ve", ve_x2)])

                    def ldx2(sy, row0=row0, xv=x_tile_ve[0], xs=x_st_sy[0]):
                        if xv:
                            sy.wait_ge(c.sem("ve"), xv)
                        if xs:
                            sy.wait_ge(c.sem("sxo"), xs)
                        i = sy.dma_start(out=x_sb[:], in_=x_in[row0:row0 + 128, :])
                        c.inc(i, "sx", 16)
                    blk.sync(ldx2)
                    sy_x2 = c.val("sx")

                    def vout(v, se_r2=se_r2, sy_x2=sy_x2):
                        v = c.chain(v, "ve")
                        v.wait_ge(c.sem("se"), se_r2)
                        v.wait_ge(c.sem("sx"), sy_x2)
                        v.tensor_scalar(out=f32b[:], in0=f32b[:], scalar1=st[:, 2:3],
                                        scalar2=st[:, 7:8], op0=ALU.subtract,
                                        op1=ALU.mult)
                        v.tensor_tensor(out=f32b[:], in0=f32b[:], in1=ln_bc[:, 2, :],
                                        op=ALU.mult)
                        v.tensor_tensor(out=f32b[:], in0=f32b[:], in1=ln_bc[:, 3, :],
                                        op=ALU.add)
                        v.scalar_tensor_tensor(out=x_sb[:], in0=f32b[:], scalar=0.5,
                                               in1=x_sb[:], op0=ALU.mult,
                                               op1=ALU.add)
                    blk.vector(vout)
                    ve_out = c.val("ve")
                    x_tile_ve[0] = ve_out

                    def stx(sy, row0=row0, ve_out=ve_out):
                        sy.wait_ge(c.sem("ve"), ve_out)
                        i = sy.dma_start(out=x_out[row0:row0 + 128, :], in_=x_sb[:])
                        c.inc(i, "sxo", 16)
                    blk.sync(stx)
                    x_st_sy[0] = c.val("sxo")

        sxo_t = c.val("sxo")

        def fin(sy, sxo_t=sxo_t):
            sy.wait_ge(c.sem("sxo"), sxo_t)
        blk.sync(fin)


def _declare_l1ffn_drams(nc, pfx=""):
    return dict(
        wqkvo=nc.dram_tensor(pfx + "wqkvo", [D, 4 * D], F32,
                             kind="ExternalInput").ap(),
        bqk=nc.dram_tensor(pfx + "bqk", [128, 16], F32,
                           kind="ExternalInput").ap(),
        bv_b=nc.dram_tensor(pfx + "bv_b", [128, D], F32,
                            kind="ExternalInput").ap(),
        bo_b=nc.dram_tensor(pfx + "bo_b", [128, D], F32,
                            kind="ExternalInput").ap(),
        ln_b=nc.dram_tensor(pfx + "ln_b", [128, 4 * D], F32,
                            kind="ExternalInput").ap(),
        gb_b=nc.dram_tensor(pfx + "gb_b", [128, 8], F32,
                            kind="ExternalInput").ap(),
        gw=nc.dram_tensor(pfx + "gw", [D, 8], F32, kind="ExternalInput").ap(),
        w1d=nc.dram_tensor(pfx + "w1d", [8, D, INTER], F32,
                           kind="ExternalInput").ap(),
        w2d=nc.dram_tensor(pfx + "w2d", [8, INTER, D], F32,
                           kind="ExternalInput").ap(),
        b1d=nc.dram_tensor(pfx + "b1d", [128, 128], F32,
                           kind="ExternalInput").ap(),
        b2d=nc.dram_tensor(pfx + "b2d", [128, 64], F32,
                           kind="ExternalInput").ap(),
    )


def build_l1ffn():
    nc = bass.Bass("TRN2", target_bir_lowering=False, debug=False)
    aps = _declare_l1ffn_drams(nc)
    aps.update(_declare_shared_consts(nc))
    aps["x_in"] = nc.dram_tensor("x_in", [TOK, D], F32, kind="ExternalInput").ap()
    aps["x_out"] = nc.dram_tensor("x_out", [TOK, D], F32,
                                  kind="ExternalOutput").ap()
    aps["combT_dram"] = nc.dram_tensor("combT_dram", [8, TOK], F32,
                                       kind="Internal").ap()
    aps["m_dram"] = nc.dram_tensor("m_dram", [16, 128], F32,
                                   kind="Internal").ap()
    with contextlib.ExitStack() as es:
        c = Ctx(nc, es)
        PB = es.enter_context(nc.psum_tensor("PB", [128, 2048], F32))
        PM = es.enter_context(nc.psum_tensor("PM", [128, 2, 512], F32))
        PA = es.enter_context(nc.psum_tensor("PA", [128, 2, 512], F32))
        sbase, _ = nc.bump_sbuf(nc.sbuf_bytes_remaining - 64)
        blk = c.wrap_block(es.enter_context(nc.Block()))
        emit_l1ffn(nc, c, es, blk, aps, "q_", sbase, PB, PM, PA)
    return nc


def _l1ffn_maps(inputs):
    Wq = np.asarray(inputs["L1_ffn_Wq"], np.float32)
    Wk = np.asarray(inputs["L1_ffn_Wk"], np.float32)
    Wv = np.asarray(inputs["L1_ffn_Wv"], np.float32)
    Wo = np.asarray(inputs["L1_ffn_Wo"], np.float32)
    bq = np.asarray(inputs["L1_ffn_bq"], np.float32)
    bk = np.asarray(inputs["L1_ffn_bk"], np.float32)
    bv = np.asarray(inputs["L1_ffn_bv"], np.float32)
    bo = np.asarray(inputs["L1_ffn_bo"], np.float32)
    b1 = np.asarray(inputs["L1_ffn_b1"], np.float32)
    b2 = np.asarray(inputs["L1_ffn_b2"], np.float32)
    lnq = np.stack([np.asarray(inputs["L1_ffn_ln1_g"], np.float32),
                    np.asarray(inputs["L1_ffn_ln1_b"], np.float32),
                    np.asarray(inputs["L1_ffn_ln2_g"], np.float32),
                    np.asarray(inputs["L1_ffn_ln2_b"], np.float32)])
    return dict(
        wqkvo=np.ascontiguousarray(np.concatenate([Wq, Wk, Wv, Wo], axis=1)),
        bqk=np.ascontiguousarray(
            np.stack([bq.reshape(8, 128).T, bk.reshape(8, 128).T],
                     axis=1).reshape(128, 16)),
        bv_b=np.ascontiguousarray(np.broadcast_to(bv, (128, D))),
        bo_b=np.ascontiguousarray(np.broadcast_to(bo, (128, D))),
        ln_b=np.ascontiguousarray(
            np.broadcast_to(lnq.reshape(1, 4 * D), (128, 4 * D))),
        gb_b=np.ascontiguousarray(
            np.broadcast_to(np.asarray(inputs["L1_ffn_gb"], np.float32), (128, 8))),
        gw=np.ascontiguousarray(np.asarray(inputs["L1_ffn_gW"], np.float32)),
        w1d=np.ascontiguousarray(np.asarray(inputs["L1_ffn_W1"], np.float32)),
        w2d=np.ascontiguousarray(np.asarray(inputs["L1_ffn_W2"], np.float32)),
        b1d=np.ascontiguousarray(b1.reshape(8, 16, 128).transpose(2, 0, 1)
                                 .reshape(128, 128)),
        b2d=np.ascontiguousarray(b2.reshape(8, 8, 128).transpose(2, 0, 1)
                                 .reshape(128, 64)),
    )


def run_l1ffn(x_full, inputs, nc=None):
    if nc is None:
        nc = build_l1ffn()
    base = _l1ffn_maps(inputs)
    base.update(_shared_const_values())
    xr = np.asarray(x_full, np.float32).reshape(N_CORES, TOK, D)
    in_maps = [dict(base, x_in=np.ascontiguousarray(xr[i])) for i in range(N_CORES)]
    res = run_bass_kernel_spmd(nc, in_maps, core_ids=list(range(N_CORES)))
    out = np.stack([res.results[i]["x_out"] for i in range(N_CORES)])
    return out.reshape(16, 2048, 1024)



def emit_l1attn(nc, c, es, blk, aps, pfx, sbase, pb_big, pb_mm, pb_at):
    x_in = aps["x_in"]
    w1T = aps["w1T"]
    w2T = aps["w2T"]
    inpT = aps["inpT"]
    xprojT_d = aps["xprojT_d"]
    dtprojT_d = aps["dtprojT_d"]
    outT_d = aps["outT_d"]
    A_d = aps["A_d"]
    Dcol_d = aps["Dcol_d"]
    convw_d = aps["convw_d"]
    convb_d = aps["convb_d"]
    dtpb_d = aps["dtpb_d"]
    b1pp_d = aps["b1pp_d"]
    b2pp_d = aps["b2pp_d"]
    lnbc_d = aps["lnbc_d"]
    id_f = aps["id_f"]
    id_bf = aps["id_bf"]
    ones_f = aps["ones_f"]
    ones_bfd = aps["ones_bf"]
    x_out = aps["x_out"]
    v_dram = aps["v_dram"]
    qx_dram = aps["qx_dram"]
    z_dram = aps["z_dram"]
    bc_dram = aps["bc_dram"]
    gq_dram = aps["gq_dram"]
    cc_in_h = aps["cc_in_h"]
    cc_out_h = aps["cc_out_h"]
    cc_in_d = cc_in_h.ap()
    cc_out_d = cc_out_h.ap()

    QCLIP = 128.0 - EPS

    if True:
        nRA, nRB, nRC = 65664, 24576, 24576
        RA, RB, RC = sbase, sbase + nRA, sbase + nRA + nRB
        cur = [sbase + nRA + nRB + nRC]
        sba = _cursor_alloc(nc, pfx, cur)
        at = lambda n, s, d, offset: nc.alloc_sbuf_tensor_at(pfx + n, s, d,
                                                             offset=offset)
        convbuf = at("convbuf", [128, 16, 2052], BF16, offset=RA)
        gatedT = at("gatedT", [128, 16, 512], BF16, offset=RA)
        u2f = at("u2f", [128, 8, 512], F32, offset=RA + 16384)
        u0_tc = at("u0_tc", [128, 4, D], F32, offset=RA + 32768)
        wf32 = at("wf32", [128, 2, 8, 128], F32, offset=RA + 49152)
        ws_bf = at("ws_bf", [128, 2, 8, 128], BF16, offset=RA + 57344)
        w2f = at("w2f", [128, 2, 1024], F32, offset=RB)
        w2s = at("w2s", [128, 2, 1024], BF16, offset=RB + 8192)
        wk_bf = at("wk_bf", [128, 2, 8, 256], BF16, offset=RB)
        dt_f32 = at("dt_f32", [128, SEQ], F32, offset=RB)
        dtx_bf = at("dtx_bf", [128, SEQ], BF16, offset=RB + 8192)
        a_bf = at("a_bf", [128, SEQ], BF16, offset=RB + 12288)
        h_bf = at("h_bf", [128, SEQ], BF16, offset=RB + 16384)
        outw = at("outw", [128, 2, 2, 1024], BF16, offset=RB)
        outf32 = at("outf32", [128, 8, 512], F32, offset=RB + 8192)
        t1T = at("t1T", [128, 8, 512], BF16, offset=RC)
        qx_t = at("qx_t", [128, D], BF16, offset=RC + 8192)
        zst = at("zst", [128, 512], BF16, offset=RC + 10240)
        bcbuf = at("bcbuf", [128, SEQ], F32, offset=RC)
        xprojT_sb = at("xprojT_sb", [128, 16, 96], BF16, offset=RC + 8192)
        y_f32 = at("y_f32", [128, SEQ], F32, offset=RC)
        dtb2 = at("dtb2", [128, SEQ], F32, offset=RB + 12288)
        zsil = at("zsil", [128, SEQ], BF16, offset=RC + 8192)
        brow = at("brow", [1, 2, SEQ], BF16, offset=RC + 12288)
        dtlowT = at("dtlowT", [64, SEQ], BF16, offset=RC + 20480)

        x_sb = sba("x_sb", [128, D])
        f32a = sba("f32a", [128, D])
        f32b = sba("f32b", [128, D])
        f32c = sba("f32c", [128, D])
        f32d = sba("f32d", [128, 512])
        st = sba("st", [128, 16])
        qpart = sba("qpart", [128, 4])
        sclpp = sba("sclpp", [128, 4])
        qloc = sba("qloc", [4, 1])
        scl = sba("scl", [1, 16])
        gval = sba("gval", [1, 128])
        cc_sb = sba("cc_sb", [128, 128])
        A_sb = sba("A_sb", [128, 256])
        Dcol = sba("Dcol", [128, 16])
        convw = sba("convw", [128, 64])
        convb = sba("convb", [128, 16])
        dtpb = sba("dtpb", [128, 16])
        b1pp = sba("b1pp", [128, 16])
        b2pp = sba("b2pp", [128, 8])
        lnbc = sba("lnbc", [128, 4, D], BF16)
        dtprojT_sb = sba("dtprojT_sb", [64, INTER], BF16)
        id_f_sb = sba("id_f_sb", [128, 128], F32)
        id_bf_sb = sba("id_bf_sb", [128, 128], BF16)
        ones_f_sb = sba("ones_f_sb", [1, 128], F32)
        ones_bf_sb = sba("ones_bf_sb", [1, 128], BF16)
        diag = sba("diag", [128, 4, 128], BF16)

        def psy_bank(j):
            if j < 4:
                return pb_big[:, j * 512:(j + 1) * 512]
            if j < 6:
                return pb_mm[:, j - 4, :]
            return pb_at[:, j - 6, :]

        def pre_sync(sy):
            for dst, srcap in ((A_sb[:], A_d[:]), (Dcol[:], Dcol_d[:]),
                               (convw[:], convw_d[:]), (convb[:], convb_d[:]),
                               (dtpb[:], dtpb_d[:]), (b1pp[:], b1pp_d[:]),
                               (b2pp[:], b2pp_d[:]), (id_f_sb[:], id_f[:]),
                               (id_bf_sb[:], id_bf[:]), (ones_f_sb[:], ones_f[:]),
                               (ones_bf_sb[:], ones_bfd[:])):
                i = sy.dma_start(out=dst, in_=srcap)
                c.inc(i, "syc", 16)
        blk.sync(pre_sync)

        def pre_gp(g):
            for dst, srcap in ((lnbc[:], lnbc_d.rearrange("p (l j) -> p l j", l=4)),
                               (dtprojT_sb[:], dtprojT_d[:])):
                i = g.dma_start(out=dst, in_=srcap)
                c.inc(i, "sgw", 16)
        blk.gpsimd(pre_gp)
        syc_v = c.val("syc")
        sgw_v = c.val("sgw")

        def pre_s(sc):
            sc = c.chain(sc, "se")
            sc.wait_ge(c.sem("syc"), syc_v)
            sc.activation(out=A_sb[:], in_=A_sb[:], func=AF.Exp)
        blk.scalar(pre_s)
        se_A = c.val("se")

        def pre_v(v, se_A=se_A):
            v = c.chain(v, "ve")
            v.wait_ge(c.sem("se"), se_A)
            v.wait_ge(c.sem("sgw"), sgw_v)
            v.tensor_scalar(out=A_sb[:], in0=A_sb[:], scalar1=-1.0, scalar2=None,
                            op0=ALU.mult)
            v.memset(qpart[:], 0.0)
            v.memset(cc_sb[:], 0.0)
        blk.vector(pre_v)

        c.barrier(blk)

        ln_st1_ve = [0]

        def emit_stats(src_ap, wv):
            def vst(v, wl=tuple(wv)):
                v = c.chain(v, "ve")
                for s_, val in wl:
                    v.wait_ge(c.sem(s_), val)
                v.reduce_sum(st[:, 0:1], src_ap, axis=AX.X)
            blk.vector(vst)

            def sst(sc, wl=tuple(wv), prev=ln_st1_ve[0]):
                sc = c.chain(sc, "se")
                for s_, val in wl:
                    sc.wait_ge(c.sem(s_), val)
                if prev:
                    sc.wait_ge(c.sem("ve"), prev)
                sc.activation(out=f32a[:], in_=src_ap, func=AF.Square,
                              accum_out=st[:, 1:2])
            blk.scalar(sst)
            se_sq = c.val("se")

            def vmt(v, se_sq=se_sq):
                v = c.chain(v, "ve")
                v.wait_ge(c.sem("se"), se_sq)
                v.tensor_scalar(out=st[:, 2:3], in0=st[:, 0:1], scalar1=1.0 / D,
                                scalar2=None, op0=ALU.mult)
                v.tensor_scalar(out=st[:, 3:4], in0=st[:, 1:2], scalar1=1.0 / D,
                                scalar2=None, op0=ALU.mult)
                v.tensor_tensor(out=st[:, 4:5], in0=st[:, 2:3], in1=st[:, 2:3],
                                op=ALU.mult)
                v.tensor_tensor(out=st[:, 5:6], in0=st[:, 3:4], in1=st[:, 4:5],
                                op=ALU.subtract)
                v.tensor_scalar(out=st[:, 5:6], in0=st[:, 5:6], scalar1=EPS,
                                scalar2=None, op0=ALU.add)
            blk.vector(vmt)
            ln_st1_ve[0] = c.val("ve")
            ve_var = c.val("ve")

            def srs(sc, ve_var=ve_var):
                sc = c.chain(sc, "se")
                sc.wait_ge(c.sem("ve"), ve_var)
                sc.activation(out=st[:, 6:7], in_=st[:, 5:6], func=AF.Ln)
                sc.activation(out=st[:, 7:8], in_=st[:, 6:7], func=AF.Exp,
                              scale=-0.5)
            blk.scalar(srs)
            return c.val("se")

        for tcg in range(8):
            tok0 = tcg * 512
            q_idx = tcg % 4
            for ti in range(4):
                row0 = tok0 + ti * 128

                def ld(sy, row0=row0, xv=c.val("ve")):
                    if xv:
                        sy.wait_ge(c.sem("ve"), xv)
                    i = sy.dma_start(out=x_sb[:], in_=x_in[row0:row0 + 128, :])
                    c.inc(i, "sx", 16)
                blk.sync(ld)
                sy_x = c.val("sx")

                se_r = emit_stats(x_sb[:], [("sx", sy_x)])

                def vu0(v, ti=ti, se_r=se_r):
                    v = c.chain(v, "ve")
                    v.wait_ge(c.sem("se"), se_r)
                    v.tensor_scalar(out=u0_tc.ap()[:, ti, :], in0=x_sb[:],
                                    scalar1=st[:, 2:3], scalar2=st[:, 7:8],
                                    op0=ALU.subtract, op1=ALU.mult)
                blk.vector(vu0)
                ve_u0 = c.val("ve")

                se_r2 = emit_stats(u0_tc.ap()[:, ti, :], [("ve", ve_u0)])

                def vt1(v, ti=ti, se_r2=se_r2, tpg=c.val("pe")):
                    v = c.chain(v, "ve")
                    v.wait_ge(c.sem("se"), se_r2)
                    if tpg:
                        v.wait_ge(c.sem("pe"), tpg)
                    v.tensor_scalar(out=f32b[:], in0=u0_tc.ap()[:, ti, :],
                                    scalar1=st[:, 2:3], scalar2=st[:, 7:8],
                                    op0=ALU.subtract, op1=ALU.mult)
                    v.tensor_tensor(out=f32b[:], in0=f32b[:], in1=lnbc[:, 0, :],
                                    op=ALU.mult)
                    v.tensor_tensor(out=f32b[:], in0=f32b[:], in1=lnbc[:, 1, :],
                                    op=ALU.add)
                blk.vector(vt1)
                ve_t1 = c.val("ve")

                def ttr(t, ve_t1=ve_t1, veg=c.val("ve")):
                    t.wait_ge(c.sem("ve"), ve_t1)
                    for g in range(8):
                        i = t.transpose(pb_big[:, g * 128:(g + 1) * 128],
                                        f32b[:, g * 128:(g + 1) * 128], id_f_sb[:])
                        c.inc(i, "pe", 1)
                blk.tensor(ttr)
                pe_tr = c.val("pe")

                def vcp(v, pe_tr=pe_tr, ti=ti):
                    v = c.chain(v, "ve")
                    v.wait_ge(c.sem("pe"), pe_tr)
                    v.tensor_copy(out=t1T.ap()[:, :, ti * 128:(ti + 1) * 128],
                                  in_=pb_big[:, 0:1024].rearrange(
                                      "p (g j) -> p g j", g=8))
                blk.vector(vcp)
            ve_t1T = c.val("ve")

            sgn_se = {0: 0, 1: 0}
            for ib in range(16):
                pp = ib % 2

                def lw1(g, ib=ib, pp=pp, wg=sgn_se[pp]):
                    if wg:
                        g.wait_ge(c.sem("se"), wg)
                    i = g.dma_start(out=wf32.ap()[:, pp, :, :],
                                    in_=w1T.rearrange("(cc p) i -> p cc i", p=128)
                                    [:, :, ib * 128:(ib + 1) * 128])
                    c.inc(i, f"sw{pp}", 16)
                blk.gpsimd(lw1)
                sw_v = c.val(f"sw{pp}")

                def ssg(sc, ib=ib, pp=pp, sw_v=sw_v, mmg=c.val("pe")):
                    sc = c.chain(sc, "se")
                    sc.wait_ge(c.sem(f"sw{pp}"), sw_v)
                    if mmg:
                        sc.wait_ge(c.sem("pe"), mmg)
                    sc.activation(out=ws_bf.ap()[:, pp, :, :],
                                  in_=wf32.ap()[:, pp, :, :], func=AF.Sign)
                blk.scalar(ssg)
                sgn_se[pp] = c.val("se")
                se_sg = c.val("se")

                def tg(t, ib=ib, pp=pp, se_sg=se_sg, ve_t1T=ve_t1T,
                       evg=c.val("ve")):
                    t.wait_ge(c.sem("se"), se_sg)
                    t.wait_ge(c.sem("ve"), ve_t1T)
                    if evg:
                        t.wait_ge(c.sem("ve"), evg)
                    for g in range(8):
                        i = t.matmul(pb_mm[:, pp, :], ws_bf.ap()[:, pp, g, :],
                                     t1T.ap()[:, g, :], start=(g == 0),
                                     stop=(g == 7))
                        c.inc(i, "pe", 1)
                blk.tensor(tg)
                pe_g = c.val("pe")

                def sgb(sc, ib=ib, pp=pp, pe_g=pe_g):
                    sc = c.chain(sc, "se")
                    sc.wait_ge(c.sem("pe"), pe_g)
                    sc.activation(out=f32d[:], in_=pb_mm[:, pp, :], func=AF.Identity,
                                  bias=b1pp[:, ib:ib + 1])
                blk.scalar(sgb)
                se_gb = c.val("se")

                def vgc(v, ib=ib, se_gb=se_gb):
                    v = c.chain(v, "ve")
                    v.wait_ge(c.sem("se"), se_gb)
                    v.tensor_scalar(out=gatedT.ap()[:, ib, :], in0=f32d[:],
                                    scalar1=1.0, scalar2=-1.0, op0=ALU.min,
                                    op1=ALU.max)
                blk.vector(vgc)
            ve_gated = c.val("ve")

            s2_se = {0: 0, 1: 0}
            for ic in range(16):
                pp = ic % 2

                def lw2(g, ic=ic, pp=pp, wg=s2_se[pp]):
                    if wg:
                        g.wait_ge(c.sem("se"), wg)
                    i = g.dma_start(out=w2f.ap()[:, pp, :],
                                    in_=w2T[ic * 128:(ic + 1) * 128, :])
                    c.inc(i, f"sv{pp}", 16)
                blk.gpsimd(lw2)
                sv_v = c.val(f"sv{pp}")

                def ss2(sc, ic=ic, pp=pp, sv_v=sv_v, mmg=c.val("pe")):
                    sc = c.chain(sc, "se")
                    sc.wait_ge(c.sem(f"sv{pp}"), sv_v)
                    if mmg:
                        sc.wait_ge(c.sem("pe"), mmg)
                    sc.activation(out=w2s.ap()[:, pp, :], in_=w2f.ap()[:, pp, :],
                                  func=AF.Sign)
                blk.scalar(ss2)
                s2_se[pp] = c.val("se")
                se_s2 = c.val("se")

                def tu2(t, ic=ic, pp=pp, se_s2=se_s2, ve_gated=ve_gated,
                        veg=c.val("ve"), seg=c.val("se")):
                    t.wait_ge(c.sem("se"), se_s2)
                    t.wait_ge(c.sem("ve"), ve_gated)
                    if ic == 0:
                        t.wait_ge(c.sem("ve"), veg)
                        t.wait_ge(c.sem("se"), seg)
                    for jb in range(8):
                        i = t.matmul(psy_bank(jb), w2s.ap()[:, pp, jb * 128:(jb + 1) * 128],
                                     gatedT.ap()[:, ic, :],
                                     start=(ic == 0), stop=(ic == 15))
                        c.inc(i, "pe", 1)
                blk.tensor(tu2)
            pe_u2 = c.val("pe")

            for jb in range(8):
                def su2(sc, jb=jb, pe_u2=pe_u2):
                    sc = c.chain(sc, "se")
                    sc.wait_ge(c.sem("pe"), pe_u2)
                    sc.activation(out=u2f.ap()[:, jb, :], in_=psy_bank(jb),
                                  func=AF.Identity, bias=b2pp[:, jb:jb + 1])
                blk.scalar(su2)
            se_u2 = c.val("se")

            for ti in range(4):
                row0 = tok0 + ti * 128

                def tvt(t, ti=ti, se_u2=se_u2, veg=c.val("ve")):
                    t.wait_ge(c.sem("se"), se_u2)
                    if veg:
                        t.wait_ge(c.sem("ve"), veg)
                    for g in range(8):
                        i = t.transpose(pb_big[:, g * 128:(g + 1) * 128],
                                        u2f.ap()[:, g, ti * 128:(ti + 1) * 128],
                                        id_f_sb[:])
                        c.inc(i, "pe", 1)
                blk.tensor(tvt)
                pe_vt = c.val("pe")

                def vvt(v, ti=ti, q_idx=q_idx, pe_vt=pe_vt, svv=c.val("svd")
                        if "svd" in c.sems else 0):
                    v = c.chain(v, "ve")
                    v.wait_ge(c.sem("pe"), pe_vt)
                    if svv:
                        v.wait_ge(c.sem("svd"), svv)
                    v.tensor_tensor(out=f32c[:], in0=pb_big[:, 0:1024],
                                    in1=u0_tc.ap()[:, ti, :], op=ALU.add)
                    v.reduce_max(st[:, 12:13], f32c[:], axis=AX.X,
                                 apply_absolute_value=True)
                    v.tensor_tensor(out=qpart[:, q_idx:q_idx + 1],
                                    in0=qpart[:, q_idx:q_idx + 1],
                                    in1=st[:, 12:13], op=ALU.max)
                blk.vector(vvt)
                ve_vt = c.val("ve")

                def stv(sy, row0=row0, ve_vt=ve_vt):
                    sy.wait_ge(c.sem("ve"), ve_vt)
                    i = sy.dma_start(out=v_dram[row0:row0 + 128, :], in_=f32c[:])
                    c.inc(i, "svd", 16)
                blk.sync(stv)

        c.barrier(blk)

        def tqp(t):
            i = t.transpose(pb_mm[0:4, 0, 0:128], qpart[:], id_f_sb[:])
            c.inc(i, "pe", 1)
        blk.tensor(tqp)
        pe_qp = c.val("pe")

        def vqp(v, pe_qp=pe_qp):
            v = c.chain(v, "ve")
            v.wait_ge(c.sem("pe"), pe_qp)
            v.reduce_max(qloc[:], pb_mm[0:4, 0, 0:128], axis=AX.X)
        blk.vector(vqp)
        ve_qp = c.val("ve")

        def syq(sy, ve_qp=ve_qp):
            sy.wait_ge(c.sem("ve"), ve_qp)
            i = sy.dma_start(out=gq_dram[:, :], in_=qloc[:])
            c.inc(i, "sq1", 16)
            sy.wait_ge(c.sem("sq1"), c.val("sq1"))
            i = sy.dma_start(out=gval[0:1, 0:4],
                             in_=gq_dram.rearrange("a e -> (a e)")[None, :])
            c.inc(i, "sq2", 16)
        blk.sync(syq)
        sq2_v = c.val("sq2")

        def vcc(v, sq2_v=sq2_v):
            v = c.chain(v, "ve")
            v.wait_ge(c.sem("sq2"), sq2_v)
            v.tensor_copy(out=cc_sb[0:1, 0:4], in_=gval[0:1, 0:4])
        blk.vector(vcc)
        ve_cc = c.val("ve")

        def gcc(g, ve_cc=ve_cc):
            g.wait_ge(c.sem("ve"), ve_cc)
            i = g.dma_start(out=cc_in_d[:, :], in_=cc_sb[:])
            c.inc(i, "sc1", 16)
            g.wait_ge(c.sem("sc1"), c.val("sc1"))
            i = g.collective_compute(
                "AllReduce", ALU.max, replica_groups=[list(range(8))],
                ins=[cc_in_h.ap().opt()], outs=[cc_out_h.ap().opt()])
            c.inc(i, "sc2", 1)
            g.wait_ge(c.sem("sc2"), c.val("sc2"))
            i = g.dma_start(out=gval[:, :], in_=cc_out_d[0:1, :])
            c.inc(i, "sc3", 16)
        blk.gpsimd(gcc)
        sc3_v = c.val("sc3")

        def vscl(v, sc3_v=sc3_v):
            v = c.chain(v, "ve")
            v.wait_ge(c.sem("sc3"), sc3_v)
            v.tensor_scalar(out=scl[0:1, 0:4], in0=gval[0:1, 0:4], scalar1=EPS,
                            scalar2=None, op0=ALU.add)
            v.reciprocal(scl[0:1, 4:8], scl[0:1, 0:4])
            v.tensor_scalar(out=scl[0:1, 0:4], in0=scl[0:1, 4:8], scalar1=128.0,
                            scalar2=None, op0=ALU.mult)
        blk.vector(vscl)
        ve_scl = c.val("ve")

        for q in range(4):
            def tsc(t, q=q, ve_scl=ve_scl):
                t.wait_ge(c.sem("ve"), ve_scl)
                i = t.matmul(pb_mm[:, 0, q:q + 1], ones_f_sb[:],
                             scl[0:1, q:q + 1], start=True, stop=True)
                c.inc(i, "pe", 1)
            blk.tensor(tsc)
        pe_scl = c.val("pe")

        def vsp(v, pe_scl=pe_scl):
            v = c.chain(v, "ve")
            v.wait_ge(c.sem("pe"), pe_scl)
            v.tensor_copy(out=sclpp[:], in_=pb_mm[:, 0, 0:4])
        blk.vector(vsp)

        c.barrier(blk)

        for m in range(NTILE):
            row0 = m * 128
            q_idx = (m % 16) // 4

            def ldv(sy, row0=row0, vg=c.val("ve")):
                if vg:
                    sy.wait_ge(c.sem("ve"), vg)
                i = sy.dma_start(out=x_sb[:], in_=v_dram[row0:row0 + 128, :])
                c.inc(i, "sx", 16)
            blk.sync(ldv)
            sy_v = c.val("sx")

            def vqx(v, q_idx=q_idx, sy_v=sy_v, qg=c.val("sqx") if "sqx" in c.sems
                    else 0):
                v = c.chain(v, "ve")
                v.wait_ge(c.sem("sx"), sy_v)
                if qg:
                    v.wait_ge(c.sem("sqx"), qg)
                v.tensor_scalar(out=f32c[:], in0=x_sb[:],
                                scalar1=sclpp[:, q_idx:q_idx + 1], scalar2=QCLIP,
                                op0=ALU.mult, op1=ALU.min)
                v.tensor_scalar(out=qx_t.ap()[:], in0=f32c[:], scalar1=-QCLIP,
                                scalar2=None, op0=ALU.max)
            blk.vector(vqx)
            ve_qx = c.val("ve")

            def stq(sy, row0=row0, ve_qx=ve_qx):
                sy.wait_ge(c.sem("ve"), ve_qx)
                i = sy.dma_start(out=qx_dram[row0:row0 + 128, :], in_=qx_t.ap()[:])
                c.inc(i, "sqx", 16)
            blk.sync(stq)

        c.barrier(blk)

        for b in range(B_PER_CORE):
            base_row = b * SEQ

            def vpad(v):
                v = c.chain(v, "ve")
                v.memset(convbuf.ap()[:, :, 0:3], 0.0)
            blk.vector(vpad)

            for tc in range(4):
                tok0 = base_row + tc * 512
                for ti in range(4):
                    row0 = tok0 + ti * 128

                    def ldq(sy, row0=row0, vg=c.val("ve")):
                        if vg:
                            sy.wait_ge(c.sem("ve"), vg)
                        i = sy.dma_start(out=qx_t.ap()[:],
                                         in_=qx_dram[row0:row0 + 128, :])
                        c.inc(i, "sx", 16)
                    blk.sync(ldq)
                    sy_q = c.val("sx")

                    se_r = emit_stats(qx_t.ap()[:], [("sx", sy_q)])

                    def vln2(v, se_r=se_r, tpg=c.val("pe")):
                        v = c.chain(v, "ve")
                        v.wait_ge(c.sem("se"), se_r)
                        if tpg:
                            v.wait_ge(c.sem("pe"), tpg)
                        v.tensor_scalar(out=f32b[:], in0=qx_t.ap()[:],
                                        scalar1=st[:, 2:3], scalar2=st[:, 7:8],
                                        op0=ALU.subtract, op1=ALU.mult)
                        v.tensor_tensor(out=f32b[:], in0=f32b[:],
                                        in1=lnbc[:, 2, :], op=ALU.mult)
                        v.tensor_tensor(out=f32b[:], in0=f32b[:],
                                        in1=lnbc[:, 3, :], op=ALU.add)
                    blk.vector(vln2)
                    ve_l2 = c.val("ve")

                    def ttru(t, ve_l2=ve_l2, ti=ti):
                        t.wait_ge(c.sem("ve"), ve_l2)
                        for g in range(8):
                            i = t.transpose(pb_big[:, g * 128:(g + 1) * 128],
                                            f32b[:, g * 128:(g + 1) * 128],
                                            id_f_sb[:])
                            c.inc(i, "pe", 1)
                    blk.tensor(ttru)
                    pe_tru = c.val("pe")

                    def vcpu(v, pe_tru=pe_tru, ti=ti):
                        v = c.chain(v, "ve")
                        v.wait_ge(c.sem("pe"), pe_tru)
                        v.tensor_copy(out=t1T.ap()[:, :, ti * 128:(ti + 1) * 128],
                                      in_=pb_big[:, 0:1024].rearrange(
                                          "p (g j) -> p g j", g=8))
                    blk.vector(vcpu)
                ve_uT = c.val("ve")

                for jbp in range(16):
                    pp = jbp % 2

                    def lwi(g, jbp=jbp, pp=pp, wg=c.val("pe")):
                        if wg:
                            g.wait_ge(c.sem("pe"), wg)
                        i = g.dma_start(
                            out=wk_bf.ap()[:, pp, :, :],
                            in_=inpT.rearrange("(cc p) j -> p cc j", p=128)
                            [:, :, jbp * 256:(jbp + 1) * 256])
                        c.inc(i, f"sw{pp}", 16)
                    blk.gpsimd(lwi)
                    swi_v = c.val(f"sw{pp}")

                    for jh in range(2):
                        jb = jbp * 2 + jh

                        def tip(t, jbp=jbp, jh=jh, pp=pp, swi_v=swi_v,
                                ve_uT=ve_uT, evg=c.val("se")):
                            t.wait_ge(c.sem(f"sw{pp}"), swi_v)
                            t.wait_ge(c.sem("ve"), ve_uT)
                            if evg:
                                t.wait_ge(c.sem("se"), evg)
                            for g in range(8):
                                i = t.matmul(pb_mm[:, jh, :],
                                             wk_bf.ap()[:, pp, g,
                                                        jh * 128:(jh + 1) * 128],
                                             t1T.ap()[:, g, :],
                                             start=(g == 0), stop=(g == 7))
                                c.inc(i, "pe", 1)
                        blk.tensor(tip)
                        pe_ip = c.val("pe")

                        if jb < 16:
                            def sxs(sc, jb=jb, jh=jh, tc=tc, pe_ip=pe_ip):
                                sc = c.chain(sc, "se")
                                sc.wait_ge(c.sem("pe"), pe_ip)
                                sc.activation(
                                    out=convbuf.ap()[:, jb,
                                                     3 + tc * 512:3 + (tc + 1) * 512],
                                    in_=pb_mm[:, jh, :], func=AF.Identity)
                            blk.scalar(sxs)
                        else:
                            def szz(sc, jh=jh, pe_ip=pe_ip, zg=c.val("szd")
                                    if "szd" in c.sems else 0):
                                sc = c.chain(sc, "se")
                                sc.wait_ge(c.sem("pe"), pe_ip)
                                if zg:
                                    sc.wait_ge(c.sem("szd"), zg)
                                sc.activation(out=zst.ap()[:], in_=pb_mm[:, jh, :],
                                              func=AF.Identity)
                            blk.scalar(szz)
                            se_z = c.val("se")

                            def stz(sy, jb=jb, tc=tc, b=b, se_z=se_z):
                                sy.wait_ge(c.sem("se"), se_z)
                                i = sy.dma_start(
                                    out=z_dram[b, jb - 16, :,
                                               tc * 512:(tc + 1) * 512],
                                    in_=zst.ap()[:])
                                c.inc(i, "szd", 16)
                            blk.sync(stz)

            c.barrier(blk)

            for cb in range(16):
                def vdg(v, cb=cb, dg=c.val("pe")):
                    v = c.chain(v, "ve")
                    if dg:
                        v.wait_ge(c.sem("pe"), dg)
                    for j in range(4):
                        v.tensor_scalar(out=diag[:, j, :], in0=id_bf_sb[:],
                                        scalar1=convw[:, cb * 4 + j:cb * 4 + j + 1],
                                        scalar2=None, op0=ALU.mult)
                blk.vector(vdg)
                ve_dg = c.val("ve")

                def tcv(t, cb=cb, ve_dg=ve_dg, eg=c.val("se")):
                    t.wait_ge(c.sem("ve"), ve_dg)
                    if eg:
                        t.wait_ge(c.sem("se"), eg)
                    for t4 in range(4):
                        for j in range(4):
                            i = t.matmul(pb_big[:, t4 * 512:(t4 + 1) * 512],
                                         diag[:, j, :],
                                         convbuf.ap()[:, cb,
                                                      t4 * 512 + j:t4 * 512 + j + 512],
                                         start=(j == 0), stop=(j == 3))
                            c.inc(i, "pe", 1)
                blk.tensor(tcv)
                pe_cv = c.val("pe")

                def scv(sc, cb=cb, pe_cv=pe_cv):
                    sc = c.chain(sc, "se")
                    sc.wait_ge(c.sem("pe"), pe_cv)
                    sc.activation(out=convbuf.ap()[:, cb, 3:3 + SEQ],
                                  in_=pb_big[:, :], func=AF.Silu,
                                  bias=convb[:, cb:cb + 1])
                blk.scalar(scv)

            c.barrier(blk)

            def lxp(g):
                i = g.dma_start(out=xprojT_sb[:],
                                in_=xprojT_d.rearrange("(cc p) j -> p cc j", p=128))
                c.inc(i, "sxp", 16)
            blk.gpsimd(lxp)
            sxp_v = c.val("sxp")

            for tc in range(4):
                def txp(t, tc=tc, eg=c.val("ve"), sxp_v=sxp_v):
                    t.wait_ge(c.sem("sxp"), sxp_v)
                    if tc == 0 and eg:
                        t.wait_ge(c.sem("ve"), eg)
                    for cb in range(16):
                        i = t.matmul(pb_big[0:96, tc * 512:(tc + 1) * 512],
                                     xprojT_sb[:, cb, :],
                                     convbuf.ap()[:, cb,
                                                  3 + tc * 512:3 + (tc + 1) * 512],
                                     start=(cb == 0), stop=(cb == 15))
                        c.inc(i, "pe", 1)
                blk.tensor(txp)
                pe_xp = c.val("pe")

                def vxp(v, tc=tc, pe_xp=pe_xp):
                    v = c.chain(v, "ve")
                    v.wait_ge(c.sem("pe"), pe_xp)
                    v.tensor_copy(out=dtlowT.ap()[:, tc * 512:(tc + 1) * 512],
                                  in_=pb_big[0:64, tc * 512:(tc + 1) * 512])
                    v.tensor_copy(out=bcbuf.ap()[64:96, tc * 512:(tc + 1) * 512],
                                  in_=pb_big[64:96, tc * 512:(tc + 1) * 512])
                blk.vector(vxp)
            ve_xp = c.val("ve")

            def stbc(sy, b=b, ve_xp=ve_xp):
                sy.wait_ge(c.sem("ve"), ve_xp)
                i = sy.dma_start(out=bc_dram[:, 0, :], in_=bcbuf.ap()[64:80, :])
                c.inc(i, "sbc", 16)
                i = sy.dma_start(out=bc_dram[:, 1, :], in_=bcbuf.ap()[80:96, :])
                c.inc(i, "sbc", 16)
                sy.wait_ge(c.sem("sbc"), c.val("sbc"))
            blk.sync(stbc)

            c.barrier(blk)

            for dblk in range(16):
                c.barrier(blk, engines=("tensor", "vector", "scalar", "gpsimd",
                                        "sync"))
                def tdt(t, dblk=dblk):
                    for tc in range(4):
                        i = t.matmul(pb_big[:, tc * 512:(tc + 1) * 512],
                                     dtprojT_sb[:, dblk * 128:(dblk + 1) * 128],
                                     dtlowT.ap()[:, tc * 512:(tc + 1) * 512],
                                     start=True, stop=True)
                        c.inc(i, "pe", 1)
                blk.tensor(tdt)
                pe_dt = c.val("pe")

                def sdt(sc, dblk=dblk, pe_dt=pe_dt):
                    sc = c.chain(sc, "se")
                    sc.wait_ge(c.sem("pe"), pe_dt)
                    sc.activation(out=dt_f32.ap()[:], in_=pb_big[:, :],
                                  func=AF.Identity, bias=dtpb[:, dblk:dblk + 1])
                    sc.activation(out=y_f32.ap()[:], in_=dt_f32.ap()[:], func=AF.Abs)
                    sc.activation(out=y_f32.ap()[:], in_=y_f32.ap()[:], func=AF.Exp,
                                  scale=-1.0)
                    sc.activation(out=y_f32.ap()[:], in_=y_f32.ap()[:], func=AF.Ln,
                                  bias=1.0)
                    sc.activation(out=dtb2.ap()[:], in_=dt_f32.ap()[:], func=AF.Relu)
                blk.scalar(sdt)
                se_dt = c.val("se")

                def vdt(v, dblk=dblk, se_dt=se_dt):
                    v = c.chain(v, "ve")
                    v.wait_ge(c.sem("se"), se_dt)
                    v.tensor_tensor(out=dt_f32.ap()[:], in0=y_f32.ap()[:],
                                    in1=dtb2.ap()[:], op=ALU.add)
                    v.tensor_tensor(out=dtx_bf.ap()[:], in0=dt_f32.ap()[:],
                                    in1=convbuf.ap()[:, dblk, 3:3 + SEQ],
                                    op=ALU.mult)
                    v.tensor_scalar(out=y_f32.ap()[:],
                                    in0=convbuf.ap()[:, dblk, 3:3 + SEQ],
                                    scalar1=Dcol[:, dblk:dblk + 1], scalar2=None,
                                    op0=ALU.mult)
                blk.vector(vdt)
                ve_dt = c.val("ve")

                for s in range(16):
                    def ldr(g, s=s, bg=c.val("pe")):
                        if bg:
                            g.wait_ge(c.sem("pe"), bg)
                        i = g.dma_start(out=brow.ap()[:, :, :],
                                        in_=bc_dram[s, :, :][None, :, :])
                        c.inc(i, "sbr", 16)
                    blk.gpsimd(ldr)
                    sbr_v = c.val("sbr")

                    def tbc(t, s=s, sbr_v=sbr_v, vg=c.val("ve")):
                        t.wait_ge(c.sem("sbr"), sbr_v)
                        if vg:
                            t.wait_ge(c.sem("ve"), vg)
                        for qc in range(4):
                            i = t.matmul(pb_big[:, qc * 512:(qc + 1) * 512],
                                         ones_bf_sb[:],
                                         brow.ap()[0:1, 0, qc * 512:(qc + 1) * 512],
                                         start=True, stop=True)
                            c.inc(i, "pe", 1)
                        for qc in range(4):
                            bank = pb_mm if qc < 2 else pb_at
                            i = t.matmul(bank[:, qc % 2, :], ones_bf_sb[:],
                                         brow.ap()[0:1, 1, qc * 512:(qc + 1) * 512],
                                         start=True, stop=True)
                            c.inc(i, "pe", 1)
                    blk.tensor(tbc)
                    pe_bc = c.val("pe")

                    def sa(sc, s=s, dblk=dblk, ve_dt=ve_dt, scg=c.val("ve")):
                        sc = c.chain(sc, "se")
                        sc.wait_ge(c.sem("ve"), ve_dt)
                        if scg:
                            sc.wait_ge(c.sem("ve"), scg)
                        sc.activation(out=a_bf.ap()[:], in_=dt_f32.ap()[:],
                                      func=AF.Exp,
                                      scale=A_sb[:, dblk * 16 + s:dblk * 16 + s + 1])
                    blk.scalar(sa)
                    se_a = c.val("se")

                    def vsc(v, s=s, pe_bc=pe_bc, se_a=se_a):
                        v = c.chain(v, "ve")
                        v.wait_ge(c.sem("pe"), pe_bc)
                        v.wait_ge(c.sem("se"), se_a)
                        v.tensor_tensor(out=h_bf.ap()[:], in0=dtx_bf.ap()[:],
                                        in1=pb_big[:, :], op=ALU.mult)
                        v.tensor_tensor_scan(out=h_bf.ap()[:], data0=a_bf.ap()[:],
                                             data1=h_bf.ap()[:], initial=0.0,
                                             op0=ALU.mult, op1=ALU.add)
                        for qc in range(4):
                            bank = pb_mm if qc < 2 else pb_at
                            v.tensor_tensor(out=a_bf.ap()[:, qc * 512:(qc + 1) * 512],
                                            in0=h_bf.ap()[:, qc * 512:(qc + 1) * 512],
                                            in1=bank[:, qc % 2, :], op=ALU.mult)
                        v.tensor_tensor(out=y_f32.ap()[:], in0=y_f32.ap()[:],
                                        in1=a_bf.ap()[:], op=ALU.add)
                    blk.vector(vsc)

                def ldz(g, b=b, dblk=dblk, zg=c.val("se")):
                    if zg:
                        g.wait_ge(c.sem("se"), zg)
                    i = g.dma_start(out=zsil.ap()[:], in_=z_dram[b, dblk, :, :])
                    c.inc(i, "szl", 16)
                blk.gpsimd(ldz)
                szl_v = c.val("szl")

                def ssz(sc, szl_v=szl_v, vg=c.val("ve")):
                    sc = c.chain(sc, "se")
                    sc.wait_ge(c.sem("szl"), szl_v)
                    sc.wait_ge(c.sem("ve"), vg)
                    sc.activation(out=zsil.ap()[:], in_=zsil.ap()[:], func=AF.Silu)
                blk.scalar(ssz)
                se_sz = c.val("se")

                def vyz(v, dblk=dblk, se_sz=se_sz):
                    v = c.chain(v, "ve")
                    v.wait_ge(c.sem("se"), se_sz)
                    v.tensor_tensor(out=convbuf.ap()[:, dblk, 3:3 + SEQ],
                                    in0=y_f32.ap()[:], in1=zsil.ap()[:],
                                    op=ALU.mult)
                blk.vector(vyz)

            c.barrier(blk)

            ev_se = [0]
            for tc in range(4):
                tok0 = base_row + tc * 512
                for icg in range(8):
                    pp = icg % 2

                    def lwo(g, icg=icg, pp=pp, wg=c.val("pe")):
                        if wg:
                            g.wait_ge(c.sem("pe"), wg)
                        i = g.dma_start(
                            out=outw.ap()[:, pp, :, :],
                            in_=outT_d.rearrange("(cc p) j -> p cc j", p=128)
                            [:, icg * 2:(icg + 1) * 2, :])
                        c.inc(i, f"sw{pp}", 16)
                    blk.gpsimd(lwo)
                    swo_v = c.val(f"sw{pp}")

                    def top(t, tc=tc, icg=icg, pp=pp, swo_v=swo_v,
                            eg=ev_se[0], veg=c.val("ve")):
                        t.wait_ge(c.sem(f"sw{pp}"), swo_v)
                        if icg == 0:
                            if eg:
                                t.wait_ge(c.sem("se"), eg)
                            if veg:
                                t.wait_ge(c.sem("ve"), veg)
                        for ici in range(2):
                            ic = icg * 2 + ici
                            for jb in range(8):
                                i = t.matmul(
                                    psy_bank(jb),
                                    outw.ap()[:, pp, ici, jb * 128:(jb + 1) * 128],
                                    convbuf.ap()[:, ic,
                                                 3 + tc * 512:3 + (tc + 1) * 512],
                                    start=(ic == 0), stop=(ic == 15))
                                c.inc(i, "pe", 1)
                    blk.tensor(top)
                pe_op = c.val("pe")

                for jb in range(8):
                    def sop(sc, jb=jb, pe_op=pe_op):
                        sc = c.chain(sc, "se")
                        sc.wait_ge(c.sem("pe"), pe_op)
                        sc.activation(out=outf32.ap()[:, jb, :], in_=psy_bank(jb),
                                      func=AF.Identity)
                    blk.scalar(sop)
                ev_se[0] = c.val("se")
                se_op = c.val("se")

                for ti in range(4):
                    row0 = tok0 + ti * 128

                    def tto(t, ti=ti, se_op=se_op, veg=c.val("ve")):
                        t.wait_ge(c.sem("se"), se_op)
                        if veg:
                            t.wait_ge(c.sem("ve"), veg)
                        for g in range(8):
                            i = t.transpose(pb_big[:, g * 128:(g + 1) * 128],
                                            outf32.ap()[:, g, ti * 128:(ti + 1) * 128],
                                            id_f_sb[:])
                            c.inc(i, "pe", 1)
                    blk.tensor(tto)
                    pe_to = c.val("pe")

                    def ldq2(sy, row0=row0, vg=c.val("ve"),
                             og=c.val("sxo")):
                        if vg:
                            sy.wait_ge(c.sem("ve"), vg)
                        if og:
                            sy.wait_ge(c.sem("sxo"), og)
                        i = sy.dma_start(out=qx_t.ap()[:],
                                         in_=qx_dram[row0:row0 + 128, :])
                        c.inc(i, "sx", 16)
                        i = sy.dma_start(out=x_sb[:], in_=x_in[row0:row0 + 128, :])
                        c.inc(i, "sx", 16)
                    blk.sync(ldq2)
                    sy_f = c.val("sx")

                    def vfin(v, pe_to=pe_to, sy_f=sy_f,
                             og=c.val("sxo") if "sxo" in c.sems else 0):
                        v = c.chain(v, "ve")
                        v.wait_ge(c.sem("pe"), pe_to)
                        v.wait_ge(c.sem("sx"), sy_f)
                        if og:
                            v.wait_ge(c.sem("sxo"), og)
                        v.tensor_tensor(out=f32c[:], in0=pb_big[:, 0:1024],
                                        in1=qx_t.ap()[:], op=ALU.add)
                        v.tensor_tensor(out=x_sb[:], in0=f32c[:], in1=x_sb[:],
                                        op=ALU.add)
                    blk.vector(vfin)
                    ve_f = c.val("ve")

                    def sto(sy, row0=row0, ve_f=ve_f):
                        sy.wait_ge(c.sem("ve"), ve_f)
                        i = sy.dma_start(out=x_out[row0:row0 + 128, :], in_=x_sb[:])
                        c.inc(i, "sxo", 16)
                    blk.sync(sto)

        sxo_t = c.val("sxo")

        def fin(sy, sxo_t=sxo_t):
            sy.wait_ge(c.sem("sxo"), sxo_t)
        blk.sync(fin)


def _declare_l1attn_drams(nc, pfx=""):
    aps = dict(
        w1T=nc.dram_tensor(pfx + "w1T", [D, INTER], F32,
                           kind="ExternalInput").ap(),
        w2T=nc.dram_tensor(pfx + "w2T", [INTER, D], F32,
                           kind="ExternalInput").ap(),
        inpT=nc.dram_tensor(pfx + "inpT", [D, 2 * INTER], F32,
                            kind="ExternalInput").ap(),
        xprojT_d=nc.dram_tensor(pfx + "xprojT_d", [INTER, 96], F32,
                                kind="ExternalInput").ap(),
        dtprojT_d=nc.dram_tensor(pfx + "dtprojT_d", [64, INTER], F32,
                                 kind="ExternalInput").ap(),
        outT_d=nc.dram_tensor(pfx + "outT_d", [INTER, D], F32,
                              kind="ExternalInput").ap(),
        A_d=nc.dram_tensor(pfx + "A_d", [128, 256], F32,
                           kind="ExternalInput").ap(),
        Dcol_d=nc.dram_tensor(pfx + "Dcol_d", [128, 16], F32,
                              kind="ExternalInput").ap(),
        convw_d=nc.dram_tensor(pfx + "convw_d", [128, 64], F32,
                               kind="ExternalInput").ap(),
        convb_d=nc.dram_tensor(pfx + "convb_d", [128, 16], F32,
                               kind="ExternalInput").ap(),
        dtpb_d=nc.dram_tensor(pfx + "dtpb_d", [128, 16], F32,
                              kind="ExternalInput").ap(),
        b1pp_d=nc.dram_tensor(pfx + "b1pp_d", [128, 16], F32,
                              kind="ExternalInput").ap(),
        b2pp_d=nc.dram_tensor(pfx + "b2pp_d", [128, 8], F32,
                              kind="ExternalInput").ap(),
        lnbc_d=nc.dram_tensor(pfx + "lnbc_d", [128, 4 * D], F32,
                              kind="ExternalInput").ap(),
    )
    aps["v_dram"] = nc.dram_tensor(pfx + "v_dram", [TOK, D], F32,
                                   kind="Internal").ap()
    aps["qx_dram"] = nc.dram_tensor(pfx + "qx_dram", [TOK, D], BF16,
                                    kind="Internal").ap()
    aps["z_dram"] = nc.dram_tensor(pfx + "z_dram", [2, 16, 128, SEQ], BF16,
                                   kind="Internal").ap()
    aps["bc_dram"] = nc.dram_tensor(pfx + "bc_dram", [16, 2, SEQ], F32,
                                    kind="Internal").ap()
    aps["gq_dram"] = nc.dram_tensor(pfx + "gq_dram", [4, 1], F32,
                                    kind="Internal").ap()
    aps["cc_in_h"] = nc.dram_tensor(pfx + "cc_in_h", [128, 128], F32,
                                    kind="Internal")
    aps["cc_out_h"] = nc.dram_tensor(pfx + "cc_out_h", [128, 128], F32,
                                     kind="Internal")
    return aps


def build_l1attn():
    nc = bass.Bass("TRN2", target_bir_lowering=False, debug=False)
    aps = _declare_l1attn_drams(nc)
    aps.update(_declare_shared_consts(nc))
    aps["x_in"] = nc.dram_tensor("x_in", [TOK, D], F32, kind="ExternalInput").ap()
    aps["x_out"] = nc.dram_tensor("x_out", [TOK, D], F32,
                                  kind="ExternalOutput").ap()
    with contextlib.ExitStack() as es:
        c = Ctx(nc, es)
        PB = es.enter_context(nc.psum_tensor("PB", [128, 2048], F32))
        PM = es.enter_context(nc.psum_tensor("PM", [128, 2, 512], F32))
        PA = es.enter_context(nc.psum_tensor("PA", [128, 2, 512], F32))
        sbase, _ = nc.bump_sbuf(nc.sbuf_bytes_remaining - 64)
        blk = c.wrap_block(es.enter_context(nc.Block()))
        emit_l1attn(nc, c, es, blk, aps, "m_", sbase, PB, PM, PA)
    return nc


def _l1attn_maps(inputs):
    W1 = np.asarray(inputs["L1_attn_W1"], np.float32)
    W2 = np.asarray(inputs["L1_attn_W2"], np.float32)
    inp = np.asarray(inputs["L1_attn_in_proj_w"], np.float32)
    xproj = np.asarray(inputs["L1_attn_x_proj_w"], np.float32)
    dtproj = np.asarray(inputs["L1_attn_dt_proj_w"], np.float32)
    outp = np.asarray(inputs["L1_attn_out_proj_w"], np.float32)
    A_log = np.asarray(inputs["L1_attn_A_log"], np.float32)
    Dp = np.asarray(inputs["L1_attn_D_param"], np.float32)
    convw = np.asarray(inputs["L1_attn_conv_w"], np.float32)
    convb = np.asarray(inputs["L1_attn_conv_b"], np.float32)
    dtpb = np.asarray(inputs["L1_attn_dt_proj_b"], np.float32)
    b1 = np.asarray(inputs["L1_attn_b1"], np.float32)
    b2 = np.asarray(inputs["L1_attn_b2"], np.float32)
    lnq = np.stack([np.asarray(inputs["L1_attn_ln1_g"], np.float32),
                    np.asarray(inputs["L1_attn_ln1_b"], np.float32),
                    np.asarray(inputs["L1_attn_ln2_g"], np.float32),
                    np.asarray(inputs["L1_attn_ln2_b"], np.float32)])

    def pp(vec, nb):
        return np.ascontiguousarray(vec.reshape(nb, 128).T)

    return dict(
        w1T=np.ascontiguousarray(W1.T),
        w2T=np.ascontiguousarray(W2.T),
        inpT=np.ascontiguousarray(inp.T),
        xprojT_d=np.ascontiguousarray(xproj.T),
        dtprojT_d=np.ascontiguousarray(dtproj.T),
        outT_d=np.ascontiguousarray(outp.T),
        A_d=np.ascontiguousarray(
            A_log.reshape(16, 128, 16).transpose(1, 0, 2).reshape(128, 256)),
        Dcol_d=pp(Dp, 16),
        convw_d=np.ascontiguousarray(
            convw.reshape(16, 128, 4).transpose(1, 0, 2).reshape(128, 64)),
        convb_d=pp(convb, 16),
        dtpb_d=pp(dtpb, 16),
        b1pp_d=pp(b1, 16),
        b2pp_d=pp(b2, 8),
        lnbc_d=np.ascontiguousarray(
            np.broadcast_to(lnq.reshape(1, 4 * D), (128, 4 * D))),
    )


def run_l1attn(x_full, inputs, nc=None):
    if nc is None:
        nc = build_l1attn()
    base = _l1attn_maps(inputs)
    base.update(_shared_const_values())
    xr = np.asarray(x_full, np.float32).reshape(N_CORES, TOK, D)
    in_maps = [dict(base, x_in=np.ascontiguousarray(xr[i])) for i in range(N_CORES)]
    res = run_bass_kernel_spmd(nc, in_maps, core_ids=list(range(N_CORES)))
    out = np.stack([res.results[i]["x_out"] for i in range(N_CORES)])
    return out.reshape(16, 2048, 1024)



def _l0ffn_maps(inputs):
    w1 = inputs["L0_ffn_w1"]
    w2 = inputs["L0_ffn_w2"]
    w3 = inputs["L0_ffn_w3"]
    sw1 = inputs["L0_ffn_sw1"]
    sw2 = inputs["L0_ffn_sw2"]
    sw3 = inputs["L0_ffn_sw3"]
    wA = np.zeros((9, D, 2 * INTER), np.float32)
    w2a = np.zeros((9, INTER, D), np.float32)
    for e in range(8):
        wA[e, :, :INTER] = np.asarray(w1[e]).T
        wA[e, :, INTER:] = np.asarray(w3[e]).T
        w2a[e] = np.asarray(w2[e]).T
    wA[8, :, :INTER] = np.asarray(sw1).T
    wA[8, :, INTER:] = np.asarray(sw3).T
    w2a[8] = np.asarray(sw2).T
    base = dict(
        wA=np.ascontiguousarray(wA),
        w2=np.ascontiguousarray(w2a),
        gw=np.ascontiguousarray(np.asarray(inputs["L0_ffn_gate_w"]).T),
        gbias=np.asarray(inputs["L0_ffn_gate_bias"]).reshape(1, 8).astype(np.float32),
    )
    return base


def run_l0ffn(x_full, inputs, nc=None):
    if nc is None:
        nc = build_l0ffn()
    base = _l0ffn_maps(inputs)
    base.update(_shared_const_values())
    xr = np.asarray(x_full, np.float32).reshape(N_CORES, TOK, D)
    in_maps = [dict(base, x_in=np.ascontiguousarray(xr[i])) for i in range(N_CORES)]
    res = run_bass_kernel_spmd(nc, in_maps, core_ids=list(range(N_CORES)))
    out = np.stack([res.results[i]["x_out"] for i in range(N_CORES)])
    return out.reshape(16, 2048, 1024)



def build_fused():
    nc = bass.Bass("TRN2", target_bir_lowering=False, debug=False)
    shared = _declare_shared_consts(nc)
    a0f = _declare_l0ffn_drams(nc, "l0f_")
    a0a = _declare_l0attn_drams(nc, "l0a_")
    a1f = _declare_l1ffn_drams(nc, "l1f_")
    a1a = _declare_l1attn_drams(nc, "l1a_")
    for d in (a0f, a0a, a1f, a1a):
        d.update(shared)
    combT = nc.dram_tensor("combT_dram", [8, TOK], F32, kind="Internal").ap()
    m_dram = nc.dram_tensor("m_dram", [16, 128], F32, kind="Internal").ap()
    a0f["combT_dram"] = combT
    a1f["combT_dram"] = combT
    a1f["m_dram"] = m_dram

    x0 = nc.dram_tensor("x_in", [TOK, D], F32, kind="ExternalInput").ap()
    xb = [nc.dram_tensor(f"xb{i}", [TOK, D], F32, kind="Internal").ap()
          for i in range(5)]
    xout = nc.dram_tensor("x_out", [TOK, D], F32, kind="ExternalOutput").ap()
    chain = [x0] + xb + [xout]

    seq = [(emit_l0ffn, a0f, "s1_"), (emit_l0attn, a0a, "s2_"),
           (emit_l0ffn, a0f, "s3_"), (emit_l1ffn, a1f, "s4_"),
           (emit_l1attn, a1a, "s5_"), (emit_l1ffn, a1f, "s6_")]

    with contextlib.ExitStack() as es:
        c = Ctx(nc, es)
        PB = es.enter_context(nc.psum_tensor("PB", [128, 2048], F32))
        PM = es.enter_context(nc.psum_tensor("PM", [128, 2, 512], F32))
        PA = es.enter_context(nc.psum_tensor("PA", [128, 2, 512], F32))
        sbase, _ = nc.bump_sbuf(nc.sbuf_bytes_remaining - 64)
        blk = c.wrap_block(es.enter_context(nc.Block()))
        for i, (emit, base_aps, pfx) in enumerate(seq):
            aps = dict(base_aps)
            aps["x_in"] = chain[i]
            aps["x_out"] = chain[i + 1]
            emit(nc, c, es, blk, aps, pfx, sbase, PB, PM, PA)
            c.barrier(blk)
    return nc


def _fused_maps(inputs):
    base = {}
    base.update({"l0f_" + k: v for k, v in _l0ffn_maps(inputs).items()})
    base.update({"l0a_" + k: v for k, v in _l0attn_maps(inputs).items()})
    base.update({"l1f_" + k: v for k, v in _l1ffn_maps(inputs).items()})
    base.update({"l1a_" + k: v for k, v in _l1attn_maps(inputs).items()})
    base.update(_shared_const_values())
    return base


def run_fused(x_full, inputs, nc=None):
    if nc is None:
        nc = build_fused()
    base = _fused_maps(inputs)
    xr = np.asarray(x_full, np.float32).reshape(N_CORES, TOK, D)
    in_maps = [dict(base, x_in=np.ascontiguousarray(xr[i]))
               for i in range(N_CORES)]
    res = run_bass_kernel_spmd(nc, in_maps, core_ids=list(range(N_CORES)))
    out = np.stack([res.results[i]["x_out"] for i in range(N_CORES)])
    return out.reshape(16, 2048, 1024)


_prog_cache = {}


def _prog(name):
    if name not in _prog_cache:
        _prog_cache[name] = {"l0ffn": build_l0ffn, "l0attn": build_l0attn,
                             "l1ffn": build_l1ffn, "l1attn": build_l1attn}[name]()
    return _prog_cache[name]


def kernel(**inputs):
    if "fused" not in _prog_cache:
        _prog_cache["fused"] = build_fused()
    x = np.asarray(inputs["L0_attn_X"], np.float32)
    out = run_fused(x, inputs, nc=_prog_cache["fused"])
    return out.astype(np.float32)
